# version 01
# baseline (speedup 1.0000x reference)
import numpy as np
import concourse.bass as bass
import concourse.mybir as mybir
from concourse.ap import AP
from concourse.bass_utils import run_bass_kernel_spmd
from contextlib import ExitStack

F32 = mybir.dt.float32
BF16 = mybir.dt.bfloat16
AF = mybir.ActivationFunctionType
ALU = mybir.AluOpType
U32 = mybir.dt.uint32
AXX = mybir.AxisListType.X

L = 2048
NSEQ = 2
T = NSEQ * L
D = 1024
DI = 2048
NST = 16
DTR = 64
TC = 512
NCH = T // TC
NEG = -1e30
ARENA_COLS = 47104


class Prog:
    ENGINES = ("sync", "vector", "scalar", "tensor", "gpsimd")

    def __init__(self, nc, es):
        self.nc = nc
        self.q = {e: [] for e in self.ENGINES}
        self.sem = {e: es.enter_context(nc.semaphore(f"sem_{e}")) for e in self.ENGINES}
        self.cnt = {e: 0 for e in self.ENGINES}
        self.last = {e: None for e in self.ENGINES}
        self.lastwait = {e: {o: 0 for o in self.ENGINES} for e in self.ENGINES}

    AUTODEP = ("vector", "scalar", "gpsimd")

    def edge(self, eng):
        if self.last[eng] is not None:
            self.tick(eng)
        self.wait(eng, eng, self.cnt[eng])

    def do(self, eng, f, dep=True):
        if dep and eng in self.AUTODEP:
            self.edge(eng)
        cell = [0]
        self.q[eng].append((f, cell))
        self.last[eng] = cell

    def dma(self, eng, out, in_, **kw):
        sem = self.sem[eng]
        self.q[eng].append(
            (lambda e, o=out, i=in_, kw=kw: e.dma_start(out=o, in_=i, **kw).then_inc(sem, 16), [0]))
        self.cnt[eng] += 16
        self.last[eng] = None
        return self.cnt[eng]

    def tick(self, eng, n=1):
        assert self.last[eng] is not None, "no instruction to tick on " + eng
        self.last[eng][0] += n
        self.cnt[eng] += n
        self.last[eng] = None
        return self.cnt[eng]

    def wait(self, eng, other, val=None):
        if val is None:
            val = self.cnt[other]
        if val <= self.lastwait[eng][other]:
            return
        self.lastwait[eng][other] = val
        sem = self.sem[other]
        self.q[eng].append((lambda e, s=sem, v=val: e.wait_ge(s, v), [0]))

    def barrier(self):
        for e in self.ENGINES:
            if self.last[e] is not None:
                self.tick(e)
        snap = dict(self.cnt)
        for e in self.ENGINES:
            for o in self.ENGINES:
                self.wait(e, o, snap[o])

    def finalize(self, block):
        semmap = self.sem
        for ename in self.ENGINES:
            queue = self.q[ename]

            def body(engine, queue=queue, ename=ename):
                for f, cell in queue:
                    inst = f(engine)
                    if cell[0]:
                        inst.then_inc(semmap[ename], cell[0])
            getattr(block, ename)(body)


def build_program(stages=("in", "l0", "l1", "l2", "l3", "out"), debug_taps=()):
    nc = bass.Bass("TRN2", target_bir_lowering=False)
    es = ExitStack()
    I = {}

    def inp(name, shape):
        I[name] = nc.dram_tensor(name, list(shape), F32, kind="ExternalInput").ap()

    inp("x", (T, D)); inp("IDT", (128, 128)); inp("TRI", (128, 128)); inp("PET", (D, L))
    inp("l0_inw", (D, 2 * DI)); inp("l0_c1w", (DI, 2)); inp("l0_c1b", (DI, 1))
    inp("l0_c2w", (DI, 4)); inp("l0_c2b", (DI, 1)); inp("l0_tw", (DI, 3)); inp("l0_tb", (DI, 1))
    inp("l0_xpw", (DI, 96)); inp("l0_dtw", (DTR, DI)); inp("l0_dtb", (DI, 1))
    inp("l0_alog", (DI, NST)); inp("l0_d", (DI, 1)); inp("l0_ow", (DI, D))
    inp("l0f_w", (D, D)); inp("l0f_a1", (D, D)); inp("l0f_a2", (D, D)); inp("l0f_b", (D, 1))
    inp("l1_rw", (D, 1)); inp("l1_rb", (1, 1))
    inp("l1f_wqkv", (D, 3 * D)); inp("l1f_wo", (D, D)); inp("l1f_wr", (D, D))
    inp("l1f_ub", (D, 1)); inp("l1f_vb", (D, 1))
    inp("l1f_ln1g", (D, 1)); inp("l1f_ln1b", (D, 1))
    inp("l1f_wff1", (D, 4096)); inp("l1f_bff1", (4096, 1))
    inp("l1f_wff2", (4096, D)); inp("l1f_bff2", (D, 1))
    inp("l1f_ln2g", (D, 1)); inp("l1f_ln2b", (D, 1))
    inp("l1f_wg", (D, 8)); inp("l1f_we1", (8, D, 4096)); inp("l1f_be1", (8, 4096))
    inp("l1f_we2", (8, 4096, D)); inp("l1f_be2", (8, D))
    inp("l1f_ln3g", (D, 1)); inp("l1f_ln3b", (D, 1)); inp("l1f_wout", (D, D))
    inp("l2_wq", (D, D)); inp("l2_wk", (D, D)); inp("l2_wv", (D, D)); inp("l2_wo", (D, D))
    inp("l2_bo", (D, 1)); inp("l2_lq1", (1, 64)); inp("l2_lk1", (1, 64))
    inp("l2_lq2", (1, 64)); inp("l2_lk2", (1, 64)); inp("l2_dg", (1, 8))
    inp("l2_chd", (1, 8)); inp("l2_clsr", (1, 1)); inp("l2_subw", (128, 1))
    inp("l2f_wg", (D, 8)); inp("l2f_bg", (1, 8)); inp("l2f_w1", (8, D, 256))
    inp("l2f_b1", (8, 256)); inp("l2f_w2", (8, 256, D)); inp("l2f_b2", (8, D))
    inp("l3_g", (D, 1)); inp("l3_b", (D, 1))
    for p in ("fw", "bw"):
        inp(f"l3_{p}_inw", (D, 2 * DI)); inp(f"l3_{p}_cw", (DI, 4)); inp(f"l3_{p}_cb", (DI, 1))
        inp(f"l3_{p}_xpw", (DI, 96)); inp(f"l3_{p}_dtw", (DTR, DI)); inp(f"l3_{p}_dtb", (DI, 1))
        inp(f"l3_{p}_alog", (DI, NST)); inp(f"l3_{p}_d", (DI, 1)); inp(f"l3_{p}_ow", (DI, D))
    inp("l3f_gw", (D, 8)); inp("l3f_w1", (8, D, 4096)); inp("l3f_b1", (8, 4096))
    inp("l3f_w2", (8, 4096, D)); inp("l3f_b2", (8, D)); inp("l3f_gam", (D, 1)); inp("l3f_bet", (D, 1))

    OUT = nc.dram_tensor("out", [T, D], F32, kind="ExternalOutput").ap()

    S = {}

    def scratch(name, shape, dtype=F32):
        kind = "ExternalOutput" if name in debug_taps else "Internal"
        S[name] = nc.dram_tensor(name, list(shape), dtype, kind=kind).ap()

    scratch("XR", (D, T)); scratch("XN", (D, T)); scratch("XNB", (D, T), BF16)
    scratch("SS", (D, T)); scratch("B0", (DI, T)); scratch("B0B", (DI, T), BF16)
    scratch("B1", (DI, T)); scratch("B2", (DI, T))
    scratch("DBL", (96, T)); scratch("DBLB", (96, T), BF16)
    scratch("YB", (DI, T), BF16)
    scratch("LNROW", (2, TC)); scratch("RWD", (1, T)); scratch("MROW", (NSEQ, L))
    scratch("QU", (D, T), BF16); scratch("QV", (D, T), BF16); scratch("KBD", (D, T), BF16)
    scratch("VN", (T, D), BF16); scratch("RTB", (D, L), BF16); scratch("QRS", (L, L), BF16)
    scratch("CTX", (D, T), BF16)
    scratch("T1", (D, T)); scratch("H1", (D, T)); scratch("H1B", (D, T), BF16)
    scratch("FF1B", (4096, T), BF16)
    scratch("T2", (D, T)); scratch("H2", (D, T)); scratch("H2B", (D, T), BF16)
    scratch("CW", (8, T)); scratch("MACC", (D, T))
    scratch("T3", (D, T)); scratch("H3", (D, T)); scratch("H3B", (D, T), BF16)
    scratch("QT2", (D, T)); scratch("QT2B", (D, T), BF16)
    scratch("KT2", (D, T)); scratch("KT2B", (D, T), BF16)
    scratch("VN2", (T, D), BF16); scratch("NRM", (32, T))
    scratch("COSS", (1, 8)); scratch("NLAM", (1, 1)); scratch("CTX2", (D, T), BF16)
    scratch("WK", (D, T)); scratch("WKB", (D, T), BF16); scratch("WKRB", (D, T), BF16)
    scratch("F3", (D, T)); scratch("XG", (D, T)); scratch("XGB", (D, T), BF16)

    sb_idt = es.enter_context(nc.sbuf_tensor("sb_idt", [128, 128], F32))
    sb_idtb = es.enter_context(nc.sbuf_tensor("sb_idtb", [128, 128], BF16))
    sb_tri = es.enter_context(nc.sbuf_tensor("sb_tri", [128, 128], F32))
    sb_ones = es.enter_context(nc.sbuf_tensor("sb_ones", [128, 1], F32))
    sb_const = es.enter_context(nc.sbuf_tensor("sb_const", [128, 16], F32))
    AR = es.enter_context(nc.sbuf_tensor("AR", [128, ARENA_COLS], F32))
    ARB = AR[:].bitcast(BF16)

    pA = es.enter_context(nc.psum_tensor("pA", [128, 2048], F32))
    pB = es.enter_context(nc.psum_tensor("pB", [128, 512], F32))
    pC = es.enter_context(nc.psum_tensor("pC", [128, 512], F32))
    pD = es.enter_context(nc.psum_tensor("pD", [128, 512], F32))
    pT = es.enter_context(nc.psum_tensor("pT", [128, 1024], BF16))

    P = Prog(nc, es)

    class Arena:
        def __init__(self):
            self.off = 0

        def f32(self, cols):
            a = AR[:, self.off:self.off + cols]
            self.off += cols
            assert self.off <= ARENA_COLS, f"arena overflow {self.off}"
            return a

        def bf16(self, cols):
            a = ARB[:, 2 * self.off:2 * self.off + cols]
            self.off += (cols + 1) // 2
            assert self.off <= ARENA_COLS, f"arena overflow {self.off}"
            return a

    CONSTVALS = [1e-5, 1.0, -1.0, 0.5, 1.0 / 128, 15.0, 0.125, 1.0 / 1024, 1e-6, 0.0, 2.0, -0.5]
    for ci, cv in enumerate(CONSTVALS):
        P.do("vector", lambda e, ci=ci, cv=cv: e.memset(sb_const[:, ci:ci + 1], cv))
        nc.const_aps.register(F32, cv, sb_const[:, ci:ci + 1])
    P.tick("vector")
    P.dma("sync", sb_idt[:], I["IDT"])
    P.dma("sync", sb_tri[:], I["TRI"])
    P.wait("vector", "sync")
    P.do("vector", lambda e: e.memset(sb_ones[:], 1.0))
    P.do("vector", lambda e: e.tensor_copy(out=sb_idtb[:], in_=sb_idt[:]))
    P.barrier()


    def stage_input():
        ar = Arena()
        xt = ar.f32(1024); sq = ar.f32(1024); xn = ar.f32(1024)
        st = ar.f32(6); tr = ar.f32(320)
        for i in range(T // 128):
            ld = P.dma("sync", xt[:], I["x"][i * 128:(i + 1) * 128, :])
            P.wait("vector", "sync", ld)
            P.wait("scalar", "sync", ld)
            P.do("vector", lambda e: e.reduce_sum(out=st[:, 0:1], in_=xt[:], axis=AXX))
            P.do("vector", lambda e: e.tensor_scalar_mul(st[:, 1:2], st[:, 0:1], 1.0 / D))
            P.do("scalar", lambda e: e.activation(out=sq[:], in_=xt[:], func=AF.Square, accum_out=st[:, 2:3]))
            P.wait("vector", "scalar", P.tick("scalar"))
            P.do("vector", lambda e: e.tensor_mul(st[:, 5:6], st[:, 1:2], st[:, 1:2]))
            P.do("vector", lambda e: e.tensor_scalar(st[:, 3:4], st[:, 2:3], 1.0 / D, None, ALU.mult))
            P.do("vector", lambda e: e.tensor_sub(st[:, 3:4], st[:, 3:4], st[:, 5:6]))
            P.wait("scalar", "vector", P.tick("vector"))
            P.do("scalar", lambda e: e.activation(out=st[:, 4:5], in_=st[:, 3:4], func=AF.Sqrt, bias=1e-5))
            P.wait("vector", "scalar", P.tick("scalar"))
            P.do("vector", lambda e: e.reciprocal(out=st[:, 4:5], in_=st[:, 4:5]))
            P.do("vector", lambda e: e.tensor_scalar(xn[:], xt[:], st[:, 1:2], st[:, 4:5], ALU.subtract, ALU.mult))
            P.wait("tensor", "vector", P.tick("vector"))
            trb = tr[:].bitcast(BF16)
            for j in range(8):
                P.do("tensor", lambda e, j=j: e.transpose(out=pB[:, 0:128], in_=xt[:, j * 128:(j + 1) * 128], identity=sb_idt[:]))
                P.wait("scalar", "tensor", P.tick("tensor"))
                P.do("scalar", lambda e: e.copy(out=tr[:, 0:128], in_=pB[:, 0:128]))
                ts = P.tick("scalar")
                P.wait("gpsimd", "scalar", ts)
                P.dma("gpsimd", S["XR"][j * 128:(j + 1) * 128, i * 128:(i + 1) * 128], tr[:, 0:128])
                P.wait("scalar", "gpsimd", P.cnt["gpsimd"])
                P.wait("tensor", "scalar", ts)
                P.do("tensor", lambda e, j=j: e.transpose(out=pB[:, 0:128], in_=xn[:, j * 128:(j + 1) * 128], identity=sb_idt[:]))
                P.wait("scalar", "tensor", P.tick("tensor"))
                P.do("scalar", lambda e: e.copy(out=tr[:, 128:256], in_=pB[:, 0:128]))
                P.do("scalar", lambda e: e.copy(out=trb[:, 512:640], in_=pB[:, 0:128]))
                ts = P.tick("scalar")
                P.wait("gpsimd", "scalar", ts)
                P.dma("gpsimd", S["XN"][j * 128:(j + 1) * 128, i * 128:(i + 1) * 128], tr[:, 128:256])
                P.dma("gpsimd", S["XNB"][j * 128:(j + 1) * 128, i * 128:(i + 1) * 128], trb[:, 512:640])
                P.wait("scalar", "gpsimd", P.cnt["gpsimd"])
                P.wait("tensor", "scalar", ts)
            P.barrier()

    def stage_ln(src, dst, dstb, gname=None, bname=None, eps=1e-5):
        ar = Arena()
        xt = [ar.f32(TC) for _ in range(8)]
        sqt = ar.f32(TC); rowa = ar.f32(TC); rowb = ar.f32(TC)
        bca = ar.f32(TC); bcb = ar.f32(TC); outf = ar.f32(TC)
        gb = ar.f32(16)
        if gname is not None:
            for j in range(8):
                P.dma("sync", gb[:, 2 * j:2 * j + 1], I[gname][j * 128:(j + 1) * 128, 0:1])
                P.dma("sync", gb[:, 2 * j + 1:2 * j + 2], I[bname][j * 128:(j + 1) * 128, 0:1])
        for c in range(NCH):
            for j in range(8):
                P.dma("sync", xt[j][:], src[j * 128:(j + 1) * 128, c * TC:(c + 1) * TC])
            ld = P.cnt["sync"]
            P.wait("scalar", "sync", ld)
            P.wait("tensor", "sync", ld)
            for j in range(8):
                P.do("tensor", lambda e, j=j: e.matmul(out=pC[0:1, 0:TC], lhsT=sb_ones[:], rhs=xt[j][:],
                                                       start=(j == 0), stop=(j == 7)))
            tmm = P.tick("tensor")
            for j in range(8):
                P.do("scalar", lambda e, j=j: e.activation(out=sqt[:], in_=xt[j][:], func=AF.Square))
                P.wait("tensor", "scalar", P.tick("scalar"))
                P.do("tensor", lambda e, j=j: e.matmul(out=pD[0:1, 0:TC], lhsT=sb_ones[:], rhs=sqt[:],
                                                       start=(j == 0), stop=(j == 7)))
                P.wait("scalar", "tensor", P.tick("tensor"))
            tmm2 = P.cnt["tensor"]
            P.wait("scalar", "tensor", tmm2)
            P.do("scalar", lambda e: e.mul(out=rowa[0:1, :], in_=pC[0:1, 0:TC], mul=1.0 / D))
            P.do("scalar", lambda e: e.mul(out=rowb[0:1, :], in_=pD[0:1, 0:TC], mul=1.0 / D))
            P.wait("vector", "scalar", P.tick("scalar"))
            P.do("vector", lambda e: e.tensor_mul(sqt[0:1, 0:TC], rowa[0:1, :], rowa[0:1, :]))
            P.do("vector", lambda e: e.tensor_sub(rowb[0:1, :], rowb[0:1, :], sqt[0:1, 0:TC]))
            P.wait("scalar", "vector", P.tick("vector"))
            P.do("scalar", lambda e: e.activation(out=rowb[0:1, :], in_=rowb[0:1, :], func=AF.Sqrt, bias=eps))
            P.wait("vector", "scalar", P.tick("scalar"))
            P.do("vector", lambda e: e.reciprocal(out=rowb[0:1, :], in_=rowb[0:1, :]))
            P.do("vector", lambda e: e.tensor_mul(rowa[0:1, :], rowa[0:1, :], rowb[0:1, :]))
            P.do("vector", lambda e: e.tensor_scalar_mul(rowa[0:1, :], rowa[0:1, :], -1.0))
            P.wait("gpsimd", "vector", P.tick("vector"))
            P.dma("gpsimd", S["LNROW"][0:1, :], rowa[0:1, :])
            P.dma("gpsimd", S["LNROW"][1:2, :], rowb[0:1, :])
            P.wait("sync", "gpsimd", P.cnt["gpsimd"])
            P.dma("sync", bcb[:], S["LNROW"][0:1, :].partition_broadcast(128)[:, 0, :])
            P.dma("sync", bca[:], S["LNROW"][1:2, :].partition_broadcast(128)[:, 0, :])
            P.wait("vector", "sync", P.cnt["sync"])
            if gname is not None:
                P.wait("vector", "sync", P.cnt["sync"])
            for j in range(8):
                P.do("vector", lambda e, j=j: e.tensor_mul(outf[:], xt[j][:], bca[:]))
                P.do("vector", lambda e: e.tensor_add(outf[:], outf[:], bcb[:]))
                if gname is not None:
                    P.do("vector", lambda e, j=j: e.tensor_scalar(
                        outf[:], outf[:], gb[:, 2 * j:2 * j + 1], gb[:, 2 * j + 1:2 * j + 2],
                        ALU.mult, ALU.add))
                tv = P.tick("vector")
                P.wait("scalar", "vector", tv)
                ob = sqt[:].bitcast(BF16)
                P.do("scalar", lambda e: e.copy(out=ob[:, 0:TC], in_=outf[:]))
                ts = P.tick("scalar")
                P.wait("gpsimd", "vector", tv)
                P.wait("gpsimd", "scalar", ts)
                if dst is not None:
                    P.dma("gpsimd", dst[j * 128:(j + 1) * 128, c * TC:(c + 1) * TC], outf[:])
                if dstb is not None:
                    P.dma("gpsimd", dstb[j * 128:(j + 1) * 128, c * TC:(c + 1) * TC], ob[:, 0:TC])
                gp = P.cnt["gpsimd"]
                P.wait("vector", "scalar", ts)
                P.wait("vector", "gpsimd", gp)
                P.wait("scalar", "gpsimd", gp)
            P.barrier()

    def load_weights_bf16(ar, wt_ap, d_in, d_out):
        kt, mt = d_in // 128, d_out // 128
        wb = ar.bf16(kt * mt * 128)
        wstage = [ar.f32(min(d_out, 4096)) for _ in range(2)]
        mt_per_pass = min(d_out, 4096) // 128
        for k in range(kt):
            for mp0 in range(0, mt, mt_per_pass):
                buf = wstage[(k + mp0) % 2]
                npass = min(mt_per_pass, mt - mp0)
                ld = P.dma("sync", buf[:, 0:npass * 128],
                           wt_ap[k * 128:(k + 1) * 128, mp0 * 128:(mp0 + npass) * 128])
                P.wait("scalar", "sync", ld)
                for m in range(mp0, mp0 + npass):
                    P.do("scalar", lambda e, k=k, m=m, buf=buf, mp0=mp0: e.copy(
                        out=wb[:, (k * mt + m) * 128:(k * mt + m) * 128 + 128],
                        in_=buf[:, (m - mp0) * 128:(m - mp0 + 1) * 128]))
                P.tick("scalar")
                P.wait("sync", "scalar", P.cnt["scalar"])

        def tile(k, m):
            return wb[:, (k * mt + m) * 128:(k * mt + m) * 128 + 128]
        return wb, tile

    def stage_matmul(src_b, d_in, d_out, wt_ap, evac, cols=T, col0=0, ar=None):
        ar = ar or Arena()
        wb, wtile = load_weights_bf16(ar, wt_ap, d_in, d_out)
        kt, mt = d_in // 128, d_out // 128
        rhs = [ar.bf16(TC) for _ in range(kt)]
        bufs = {"f32": ar.f32(TC), "bf16": ar.bf16(TC), "f32b": ar.f32(TC), "ar": ar}
        P.barrier()
        nch = cols // TC
        for c in range(nch):
            cc = col0 + c * TC
            for k in range(kt):
                P.dma("sync", rhs[k][:], src_b[k * 128:(k + 1) * 128, cc:cc + TC])
            P.wait("tensor", "sync", P.cnt["sync"])
            for m in range(mt):
                bank = pB if (m % 2 == 0) else pC
                for k in range(kt):
                    P.do("tensor", lambda e, k=k, m=m, bank=bank: e.matmul(
                        out=bank[:, 0:TC], lhsT=wtile(k, m), rhs=rhs[k][:],
                        start=(k == 0), stop=(k == kt - 1)))
                tmm = P.tick("tensor")
                P.wait("scalar", "tensor", tmm)
                P.wait("vector", "tensor", tmm)
                evac(m, c, bank[:, 0:TC], bufs)
                for eng in ("scalar", "vector"):
                    if P.last[eng] is not None:
                        P.tick(eng)
                evs, evv = P.cnt["scalar"], P.cnt["vector"]
                P.wait("tensor", "scalar", evs)
                P.wait("tensor", "vector", evv)
                P.wait("scalar", "gpsimd", P.cnt["gpsimd"])
                P.wait("vector", "gpsimd", P.cnt["gpsimd"])
            P.barrier()

    def ev_store(dstf=None, dstb=None, act=None, bias_sb=None, scale=1.0, row0=0):
        def evac(m, c, ps, bufs):
            of, ob = bufs["f32"], bufs["bf16"]
            bias = bias_sb[:, m:m + 1] if bias_sb is not None else 0.0
            fn = act if act is not None else (AF.Identity if bias_sb is not None else AF.Copy)
            P.do("scalar", lambda e: e.activation(out=of[:], in_=ps, func=fn, bias=bias, scale=scale))
            ts = P.tick("scalar")
            made_b = False
            if dstb is not None:
                P.do("scalar", lambda e: e.copy(out=ob[:], in_=of[:]))
                ts = P.tick("scalar")
                made_b = True
            P.wait("gpsimd", "scalar", ts)
            r0 = row0 + m * 128
            if dstf is not None:
                P.dma("gpsimd", dstf[r0:r0 + 128, c * TC:(c + 1) * TC], of[:])
            if made_b:
                P.dma("gpsimd", dstb[r0:r0 + 128, c * TC:(c + 1) * TC], ob[:])
        return evac

    def load_bias_cols(ar, ap, d_out):
        mt = d_out // 128
        b = ar.f32(mt)
        for m in range(mt):
            P.dma("sync", b[:, m:m + 1], ap[m * 128:(m + 1) * 128, 0:1])
        return b


    def stage_conv_l0():
        ar = Arena()
        up = ar.f32(L + 5)
        a1 = ar.f32(L); a2 = ar.f32(L); a3 = ar.f32(L); xf = ar.f32(L)
        wcols = ar.f32(16)
        for s in range(NSEQ):
            for j in range(DI // 128):
                r = slice(j * 128, (j + 1) * 128)
                P.dma("sync", wcols[:, 0:2], I["l0_c1w"][r, :])
                P.dma("sync", wcols[:, 2:6], I["l0_c2w"][r, :])
                P.dma("sync", wcols[:, 6:9], I["l0_tw"][r, :])
                P.dma("sync", wcols[:, 9:10], I["l0_c1b"][r, :])
                P.dma("sync", wcols[:, 10:11], I["l0_c2b"][r, :])
                P.dma("sync", wcols[:, 11:12], I["l0_tb"][r, :])
                P.wait("vector", "sync", P.cnt["sync"])
                P.do("vector", lambda e: e.memset(up[:, 0:3], 0.0))
                P.do("vector", lambda e: e.memset(up[:, L + 3:L + 4], 0.0))
                P.tick("vector")
                P.wait("sync", "vector", P.cnt["vector"])
                ld = P.dma("sync", up[:, 3:L + 3], S["B0"][r, s * L:(s + 1) * L])
                P.wait("vector", "sync", ld)
                P.do("vector", lambda e: e.tensor_scalar(a1[:], up[:, 2:L + 2], wcols[:, 0:1], None, ALU.mult))
                P.do("vector", lambda e: e.scalar_tensor_tensor(a1[:], up[:, 3:L + 3], wcols[:, 1:2], a1[:], ALU.mult, ALU.add))
                P.do("vector", lambda e: e.tensor_scalar(a2[:], up[:, 0:L], wcols[:, 2:3], None, ALU.mult))
                for kk in range(1, 4):
                    P.do("vector", lambda e, kk=kk: e.scalar_tensor_tensor(a2[:], up[:, kk:L + kk], wcols[:, 2 + kk:3 + kk], a2[:], ALU.mult, ALU.add))
                P.do("vector", lambda e: e.tensor_scalar(a3[:], up[:, 2:L + 2], wcols[:, 6:7], None, ALU.mult))
                for kk in range(1, 3):
                    P.do("vector", lambda e, kk=kk: e.scalar_tensor_tensor(a3[:], up[:, 2 + kk:L + 2 + kk], wcols[:, 6 + kk:7 + kk], a3[:], ALU.mult, ALU.add))
                P.do("vector", lambda e: e.tensor_scalar(a3[:], a3[:], wcols[:, 11:12], None, ALU.add))
                P.wait("scalar", "vector", P.tick("vector"))
                P.do("scalar", lambda e: e.activation(out=a1[:], in_=a1[:], func=AF.Silu, bias=wcols[:, 9:10]))
                P.do("scalar", lambda e: e.activation(out=a2[:], in_=a2[:], func=AF.Silu, bias=wcols[:, 10:11]))
                P.wait("vector", "scalar", P.tick("scalar"))
                P.do("vector", lambda e: e.tensor_mul(xf[:], a1[:], a2[:]))
                P.do("vector", lambda e: e.tensor_add(xf[:], xf[:], a3[:]))
                P.wait("scalar", "vector", P.tick("vector"))
                xfb = a1[:].bitcast(BF16)
                P.do("scalar", lambda e: e.copy(out=xfb[:, 0:L], in_=xf[:]))
                P.wait("gpsimd", "scalar", P.tick("scalar"))
                P.dma("gpsimd", S["B0"][r, s * L:(s + 1) * L], xf[:])
                P.dma("gpsimd", S["B0B"][r, s * L:(s + 1) * L], xfb[:, 0:L])
                P.barrier()

    def stage_conv_l3(pfx):
        ar = Arena()
        up = ar.f32(L + 4)
        a2 = ar.f32(L)
        wcols = ar.f32(8)
        for s in range(NSEQ):
            for j in range(DI // 128):
                r = slice(j * 128, (j + 1) * 128)
                P.dma("sync", wcols[:, 0:4], I[f"l3_{pfx}_cw"][r, :])
                P.dma("sync", wcols[:, 4:5], I[f"l3_{pfx}_cb"][r, :])
                P.wait("vector", "sync", P.cnt["sync"])
                P.do("vector", lambda e: e.memset(up[:, 0:3], 0.0))
                P.tick("vector")
                P.wait("sync", "vector", P.cnt["vector"])
                ld = P.dma("sync", up[:, 3:L + 3], S["B0"][r, s * L:(s + 1) * L])
                P.wait("vector", "sync", ld)
                P.do("vector", lambda e: e.tensor_scalar(a2[:], up[:, 0:L], wcols[:, 0:1], None, ALU.mult))
                for kk in range(1, 4):
                    P.do("vector", lambda e, kk=kk: e.scalar_tensor_tensor(a2[:], up[:, kk:L + kk], wcols[:, kk:kk + 1], a2[:], ALU.mult, ALU.add))
                P.wait("scalar", "vector", P.tick("vector"))
                P.do("scalar", lambda e: e.activation(out=a2[:], in_=a2[:], func=AF.Silu, bias=wcols[:, 4:5]))
                P.wait("vector", "scalar", P.tick("scalar"))
                ub = up[:].bitcast(BF16)
                P.do("vector", lambda e: e.tensor_copy(out=ub[:, 0:L], in_=a2[:]))
                P.wait("gpsimd", "vector", P.tick("vector"))
                P.dma("gpsimd", S["B0"][r, s * L:(s + 1) * L], a2[:])
                P.dma("gpsimd", S["B0B"][r, s * L:(s + 1) * L], ub[:, 0:L])
                P.barrier()

    def stage_dt(dtw_name, dtb_name, double_sp):
        ar = Arena()
        wf = ar.f32(DI)
        rhs = ar.f32(TC)
        xb = ar.f32(TC); ab = ar.f32(TC); sp = ar.f32(TC)
        bcols = load_bias_cols(ar, I[dtb_name], DI)
        ldw = P.dma("sync", wf[0:DTR, :], I[dtw_name])
        P.barrier()
        for c in range(NCH):
            P.dma("sync", rhs[0:DTR, :], S["DBL"][0:DTR, c * TC:(c + 1) * TC])
            P.wait("tensor", "sync", P.cnt["sync"])
            for m in range(DI // 128):
                bank = pB if (m % 2 == 0) else pC
                P.do("tensor", lambda e, m=m, bank=bank: e.matmul(
                    out=bank[:, 0:TC], lhsT=wf[0:DTR, m * 128:(m + 1) * 128], rhs=rhs[0:DTR, :],
                    start=True, stop=True))
                tmm = P.tick("tensor")
                P.wait("scalar", "tensor", tmm)
                P.do("scalar", lambda e, m=m, bank=bank: e.activation(out=xb[:], in_=bank[:, 0:TC], func=AF.Identity, bias=bcols[:, m:m + 1]))
                P.do("scalar", lambda e: e.activation(out=ab[:], in_=xb[:], func=AF.Abs))
                P.do("scalar", lambda e: e.activation(out=ab[:], in_=ab[:], func=AF.Exp, scale=-1.0))
                P.do("scalar", lambda e: e.activation(out=ab[:], in_=ab[:], func=AF.Ln, bias=1.0))
                P.wait("vector", "scalar", P.tick("scalar"))
                P.do("vector", lambda e: e.tensor_scalar_max(sp[:], xb[:], 0.0))
                P.do("vector", lambda e: e.tensor_add(sp[:], sp[:], ab[:]))
                tv = P.tick("vector")
                if double_sp:
                    P.wait("scalar", "vector", tv)
                    P.do("scalar", lambda e: e.activation(out=ab[:], in_=sp[:], func=AF.Exp, scale=-1.0))
                    P.do("scalar", lambda e: e.activation(out=ab[:], in_=ab[:], func=AF.Ln, bias=1.0))
                    P.wait("vector", "scalar", P.tick("scalar"))
                    P.do("vector", lambda e: e.tensor_add(sp[:], sp[:], ab[:]))
                    tv = P.tick("vector")
                P.wait("gpsimd", "vector", tv)
                P.dma("gpsimd", S["B2"][m * 128:(m + 1) * 128, c * TC:(c + 1) * TC], sp[:])
                gp = P.cnt["gpsimd"]
                P.wait("tensor", "scalar", P.cnt["scalar"])
                P.wait("scalar", "gpsimd", gp)
                P.wait("vector", "gpsimd", gp)
            P.barrier()

    def stage_scan(alog_name, d_name):
        SCH = 1024
        ar = Arena()
        dts = ar.f32(SCH); us = ar.f32(SCH); dtu = ar.f32(SCH)
        dA = ar.f32(SCH); in2 = ar.f32(SCH); hh = ar.f32(SCH)
        yacc = ar.f32(SCH); zs = ar.f32(SCH); tmp2 = ar.f32(SCH)
        Bb = [ar.bf16(SCH) for _ in range(NST)]
        Cb = [ar.bf16(SCH) for _ in range(NST)]
        An = ar.f32(NST)
        dcol = ar.f32(1)
        state = ar.f32(NST * (DI // 128))
        for s in range(NSEQ):
            P.do("vector", lambda e: e.memset(state[:], 0.0))
            P.tick("vector")
            P.barrier()
            for ch in range(L // SCH):
                c0 = s * L + ch * SCH
                for n in range(NST):
                    P.dma("sync", Bb[n][:], S["DBLB"][64 + n:65 + n, c0:c0 + SCH].partition_broadcast(128)[:, 0, :])
                    P.dma("sync", Cb[n][:], S["DBLB"][80 + n:81 + n, c0:c0 + SCH].partition_broadcast(128)[:, 0, :])
                ldbc = P.cnt["sync"]
                for j in range(DI // 128):
                    r = slice(j * 128, (j + 1) * 128)
                    P.dma("sync", dts[:], S["B2"][r, c0:c0 + SCH])
                    P.dma("sync", us[:], S["B0"][r, c0:c0 + SCH])
                    P.dma("sync", An[:], I[alog_name][r, :])
                    P.dma("sync", dcol[:], I[d_name][r, :])
                    P.dma("sync", zs[:], S["B1"][r, c0:c0 + SCH])
                    ld = P.cnt["sync"]
                    P.wait("vector", "sync", ld)
                    P.wait("scalar", "sync", ldbc)
                    P.wait("scalar", "sync", ld)
                    P.do("vector", lambda e: e.tensor_mul(dtu[:], dts[:], us[:]))
                    P.do("scalar", lambda e: e.activation(out=An[:], in_=An[:], func=AF.Exp))
                    P.wait("vector", "scalar", P.tick("scalar"))
                    P.do("vector", lambda e: e.tensor_scalar_mul(An[:], An[:], -1.0))
                    P.wait("scalar", "vector", P.tick("vector"))
                    for n in range(NST):
                        P.do("scalar", lambda e, n=n: e.activation(out=dA[:], in_=dts[:], func=AF.Exp, scale=An[:, n:n + 1]))
                        P.wait("vector", "scalar", P.tick("scalar"))
                        P.do("vector", lambda e, n=n: e.tensor_mul(in2[:], dtu[:], Bb[n][:]))
                        init = 0.0 if ch == 0 else state[:, j * NST + n:j * NST + n + 1]
                        P.do("vector", lambda e, init=init: e.tensor_tensor_scan(
                            out=hh[:], data0=dA[:], data1=in2[:], initial=init, op0=ALU.mult, op1=ALU.add))
                        P.do("vector", lambda e, n=n, j=j: e.tensor_copy(out=state[:, j * NST + n:j * NST + n + 1], in_=hh[:, SCH - 1:SCH]))
                        if n == 0:
                            P.do("vector", lambda e, n=n: e.tensor_mul(yacc[:], hh[:], Cb[n][:]))
                        else:
                            P.do("vector", lambda e, n=n: e.tensor_mul(tmp2[:], hh[:], Cb[n][:]))
                            P.do("vector", lambda e: e.tensor_add(yacc[:], yacc[:], tmp2[:]))
                        tv = P.tick("vector")
                        P.wait("scalar", "vector", tv)
                    P.do("vector", lambda e: e.scalar_tensor_tensor(yacc[:], us[:], dcol[:, 0:1], yacc[:], ALU.mult, ALU.add))
                    P.wait("scalar", "sync", ld)
                    P.do("scalar", lambda e: e.activation(out=zs[:], in_=zs[:], func=AF.Silu))
                    P.wait("vector", "scalar", P.tick("scalar"))
                    P.do("vector", lambda e: e.tensor_mul(yacc[:], yacc[:], zs[:]))
                    yb = tmp2[:].bitcast(BF16)
                    P.do("vector", lambda e: e.tensor_copy(out=yb[:, 0:SCH], in_=yacc[:]))
                    P.wait("gpsimd", "vector", P.tick("vector"))
                    P.dma("gpsimd", S["YB"][r, c0:c0 + SCH], yb[:, 0:SCH])
                    gp = P.cnt["gpsimd"]
                    P.wait("sync", "gpsimd", gp)
                    P.wait("sync", "vector", P.cnt["vector"])
                P.barrier()

    def stage_combine(g_src, s_src, extra=None):
        ar = Arena()
        gt = ar.f32(TC); st_ = ar.f32(TC); xt = ar.f32(TC)
        for c in range(NCH):
            for j in range(8):
                r = slice(j * 128, (j + 1) * 128)
                cs = slice(c * TC, (c + 1) * TC)
                P.dma("sync", gt[:], g_src[r, cs])
                P.dma("sync", st_[:], s_src[r, cs])
                P.dma("sync", xt[:], S["XR"][r, cs])
                ld = P.cnt["sync"]
                P.wait("scalar", "sync", ld)
                P.do("scalar", lambda e: e.activation(out=gt[:], in_=gt[:], func=AF.Silu))
                P.wait("vector", "scalar", P.tick("scalar"))
                P.do("vector", lambda e: e.tensor_mul(gt[:], gt[:], st_[:]))
                P.do("vector", lambda e: e.tensor_add(xt[:], xt[:], gt[:]))
                P.wait("gpsimd", "vector", P.tick("vector"))
                P.dma("gpsimd", S["XR"][r, cs], xt[:])
                P.wait("sync", "gpsimd", P.cnt["gpsimd"])
                P.wait("sync", "vector", P.cnt["vector"])
                P.wait("sync", "scalar", P.cnt["scalar"])
        P.barrier()


    def _ev_split_uz(m, c, ps, bufs):
        of = bufs["f32"]
        P.do("scalar", lambda e: e.copy(out=of[:], in_=ps))
        ts = P.tick("scalar")
        P.wait("gpsimd", "scalar", ts)
        if m < 16:
            P.dma("gpsimd", S["B0"][m * 128:(m + 1) * 128, c * TC:(c + 1) * TC], of[:])
        else:
            P.dma("gpsimd", S["B1"][(m - 16) * 128:(m - 16 + 1) * 128, c * TC:(c + 1) * TC], of[:])

    def stage_xproj(xpw_name):
        ar = Arena()
        wb = ar.bf16(16 * 96)
        wstage = ar.f32(96)
        rhs = [ar.bf16(TC) for _ in range(16)]
        of = ar.f32(TC); ob = ar.bf16(TC)
        for k in range(16):
            ld = P.dma("sync", wstage[:, 0:96], I[xpw_name][k * 128:(k + 1) * 128, :])
            P.wait("scalar", "sync", ld)
            P.do("scalar", lambda e, k=k: e.copy(out=wb[:, k * 96:(k + 1) * 96], in_=wstage[:, 0:96]))
            P.tick("scalar")
            P.wait("sync", "scalar", P.cnt["scalar"])
        P.barrier()
        for c in range(NCH):
            for k in range(16):
                P.dma("sync", rhs[k][:], S["B0B"][k * 128:(k + 1) * 128, c * TC:(c + 1) * TC])
            P.wait("tensor", "sync", P.cnt["sync"])
            for k in range(16):
                P.do("tensor", lambda e, k=k: e.matmul(
                    out=pB[0:96, 0:TC], lhsT=wb[:, k * 96:(k + 1) * 96], rhs=rhs[k][:],
                    start=(k == 0), stop=(k == 15)))
            tmm = P.tick("tensor")
            P.wait("scalar", "tensor", tmm)
            P.do("scalar", lambda e: e.copy(out=of[0:96, :], in_=pB[0:96, 0:TC]))
            P.do("scalar", lambda e: e.copy(out=ob[0:96, :], in_=pB[0:96, 0:TC]))
            ts = P.tick("scalar")
            P.wait("gpsimd", "scalar", ts)
            P.dma("gpsimd", S["DBL"][:, c * TC:(c + 1) * TC], of[0:96, :])
            P.dma("gpsimd", S["DBLB"][:, c * TC:(c + 1) * TC], ob[0:96, :])
            P.wait("tensor", "scalar", ts)
            P.wait("scalar", "gpsimd", P.cnt["gpsimd"])
        P.barrier()

    def layer0_full():
        stage_matmul(S["XNB"], D, 2 * DI, I["l0_inw"], _ev_split_uz)
        P.barrier()
        stage_conv_l0()
        stage_xproj("l0_xpw")
        stage_dt("l0_dtw", "l0_dtb", double_sp=True)
        stage_scan("l0_alog", "l0_d")
        stage_matmul(S["YB"], DI, D, I["l0_ow"], ev_store(dstf=S["SS"]))
        stage_matmul(S["XNB"], D, D, I["l0f_w"], ev_store(dstb=S["H1B"]))
        stage_matmul(S["H1B"], D, D, I["l0f_a1"], ev_store(dstb=S["H2B"]))
        ar = Arena()
        bco = load_bias_cols(ar, I["l0f_b"], D)
        stage_matmul(S["H2B"], D, D, I["l0f_a2"], ev_store(dstf=S["T1"], act=AF.Tanh, bias_sb=bco), ar=ar)
        stage_combine(S["T1"], S["SS"])


    def stage_router():
        ar = Arena()
        wf = ar.f32(8)
        rhs = ar.f32(TC)
        rb = ar.f32(1)
        orow = ar.f32(TC)
        for k in range(8):
            P.dma("sync", wf[:, k:k + 1], I["l1_rw"][k * 128:(k + 1) * 128, 0:1])
        P.dma("sync", rb[0:1, :], I["l1_rb"])
        P.barrier()
        for c in range(NCH):
            for k in range(8):
                ld = P.dma("sync", rhs[:], S["XN"][k * 128:(k + 1) * 128, c * TC:(c + 1) * TC])
                P.wait("tensor", "sync", ld)
                P.do("tensor", lambda e, k=k: e.matmul(out=pD[0:1, 0:TC], lhsT=wf[:, k:k + 1], rhs=rhs[:],
                                                       start=(k == 0), stop=(k == 7)))
                tmm = P.tick("tensor")
                P.wait("sync", "tensor", tmm)
            P.wait("scalar", "tensor", P.cnt["tensor"])
            P.do("scalar", lambda e: e.activation(out=orow[0:1, :], in_=pD[0:1, 0:TC], func=AF.Identity, bias=rb[0:1, 0:1]))
            P.wait("gpsimd", "scalar", P.tick("scalar"))
            P.dma("gpsimd", S["RWD"][0:1, c * TC:(c + 1) * TC], orow[0:1, :])
            P.wait("tensor", "gpsimd", P.cnt["gpsimd"])
            P.wait("scalar", "gpsimd", P.cnt["gpsimd"])
        P.barrier()

    def stage_topk():
        ar = Arena()
        rw = ar.f32(L)
        ge = ar.f32(L)
        lo = ar.f32(1); hi = ar.f32(1); mid = ar.f32(1); cnt_ = ar.f32(1); sel = ar.f32(1)
        P.dma("sync", rw[0:NSEQ, :], S["RWD"].rearrange("a (s l) -> (a s) l", s=NSEQ))
        P.wait("vector", "sync", P.cnt["sync"])
        P.do("vector", lambda e: e.tensor_reduce(out=lo[0:NSEQ, :], in_=rw[0:NSEQ, :], axis=AXX, op=ALU.min))
        P.do("vector", lambda e: e.tensor_scalar_add(lo[0:NSEQ, :], lo[0:NSEQ, :], -1.0))
        P.do("vector", lambda e: e.reduce_max(out=hi[0:NSEQ, :], in_=rw[0:NSEQ, :], axis=AXX))
        P.do("vector", lambda e: e.tensor_scalar_add(hi[0:NSEQ, :], hi[0:NSEQ, :], 1.0))
        for _ in range(48):
            P.do("vector", lambda e: e.tensor_add(mid[0:NSEQ, :], lo[0:NSEQ, :], hi[0:NSEQ, :]))
            P.do("vector", lambda e: e.tensor_scalar_mul(mid[0:NSEQ, :], mid[0:NSEQ, :], 0.5))
            P.do("vector", lambda e: e.tensor_scalar(ge[0:NSEQ, :], rw[0:NSEQ, :], mid[0:NSEQ, 0:1], None, ALU.is_ge))
            P.do("vector", lambda e: e.reduce_sum(out=cnt_[0:NSEQ, :], in_=ge[0:NSEQ, :], axis=AXX))
            selu = sel[:].bitcast(U32)
            P.do("vector", lambda e, selu=selu: e.tensor_scalar(selu[0:NSEQ, :], cnt_[0:NSEQ, :], float(L // 2), None, ALU.is_ge))
            P.do("vector", lambda e, selu=selu: e.select(out=lo[0:NSEQ, :], mask=selu[0:NSEQ, :], on_true=mid[0:NSEQ, :], on_false=lo[0:NSEQ, :]))
            P.do("vector", lambda e, selu=selu: e.select(out=hi[0:NSEQ, :], mask=selu[0:NSEQ, :], on_true=hi[0:NSEQ, :], on_false=mid[0:NSEQ, :]))
        P.do("vector", lambda e: e.tensor_scalar(ge[0:NSEQ, :], rw[0:NSEQ, :], lo[0:NSEQ, 0:1], None, ALU.is_ge))
        P.wait("gpsimd", "vector", P.tick("vector"))
        P.dma("gpsimd", S["MROW"], ge[0:NSEQ, :])
        P.barrier()

    def stage_s1():
        ar = Arena()
        mb = ar.f32(L); xv = ar.f32(L); rv = ar.f32(L)
        for s in range(NSEQ):
            P.dma("sync", mb[:], S["MROW"][s:s + 1, :].partition_broadcast(128)[:, 0, :])
            for j in range(8):
                r = slice(j * 128, (j + 1) * 128)
                ld = P.dma("sync", xv[:], S["XN"][r, s * L:(s + 1) * L])
                P.wait("vector", "sync", ld)
                P.do("vector", lambda e: e.tensor_copy(out=rv[:], in_=xv[:, ::-1]))
                P.do("vector", lambda e: e.tensor_mul(rv[:], rv[:], mb[:]))
                P.wait("gpsimd", "vector", P.tick("vector"))
                P.dma("gpsimd", S["SS"][r, s * L:(s + 1) * L], rv[:])
                P.wait("sync", "gpsimd", P.cnt["gpsimd"])
                P.wait("sync", "vector", P.cnt["vector"])
                P.wait("vector", "gpsimd", P.cnt["gpsimd"])
        P.barrier()

    def stage_qkv():
        ar = Arena()
        wb, wtile = load_weights_bf16(ar, I["l1f_wqkv"], D, 2 * D)
        rhs = [ar.bf16(TC) for _ in range(8)]
        of = ar.f32(TC); ob = ar.bf16(TC)
        ub = load_bias_cols(ar, I["l1f_ub"], D)
        vb = load_bias_cols(ar, I["l1f_vb"], D)
        ob2 = ar.bf16(TC)
        P.barrier()
        for c in range(NCH):
            for k in range(8):
                P.dma("sync", rhs[k][:], S["XNB"][k * 128:(k + 1) * 128, c * TC:(c + 1) * TC])
            P.wait("tensor", "sync", P.cnt["sync"])
            for m in range(16):
                bank = pB if (m % 2 == 0) else pC
                for k in range(8):
                    P.do("tensor", lambda e, k=k, m=m, bank=bank: e.matmul(
                        out=bank[:, 0:TC], lhsT=wtile(k, m), rhs=rhs[k][:], start=(k == 0), stop=(k == 7)))
                tmm = P.tick("tensor")
                P.wait("vector", "tensor", tmm)
                cs = slice(c * TC, (c + 1) * TC)
                if m < 8:
                    r = slice(m * 128, (m + 1) * 128)
                    P.do("vector", lambda e, m=m, bank=bank: e.tensor_scalar(ob[:], bank[:, 0:TC], ub[:, m:m + 1], 0.125, ALU.add, ALU.mult))
                    P.do("vector", lambda e, m=m, bank=bank: e.tensor_scalar(ob2[:], bank[:, 0:TC], vb[:, m:m + 1], 0.125, ALU.add, ALU.mult))
                    tv = P.tick("vector")
                    P.wait("gpsimd", "vector", tv)
                    P.dma("gpsimd", S["QU"][r, cs], ob[:])
                    P.dma("gpsimd", S["QV"][r, cs], ob2[:])
                else:
                    r = slice((m - 8) * 128, (m - 8 + 1) * 128)
                    P.do("vector", lambda e, bank=bank: e.tensor_copy(out=ob[:], in_=bank[:, 0:TC]))
                    tv = P.tick("vector")
                    P.wait("gpsimd", "vector", tv)
                    P.dma("gpsimd", S["KBD"][r, cs], ob[:])
                gp = P.cnt["gpsimd"]
                P.wait("tensor", "vector", P.cnt["vector"])
                P.wait("vector", "gpsimd", gp)
            P.barrier()
        wv = ar.bf16(8 * 1024)
        wstage = ar.f32(1024)
        for k in range(8):
            ld = P.dma("sync", wstage[:, 0:1024], I["l1f_wqkv"][k * 128:(k + 1) * 128, 2048:3072])
            P.wait("scalar", "sync", ld)
            P.do("scalar", lambda e, k=k: e.copy(out=wv[:, k * 1024:(k + 1) * 1024], in_=wstage[:, 0:1024]))
            P.tick("scalar")
            P.wait("sync", "scalar", P.cnt["scalar"])
        lhs = [ar.bf16(128) for _ in range(8)]
        P.barrier()
        for i in range(T // 128):
            for k in range(8):
                P.dma("sync", lhs[k][:], S["XNB"][k * 128:(k + 1) * 128, i * 128:(i + 1) * 128])
            P.wait("tensor", "sync", P.cnt["sync"])
            for nh in range(2):
                bank = pB if nh == 0 else pC
                for k in range(8):
                    P.do("tensor", lambda e, k=k, nh=nh, bank=bank: e.matmul(
                        out=bank[:, 0:TC], lhsT=lhs[k][:], rhs=wv[:, k * 1024 + nh * 512:k * 1024 + (nh + 1) * 512],
                        start=(k == 0), stop=(k == 7)))
                tmm = P.tick("tensor")
                P.wait("scalar", "tensor", tmm)
                P.do("scalar", lambda e, bank=bank: e.copy(out=ob[:], in_=bank[:, 0:TC]))
                ts = P.tick("scalar")
                P.wait("gpsimd", "scalar", ts)
                P.dma("gpsimd", S["VN"][i * 128:(i + 1) * 128, nh * 512:(nh + 1) * 512], ob[:])
                P.wait("tensor", "scalar", ts)
                P.wait("scalar", "gpsimd", P.cnt["gpsimd"])
            P.barrier()

    def stage_rt():
        ar = Arena()
        wb, wtile = load_weights_bf16(ar, I["l1f_wr"], D, D)
        rhs = [ar.bf16(TC) for _ in range(8)]
        pestage = ar.f32(TC)
        ob = ar.bf16(TC)
        P.barrier()
        for c in range(L // TC):
            for k in range(8):
                ld = P.dma("sync", pestage[:], I["PET"][k * 128:(k + 1) * 128, c * TC:(c + 1) * TC])
                P.wait("vector", "sync", ld)
                P.do("vector", lambda e, k=k: e.tensor_copy(out=rhs[k][:], in_=pestage[:]))
                P.wait("sync", "vector", P.tick("vector"))
            P.wait("tensor", "vector", P.cnt["vector"])
            for m in range(8):
                bank = pB if (m % 2 == 0) else pC
                for k in range(8):
                    P.do("tensor", lambda e, k=k, m=m, bank=bank: e.matmul(
                        out=bank[:, 0:TC], lhsT=wtile(k, m), rhs=rhs[k][:], start=(k == 0), stop=(k == 7)))
                tmm = P.tick("tensor")
                P.wait("scalar", "tensor", tmm)
                P.do("scalar", lambda e, bank=bank: e.copy(out=ob[:], in_=bank[:, 0:TC]))
                ts = P.tick("scalar")
                P.wait("gpsimd", "scalar", ts)
                P.dma("gpsimd", S["RTB"][m * 128:(m + 1) * 128, c * TC:(c + 1) * TC], ob[:])
                P.wait("tensor", "scalar", ts)
                P.wait("scalar", "gpsimd", P.cnt["gpsimd"])
            P.barrier()

    def stage_attn_l1():
        ar = Arena()
        qv_h = ar.bf16(L)
        qu_h = ar.bf16(L)
        kt_h = ar.bf16(L)
        rt_h = ar.bf16(L)
        vnh = ar.bf16(1024)
        qr_sh = ar.bf16(L)
        sc = ar.f32(L)
        ex = ar.f32(L)
        pbf = ar.bf16(L)
        ptr_sb = ar.bf16(L)
        stt = ar.f32(4)
        ob = ar.bf16(TC)
        ctxs = ar.bf16(128)
        for s in range(NSEQ):
            for h in range(16):
                hr = slice(h * 64, h * 64 + 64)
                cs = slice(s * L, (s + 1) * L)
                P.dma("sync", qv_h[0:64, :], S["QV"][hr, cs])
                P.dma("sync", qu_h[0:64, :], S["QU"][hr, cs])
                P.dma("sync", kt_h[0:64, :], S["KBD"][hr, cs])
                P.dma("sync", rt_h[0:64, :], S["RTB"][hr, :])
                P.barrier()
                for qt in range(16):
                    bank = pB if (qt % 2 == 0) else pC
                    for pc in range(4):
                        P.do("tensor", lambda e, qt=qt, pc=pc, bank=bank: e.matmul(
                            out=bank[:, 0:TC], lhsT=qv_h[0:64, qt * 128:(qt + 1) * 128],
                            rhs=rt_h[0:64, pc * TC:(pc + 1) * TC], start=True, stop=True))
                        tmm = P.tick("tensor")
                        P.wait("scalar", "tensor", tmm)
                        P.do("scalar", lambda e, bank=bank: e.copy(out=ob[:], in_=bank[:, 0:TC]))
                        ts = P.tick("scalar")
                        P.wait("gpsimd", "scalar", ts)
                        P.dma("gpsimd", S["QRS"][qt * 128:(qt + 1) * 128, pc * TC:(pc + 1) * TC], ob[:])
                        P.wait("tensor", "scalar", ts)
                        P.wait("scalar", "gpsimd", P.cnt["gpsimd"])
                for kc in range(16):
                    P.dma("sync", vnh[:, kc * 64:(kc + 1) * 64],
                          S["VN"][s * L + kc * 128:s * L + (kc + 1) * 128, h * 64:h * 64 + 64])
                P.barrier()
                for qt in range(16):
                    q0 = qt * 128
                    for pc in range(4):
                        P.do("tensor", lambda e, qt=qt, pc=pc: e.matmul(
                            out=pA[:, pc * TC:(pc + 1) * TC], lhsT=qu_h[0:64, qt * 128:(qt + 1) * 128],
                            rhs=kt_h[0:64, pc * TC:(pc + 1) * TC], start=True, stop=True))
                    tmm = P.tick("tensor")
                    P.dma("sync", qr_sh[:],
                          AP(S["QRS"].tensor, (q0 * (L - 1) + L - 1), [[L - 1, 128], [1, L]]))
                    ldq = P.cnt["sync"]
                    P.wait("vector", "tensor", tmm)
                    P.wait("vector", "sync", ldq)
                    P.do("vector", lambda e: e.tensor_add(sc[:], pA[:, 0:L], qr_sh[:]))
                    if q0 + 128 < L:
                        P.do("vector", lambda e, q0=q0: e.memset(sc[:, q0 + 128:L], NEG))
                    P.do("vector", lambda e, q0=q0: e.tensor_add(sc[:, q0:q0 + 128], sc[:, q0:q0 + 128], sb_tri[:]))
                    P.do("vector", lambda e: e.reduce_max(out=stt[:, 0:1], in_=sc[:], axis=AXX))
                    P.do("vector", lambda e: e.tensor_scalar_mul(stt[:, 1:2], stt[:, 0:1], -1.0))
                    P.wait("scalar", "vector", P.tick("vector"))
                    P.do("scalar", lambda e: e.activation(out=ex[:], in_=sc[:], func=AF.Exp, bias=stt[:, 1:2], accum_out=stt[:, 2:3]))
                    P.wait("vector", "scalar", P.tick("scalar"))
                    P.do("vector", lambda e: e.reciprocal(out=stt[:, 3:4], in_=stt[:, 2:3]))
                    P.do("vector", lambda e: e.tensor_scalar(pbf[:], ex[:], stt[:, 3:4], None, ALU.mult))
                    P.wait("tensor", "vector", P.tick("vector"))
                    for kc in range(16):
                        P.do("tensor", lambda e, kc=kc: e.transpose(
                            out=pT[:, 0:128], in_=pbf[:, kc * 128:(kc + 1) * 128], identity=sb_idtb[:]))
                        P.wait("scalar", "tensor", P.tick("tensor"))
                        P.do("scalar", lambda e, kc=kc: e.copy(out=ptr_sb[:, kc * 128:(kc + 1) * 128], in_=pT[:, 0:128]))
                        P.wait("tensor", "scalar", P.tick("scalar"))
                    P.wait("tensor", "scalar", P.cnt["scalar"])
                    for kc in range(16):
                        P.do("tensor", lambda e, kc=kc: e.matmul(
                            out=pD[0:64, 0:128], lhsT=vnh[:, kc * 64:(kc + 1) * 64],
                            rhs=ptr_sb[:, kc * 128:(kc + 1) * 128], start=(kc == 0), stop=(kc == 15)))
                    tmm = P.tick("tensor")
                    P.wait("scalar", "tensor", tmm)
                    P.do("scalar", lambda e: e.copy(out=ctxs[0:64, :], in_=pD[0:64, 0:128]))
                    ts = P.tick("scalar")
                    P.wait("gpsimd", "scalar", ts)
                    P.dma("gpsimd", S["CTX"][h * 64:h * 64 + 64, s * L + q0:s * L + q0 + 128], ctxs[0:64, :])
                    P.wait("tensor", "scalar", ts)
                    P.wait("scalar", "gpsimd", P.cnt["gpsimd"])
                    P.wait("vector", "scalar", P.cnt["scalar"])
                    P.wait("sync", "vector", P.cnt["vector"])
                P.barrier()

    def stage_addln(srcb, add_src, wt_ap, d_in, tdst, hdst, hbdst, gname, bname, bias_name=None):
        ar = Arena()
        bco = load_bias_cols(ar, I[bias_name], D) if bias_name else None

        def evac(m, c, ps, bufs):
            of = bufs["f32"]
            x2 = bufs["f32b"]
            cs = slice(c * TC, (c + 1) * TC)
            r = slice(m * 128, (m + 1) * 128)
            P.wait("sync", "vector", P.cnt["vector"])
            P.dma("sync", x2[:], add_src[r, cs])
            P.wait("scalar", "sync", P.cnt["sync"])
            P.wait("vector", "sync", P.cnt["sync"])
            if bco is not None:
                P.do("scalar", lambda e: e.activation(out=of[:], in_=ps, func=AF.Identity, bias=bco[:, m:m + 1]))
                P.wait("vector", "scalar", P.tick("scalar"))
                P.do("vector", lambda e: e.tensor_add(of[:], of[:], x2[:]))
            else:
                P.do("vector", lambda e: e.tensor_add(of[:], ps, x2[:]))
            tv = P.tick("vector")
            P.wait("gpsimd", "vector", tv)
            P.dma("gpsimd", tdst[r, cs], of[:])
            P.wait("sync", "gpsimd", P.cnt["gpsimd"])
        stage_matmul(srcb, d_in, D, wt_ap, evac, ar=ar)
        stage_ln(tdst, hdst, hbdst, gname, bname)

    def stage_gate_cw(src_f32, wg_name, bg_name, mode):
        ar = Arena()
        lhs = [ar.f32(128) for _ in range(8)]
        wg_all = ar.f32(64)
        pr = ar.f32(8); srt = ar.f32(8); cw = ar.f32(8)
        stt = ar.f32(4)
        bgb = ar.f32(8)
        cwt = ar.f32(128)
        for k in range(8):
            P.dma("sync", wg_all[:, k * 8:(k + 1) * 8], I[wg_name][k * 128:(k + 1) * 128, :])
        if bg_name:
            P.dma("sync", bgb[:], I[bg_name].partition_broadcast(128)[:, 0, :])
        P.barrier()
        for i in range(T // 128):
            for k in range(8):
                P.dma("sync", lhs[k][:], src_f32[k * 128:(k + 1) * 128, i * 128:(i + 1) * 128])
            P.wait("tensor", "sync", P.cnt["sync"])
            for k in range(8):
                P.do("tensor", lambda e, k=k: e.matmul(out=pD[:, 0:8], lhsT=lhs[k][:], rhs=wg_all[:, k * 8:(k + 1) * 8],
                                                       start=(k == 0), stop=(k == 7)))
            tmm = P.tick("tensor")
            P.wait("vector", "tensor", tmm)
            if bg_name:
                P.do("vector", lambda e: e.tensor_add(pr[:], pD[:, 0:8], bgb[:]))
            else:
                P.do("vector", lambda e: e.tensor_copy(out=pr[:], in_=pD[:, 0:8]))
            P.do("vector", lambda e: e.reduce_max(out=stt[:, 0:1], in_=pr[:], axis=AXX))
            P.do("vector", lambda e: e.tensor_scalar_mul(stt[:, 1:2], stt[:, 0:1], -1.0))
            P.wait("scalar", "vector", P.tick("vector"))
            P.do("scalar", lambda e: e.activation(out=pr[:], in_=pr[:], func=AF.Exp, bias=stt[:, 1:2], accum_out=stt[:, 2:3]))
            P.wait("vector", "scalar", P.tick("scalar"))
            P.do("vector", lambda e: e.reciprocal(out=stt[:, 3:4], in_=stt[:, 2:3]))
            P.do("vector", lambda e: e.tensor_scalar(pr[:], pr[:], stt[:, 3:4], None, ALU.mult))
            if mode == "argmax":
                P.do("vector", lambda e: e.reduce_max(out=stt[:, 0:1], in_=pr[:], axis=AXX))
                P.do("vector", lambda e: e.tensor_scalar(cw[:], pr[:], stt[:, 0:1], None, ALU.is_ge))
                P.do("vector", lambda e: e.tensor_scalar(cw[:], cw[:], stt[:, 0:1], None, ALU.mult))
            elif mode == "sort":
                P.do("vector", lambda e: e.max(out=cw[:], in_=pr[:]))
            else:
                P.do("vector", lambda e: e.max(out=srt[:], in_=pr[:]))
                P.do("vector", lambda e: e.tensor_scalar(cw[:], pr[:], srt[:, 1:2], None, ALU.is_ge))
                P.do("vector", lambda e: e.tensor_mul(cw[:], cw[:], pr[:]))
                P.do("vector", lambda e: e.tensor_add(stt[:, 0:1], srt[:, 0:1], srt[:, 1:2]))
                P.do("vector", lambda e: e.reciprocal(out=stt[:, 1:2], in_=stt[:, 0:1]))
                P.do("vector", lambda e: e.tensor_scalar(cw[:], cw[:], stt[:, 1:2], None, ALU.mult))
            P.wait("tensor", "vector", P.tick("vector"))
            P.do("tensor", lambda e: e.transpose(out=pB[0:8, 0:128], in_=cw[:], identity=sb_idt[:]))
            P.wait("scalar", "tensor", P.tick("tensor"))
            P.do("scalar", lambda e: e.copy(out=cwt[0:8, :], in_=pB[0:8, 0:128]))
            P.wait("gpsimd", "scalar", P.tick("scalar"))
            P.dma("gpsimd", S["CW"][0:8, i * 128:(i + 1) * 128], cwt[0:8, :])
            gp = P.cnt["gpsimd"]
            P.wait("tensor", "scalar", P.cnt["scalar"])
            P.wait("scalar", "gpsimd", gp)
            P.wait("vector", "scalar", P.cnt["scalar"])
            P.wait("sync", "vector", P.cnt["vector"])
        P.barrier()

    def stage_moe(src_b, we1_ap, be1_ap, we2_ap, be2_ap, d_hid, act, dst):
        kt1, mt1 = D // 128, d_hid // 128
        kt2, mt2 = d_hid // 128, D // 128
        ar = Arena()
        w1b = ar.bf16(kt1 * mt1 * 128)
        w2b = ar.bf16(kt2 * mt2 * 128)
        wstage = ar.f32(1024)
        rhs = [ar.bf16(TC) for _ in range(kt1)]
        hid = [ar.bf16(TC) for _ in range(mt1)]
        cwb = ar.f32(TC)
        be1c = ar.f32(mt1)
        be2row = ar.bf16(D)
        cwrow = ar.bf16(TC)
        cwstage = ar.f32(TC)
        of = ar.f32(TC)
        ncol1 = min(d_hid, 1024)
        for ei in range(8):
            for k in range(kt1):
                for c0 in range(0, d_hid, ncol1):
                    ld = P.dma("sync", wstage[:, 0:ncol1], we1_ap[ei, k * 128:(k + 1) * 128, c0:c0 + ncol1])
                    P.wait("scalar", "sync", ld)
                    for mm_ in range(ncol1 // 128):
                        m = c0 // 128 + mm_
                        P.do("scalar", lambda e, k=k, m=m, mm_=mm_: e.copy(
                            out=w1b[:, (k * mt1 + m) * 128:(k * mt1 + m) * 128 + 128],
                            in_=wstage[:, mm_ * 128:(mm_ + 1) * 128]))
                    P.tick("scalar")
                    P.wait("sync", "scalar", P.cnt["scalar"])
            for k in range(kt2):
                ld = P.dma("sync", wstage[:, 0:D], we2_ap[ei, k * 128:(k + 1) * 128, :])
                P.wait("scalar", "sync", ld)
                for m in range(mt2):
                    P.do("scalar", lambda e, k=k, m=m: e.copy(
                        out=w2b[:, (k * mt2 + m) * 128:(k * mt2 + m) * 128 + 128], in_=wstage[:, m * 128:(m + 1) * 128]))
                P.tick("scalar")
                P.wait("sync", "scalar", P.cnt["scalar"])
            for m in range(mt1):
                P.dma("sync", be1c[:, m:m + 1], be1_ap[ei, m * 128:(m + 1) * 128])
            ld = P.dma("sync", wstage[0:1, 0:D], be2_ap[ei:ei + 1, :])
            P.wait("vector", "sync", ld)
            P.do("vector", lambda e: e.tensor_copy(out=be2row[0:1, :], in_=wstage[0:1, 0:D]))
            P.tick("vector")
            P.barrier()
            for c in range(NCH):
                cs = slice(c * TC, (c + 1) * TC)
                for k in range(kt1):
                    P.dma("sync", rhs[k][:], src_b[k * 128:(k + 1) * 128, cs])
                P.dma("sync", cwstage[0:1, :], S["CW"][ei:ei + 1, cs])
                ldc = P.cnt["sync"]
                P.wait("vector", "sync", ldc)
                P.do("vector", lambda e: e.tensor_copy(out=cwrow[0:1, :], in_=cwstage[0:1, :]))
                P.tick("vector")
                P.wait("sync", "vector", P.cnt["vector"])
                P.dma("sync", cwb[:], S["CW"][ei:ei + 1, cs].partition_broadcast(128)[:, 0, :])
                ldb = P.cnt["sync"]
                P.wait("tensor", "sync", ldb)
                for m in range(mt1):
                    bank = pB if (m % 2 == 0) else pC
                    for k in range(kt1):
                        P.do("tensor", lambda e, k=k, m=m, bank=bank: e.matmul(
                            out=bank[:, 0:TC], lhsT=w1b[:, (k * mt1 + m) * 128:(k * mt1 + m) * 128 + 128],
                            rhs=rhs[k][:], start=(k == 0), stop=(k == kt1 - 1)))
                    tmm = P.tick("tensor")
                    P.wait("scalar", "tensor", tmm)
                    P.do("scalar", lambda e, m=m, bank=bank: e.activation(out=of[:], in_=bank[:, 0:TC], func=act, bias=be1c[:, m:m + 1]))
                    P.wait("vector", "scalar", P.tick("scalar"))
                    P.do("vector", lambda e, m=m: e.tensor_mul(hid[m][:], of[:], cwb[:]))
                    tv = P.tick("vector")
                    P.wait("tensor", "vector", tv)
                    P.wait("scalar", "vector", tv)
                for m in range(mt2):
                    bank = pB if (m % 2 == 0) else pC
                    P.do("tensor", lambda e, m=m, bank=bank: e.matmul(
                        out=bank[:, 0:TC], lhsT=be2row[0:1, m * 128:(m + 1) * 128], rhs=cwrow[0:1, :],
                        start=True, stop=False))
                    for k in range(kt2):
                        P.do("tensor", lambda e, k=k, m=m, bank=bank: e.matmul(
                            out=bank[:, 0:TC], lhsT=w2b[:, (k * mt2 + m) * 128:(k * mt2 + m) * 128 + 128],
                            rhs=hid[k][:], start=False, stop=(k == kt2 - 1)))
                    tmm = P.tick("tensor")
                    P.wait("scalar", "tensor", tmm)
                    P.do("scalar", lambda e, bank=bank: e.copy(out=of[:], in_=bank[:, 0:TC]))
                    ts = P.tick("scalar")
                    P.wait("gpsimd", "scalar", ts)
                    if ei == 0:
                        P.dma("gpsimd", dst[m * 128:(m + 1) * 128, cs], of[:])
                    else:
                        P.dma("gpsimd", dst[m * 128:(m + 1) * 128, cs], of[:], accum_op=ALU.add)
                    P.wait("tensor", "scalar", ts)
                    P.wait("scalar", "gpsimd", P.cnt["gpsimd"])
                P.barrier()

    def layer1_full():
        stage_ln(S["XR"], S["XN"], S["XNB"])
        stage_router()
        stage_topk()
        stage_s1()
        stage_qkv()
        stage_rt()
        stage_attn_l1()
        stage_addln(S["CTX"], S["XN"], I["l1f_wo"], D, S["T1"], S["H1"], S["H1B"], "l1f_ln1g", "l1f_ln1b")
        ar = Arena()
        b1 = load_bias_cols(ar, I["l1f_bff1"], 4096)
        stage_matmul(S["H1B"], D, 4096, I["l1f_wff1"], ev_store(dstb=S["FF1B"], act=AF.Relu, bias_sb=b1), ar=ar)
        stage_addln(S["FF1B"], S["H1"], I["l1f_wff2"], 4096, S["T2"], S["H2"], S["H2B"], "l1f_ln2g", "l1f_ln2b", bias_name="l1f_bff2")
        stage_gate_cw(S["H2"], "l1f_wg", None, "argmax")
        stage_moe(S["H2B"], I["l1f_we1"], I["l1f_be1"], I["l1f_we2"], I["l1f_be2"], 4096, AF.Relu, S["MACC"])
        stage_add(S["H2"], S["MACC"], S["T3"])
        stage_ln(S["T3"], S["H3"], S["H3B"], "l1f_ln3g", "l1f_ln3b")
        stage_matmul(S["H3B"], D, D, I["l1f_wout"], ev_store(dstf=S["T1"]))
        stage_combine(S["T1"], S["SS"])

    def layer1_partial():
        stage_ln(S["XR"], S["XN"], S["XNB"])
        stage_router()
        stage_topk()
        stage_s1()
        stage_qkv()
        stage_rt()
        stage_attn_l1()
        stage_addln(S["CTX"], S["XN"], I["l1f_wo"], D, S["T1"], S["H1"], S["H1B"], "l1f_ln1g", "l1f_ln1b")

    def stage_add(a_src, b_src, dst):
        ar = Arena()
        at = ar.f32(TC); bt = ar.f32(TC)
        for c in range(NCH):
            for j in range(8):
                r = slice(j * 128, (j + 1) * 128)
                cs = slice(c * TC, (c + 1) * TC)
                P.dma("sync", at[:], a_src[r, cs])
                P.dma("sync", bt[:], b_src[r, cs])
                P.wait("vector", "sync", P.cnt["sync"])
                P.do("vector", lambda e: e.tensor_add(at[:], at[:], bt[:]))
                P.wait("gpsimd", "vector", P.tick("vector"))
                P.dma("gpsimd", dst[r, cs], at[:])
                P.wait("sync", "gpsimd", P.cnt["gpsimd"])
                P.wait("sync", "vector", P.cnt["vector"])
        P.barrier()


    def stage_l2_pre():
        ar = Arena()
        a = ar.f32(64); b = ar.f32(64); t = ar.f32(8); u = ar.f32(8)
        P.dma("sync", a[0:1, :], I["l2_lq1"]); P.dma("sync", b[0:1, :], I["l2_lk1"])
        P.wait("vector", "sync", P.cnt["sync"])
        P.do("vector", lambda e: e.tensor_mul(a[0:1, :], a[0:1, :], b[0:1, :]))
        P.do("vector", lambda e: e.reduce_sum(out=t[0:1, 0:1], in_=a[0:1, :], axis=AXX))
        P.wait("sync", "vector", P.tick("vector"))
        P.dma("sync", a[0:1, :], I["l2_lq2"]); P.dma("sync", b[0:1, :], I["l2_lk2"])
        P.wait("vector", "sync", P.cnt["sync"])
        P.do("vector", lambda e: e.tensor_mul(a[0:1, :], a[0:1, :], b[0:1, :]))
        P.do("vector", lambda e: e.reduce_sum(out=t[0:1, 1:2], in_=a[0:1, :], axis=AXX))
        P.wait("scalar", "vector", P.tick("vector"))
        P.do("scalar", lambda e: e.activation(out=t[0:1, 2:3], in_=t[0:1, 0:1], func=AF.Exp))
        P.do("scalar", lambda e: e.activation(out=t[0:1, 3:4], in_=t[0:1, 1:2], func=AF.Exp))
        P.wait("vector", "scalar", P.tick("scalar"))
        P.do("vector", lambda e: e.tensor_sub(t[0:1, 4:5], t[0:1, 3:4], t[0:1, 2:3]))
        P.do("vector", lambda e: e.tensor_scalar_add(t[0:1, 4:5], t[0:1, 4:5], -0.5))
        P.wait("gpsimd", "vector", P.tick("vector"))
        P.dma("gpsimd", S["NLAM"], t[0:1, 4:5])
        P.wait("sync", "vector", P.cnt["vector"])
        P.dma("sync", u[0:1, 0:1], I["l2_clsr"])
        P.dma("sync", a[0:1, 0:8], I["l2_chd"])
        P.wait("scalar", "sync", P.cnt["sync"])
        P.do("scalar", lambda e: e.activation(out=u[0:1, 1:2], in_=u[0:1, 0:1], func=AF.Sigmoid))
        P.do("scalar", lambda e: e.mul(out=u[0:1, 1:2], in_=u[0:1, 1:2], mul=15.0))
        P.wait("vector", "scalar", P.tick("scalar"))
        P.do("vector", lambda e: e.reduce_sum(out=u[0:1, 2:3], in_=a[0:1, 0:8], axis=AXX))
        P.do("vector", lambda e: e.tensor_scalar_mul(u[0:1, 2:3], u[0:1, 2:3], 1.0 / 8))
        P.do("vector", lambda e: e.tensor_scalar(a[0:1, 0:8], a[0:1, 0:8], u[0:1, 2:3], None, ALU.subtract))
        P.wait("scalar", "vector", P.tick("vector"))
        P.do("scalar", lambda e: e.activation(out=a[0:1, 0:8], in_=a[0:1, 0:8], func=AF.Tanh))
        P.do("scalar", lambda e: e.activation(out=a[0:1, 0:8], in_=a[0:1, 0:8], func=AF.Identity, bias=1.0, scale=0.5))
        P.wait("vector", "scalar", P.tick("scalar"))
        P.do("vector", lambda e: e.tensor_scalar(a[0:1, 0:8], a[0:1, 0:8], u[0:1, 1:2], None, ALU.mult))
        P.wait("gpsimd", "vector", P.tick("vector"))
        P.dma("gpsimd", S["COSS"], a[0:1, 0:8])
        P.barrier()

    def stage_l2_norms():
        ar = Arena()
        xt = ar.f32(TC); sq = ar.f32(TC); row = ar.f32(TC)
        P.barrier()
        for (src, base) in ((S["QT2"], 0), (S["KT2"], 16)):
            for h in range(8):
                for c in range(NCH):
                    ld = P.dma("sync", xt[:], src[h * 128:(h + 1) * 128, c * TC:(c + 1) * TC])
                    P.wait("scalar", "sync", ld)
                    P.do("scalar", lambda e: e.activation(out=sq[:], in_=xt[:], func=AF.Square))
                    P.wait("tensor", "scalar", P.tick("scalar"))
                    for half in range(2):
                        bank = pC if half == 0 else pD
                        P.do("tensor", lambda e, half=half, bank=bank: e.matmul(
                            out=bank[0:1, 0:TC], lhsT=sb_ones[half * 64:half * 64 + 64, :],
                            rhs=sq[half * 64:half * 64 + 64, :], start=True, stop=True))
                        tmm = P.tick("tensor")
                        P.wait("scalar", "tensor", tmm)
                        P.do("scalar", lambda e, bank=bank: e.activation(out=row[0:1, :], in_=bank[0:1, 0:TC], func=AF.Sqrt))
                        P.wait("vector", "scalar", P.tick("scalar"))
                        P.do("vector", lambda e: e.tensor_scalar_max(row[0:1, :], row[0:1, :], 1e-6))
                        P.do("vector", lambda e: e.reciprocal(out=row[0:1, :], in_=row[0:1, :]))
                        P.wait("gpsimd", "vector", P.tick("vector"))
                        P.dma("gpsimd", S["NRM"][base + 2 * h + half:base + 2 * h + half + 1, c * TC:(c + 1) * TC], row[0:1, :])
                        P.wait("scalar", "gpsimd", P.cnt["gpsimd"])
                        P.wait("vector", "gpsimd", P.cnt["gpsimd"])
                        P.wait("tensor", "scalar", P.cnt["scalar"])
                    P.wait("scalar", "tensor", P.cnt["tensor"])
                    P.wait("sync", "scalar", P.cnt["scalar"])
        P.barrier()

    def stage_attn_l2():
        ar = Arena()
        q_h = ar.bf16(L)
        k_h = ar.bf16(L)
        vnh = ar.bf16(2048)
        rk1 = ar.f32(L); rk2 = ar.f32(L)
        rq = ar.f32(32)
        l1s = ar.f32(L); l2s = ar.f32(L); gg = ar.f32(L); ex = ar.f32(L)
        pbf = ar.bf16(L); ptr_sb = ar.bf16(L)
        stt = ar.f32(8)
        csc = ar.f32(4)
        sq = ar.f32(128); row = ar.f32(128); rb = ar.f32(128)
        ctxs = ar.bf16(128)
        subw = ar.f32(1)
        ones_loc = sb_ones
        P.dma("sync", subw[:, 0:1], I["l2_subw"])
        P.barrier()
        for s in range(NSEQ):
            cs = slice(s * L, (s + 1) * L)
            for h in range(8):
                hr = slice(h * 128, (h + 1) * 128)
                P.dma("sync", q_h[:], S["QT2B"][hr, cs])
                P.dma("sync", k_h[:], S["KT2B"][hr, cs])
                P.dma("sync", rk1[:], S["NRM"][16 + 2 * h:17 + 2 * h, cs].partition_broadcast(128)[:, 0, :])
                P.dma("sync", rk2[:], S["NRM"][17 + 2 * h:18 + 2 * h, cs].partition_broadcast(128)[:, 0, :])
                for half in range(2):
                    P.dma("sync", sq[0:16, 0:128], AP(S["NRM"].tensor, (2 * h + half) * T + s * L, [[128, 16], [1, 128]]))
                    P.wait("tensor", "sync", P.cnt["sync"])
                    P.do("tensor", lambda e: e.transpose(out=pB[:, 0:16], in_=sq[0:16, 0:128], identity=sb_idt[0:16, 0:16]))
                    P.wait("scalar", "tensor", P.tick("tensor"))
                    P.do("scalar", lambda e, half=half: e.copy(out=rq[:, half * 16:half * 16 + 16], in_=pB[:, 0:16]))
                    P.wait("sync", "scalar", P.tick("scalar"))
                    P.wait("tensor", "scalar", P.cnt["scalar"])
                P.dma("sync", csc[:, 0:1], AP(S["COSS"].tensor, h, [[0, 128], [1, 1]]))
                P.dma("sync", csc[:, 1:2], AP(I["l2_dg"].tensor, h, [[0, 128], [1, 1]]))
                P.dma("sync", csc[:, 2:3], AP(S["NLAM"].tensor, 0, [[0, 128], [1, 1]]))
                for kc in range(16):
                    P.dma("sync", vnh[:, kc * 128:(kc + 1) * 128],
                          S["VN2"][s * L + kc * 128:s * L + (kc + 1) * 128, h * 128:(h + 1) * 128])
                P.barrier()
                for qt in range(16):
                    q0 = qt * 128
                    for pc in range(4):
                        P.do("tensor", lambda e, qt=qt, pc=pc: e.matmul(
                            out=pA[:, pc * TC:(pc + 1) * TC], lhsT=q_h[0:64, qt * 128:(qt + 1) * 128],
                            rhs=k_h[0:64, pc * TC:(pc + 1) * TC], start=True, stop=True))
                    tmm = P.tick("tensor")
                    P.wait("vector", "tensor", tmm)
                    P.do("vector", lambda e, qt=qt: e.tensor_scalar(l1s[:], pA[:, 0:L], rq[:, qt:qt + 1], csc[:, 0:1], ALU.mult, ALU.mult))
                    P.do("vector", lambda e: e.tensor_mul(l1s[:], l1s[:], rk1[:]))
                    tv = P.tick("vector")
                    P.wait("tensor", "vector", tv)
                    for pc in range(4):
                        P.do("tensor", lambda e, qt=qt, pc=pc: e.matmul(
                            out=pA[:, pc * TC:(pc + 1) * TC], lhsT=q_h[64:128, qt * 128:(qt + 1) * 128],
                            rhs=k_h[64:128, pc * TC:(pc + 1) * TC], start=True, stop=True))
                    tmm = P.tick("tensor")
                    P.wait("vector", "tensor", tmm)
                    P.do("vector", lambda e, qt=qt: e.tensor_scalar(l2s[:], pA[:, 0:L], rq[:, 16 + qt:17 + qt], csc[:, 0:1], ALU.mult, ALU.mult))
                    P.do("vector", lambda e: e.tensor_mul(l2s[:], l2s[:], rk2[:]))
                    P.do("vector", lambda e: e.tensor_sub(gg[:], l1s[:], l2s[:]))
                    P.wait("scalar", "vector", P.tick("vector"))
                    P.do("scalar", lambda e: e.activation(out=gg[:], in_=gg[:], func=AF.Tanh, scale=csc[:, 1:2]))
                    P.wait("vector", "scalar", P.tick("scalar"))
                    P.do("vector", lambda e: e.tensor_add(l1s[:], l1s[:], gg[:]))
                    P.do("vector", lambda e: e.tensor_sub(l2s[:], l2s[:], gg[:]))
                    P.do("vector", lambda e: e.reduce_max(out=stt[:, 0:1], in_=l1s[:], axis=AXX))
                    P.do("vector", lambda e: e.tensor_scalar_mul(stt[:, 1:2], stt[:, 0:1], -1.0))
                    P.wait("scalar", "vector", P.tick("vector"))
                    P.do("scalar", lambda e: e.activation(out=ex[:], in_=l1s[:], func=AF.Exp, bias=stt[:, 1:2], accum_out=stt[:, 2:3]))
                    P.wait("vector", "scalar", P.tick("scalar"))
                    P.do("vector", lambda e: e.reciprocal(out=stt[:, 3:4], in_=stt[:, 2:3]))
                    P.do("vector", lambda e: e.tensor_scalar(l1s[:], ex[:], stt[:, 3:4], None, ALU.mult))
                    P.do("vector", lambda e: e.reduce_max(out=stt[:, 4:5], in_=l2s[:], axis=AXX))
                    P.do("vector", lambda e: e.tensor_scalar_mul(stt[:, 5:6], stt[:, 4:5], -1.0))
                    P.wait("scalar", "vector", P.tick("vector"))
                    P.do("scalar", lambda e: e.activation(out=ex[:], in_=l2s[:], func=AF.Exp, bias=stt[:, 5:6], accum_out=stt[:, 6:7]))
                    P.wait("vector", "scalar", P.tick("scalar"))
                    P.do("vector", lambda e: e.reciprocal(out=stt[:, 7:8], in_=stt[:, 6:7]))
                    P.do("vector", lambda e: e.tensor_scalar(ex[:], ex[:], stt[:, 7:8], None, ALU.mult))
                    P.do("vector", lambda e: e.scalar_tensor_tensor(pbf[:], ex[:], csc[:, 2:3], l1s[:], ALU.mult, ALU.add))
                    P.wait("tensor", "vector", P.tick("vector"))
                    for kc in range(16):
                        P.do("tensor", lambda e, kc=kc: e.transpose(
                            out=pT[:, 0:128], in_=pbf[:, kc * 128:(kc + 1) * 128], identity=sb_idtb[:]))
                        P.wait("scalar", "tensor", P.tick("tensor"))
                        P.do("scalar", lambda e, kc=kc: e.copy(out=ptr_sb[:, kc * 128:(kc + 1) * 128], in_=pT[:, 0:128]))
                        P.wait("tensor", "scalar", P.tick("scalar"))
                    for kc in range(16):
                        P.do("tensor", lambda e, kc=kc: e.matmul(
                            out=pB[:, 0:128], lhsT=vnh[:, kc * 128:(kc + 1) * 128],
                            rhs=ptr_sb[:, kc * 128:(kc + 1) * 128], start=(kc == 0), stop=(kc == 15)))
                    tmm = P.tick("tensor")
                    P.wait("scalar", "tensor", tmm)
                    P.do("scalar", lambda e: e.activation(out=sq[:], in_=pB[:, 0:128], func=AF.Square))
                    P.wait("tensor", "scalar", P.tick("scalar"))
                    P.do("tensor", lambda e: e.matmul(out=pD[0:1, 0:128], lhsT=ones_loc[:], rhs=sq[:], start=True, stop=True))
                    tmm2 = P.tick("tensor")
                    P.wait("scalar", "tensor", tmm2)
                    P.do("scalar", lambda e: e.activation(out=row[0:1, 0:128], in_=pD[0:1, 0:128], func=AF.Sqrt, bias=1e-5, scale=1.0 / 128))
                    P.wait("vector", "scalar", P.tick("scalar"))
                    P.do("vector", lambda e: e.reciprocal(out=row[0:1, 0:128], in_=row[0:1, 0:128]))
                    P.wait("gpsimd", "vector", P.tick("vector"))
                    P.wait("gpsimd", "sync", P.cnt["sync"])
                    P.dma("gpsimd", S["LNROW"][0:1, 0:128], row[0:1, 0:128])
                    P.wait("sync", "gpsimd", P.cnt["gpsimd"])
                    P.dma("sync", rb[:, 0:128], S["LNROW"][0:1, 0:128].partition_broadcast(128)[:, 0, :])
                    P.wait("vector", "sync", P.cnt["sync"])
                    P.do("vector", lambda e: e.tensor_mul(sq[:], pB[:, 0:128], rb[:, 0:128]))
                    P.do("vector", lambda e: e.tensor_scalar(ctxs[:], sq[:], subw[:, 0:1], 0.5, ALU.mult, ALU.mult))
                    P.wait("gpsimd", "vector", P.tick("vector"))
                    P.dma("gpsimd", S["CTX2"][hr, s * L + q0:s * L + q0 + 128], ctxs[:])
                    gp = P.cnt["gpsimd"]
                    P.wait("tensor", "vector", P.cnt["vector"])
                    P.wait("vector", "gpsimd", gp)
                    P.wait("scalar", "gpsimd", gp)
                P.barrier()

    def layer2_full():
        stage_ln(S["XR"], S["XN"], S["XNB"])
        stage_l2_pre()
        stage_matmul(S["XNB"], D, D, I["l2_wq"], ev_store(dstf=S["QT2"], dstb=S["QT2B"]))
        stage_matmul(S["XNB"], D, D, I["l2_wk"], ev_store(dstf=S["KT2"], dstb=S["KT2B"]))
        stage_vnat(S["XNB"], I["l2_wv"], S["VN2"])
        stage_l2_norms()
        stage_attn_l2()
        ar = Arena()
        bo = load_bias_cols(ar, I["l2_bo"], D)
        stage_matmul(S["CTX2"], D, D, I["l2_wo"], ev_store(dstf=S["SS"], bias_sb=bo), ar=ar)
        stage_gate_cw(S["XN"], "l2f_wg", "l2f_bg", "sort")
        stage_moe(S["XNB"], I["l2f_w1"], I["l2f_b1"], I["l2f_w2"], I["l2f_b2"], 256, AF.Gelu, S["MACC"])
        stage_combine(S["MACC"], S["SS"])

    def stage_vnat(src_b, w_ap, dst):
        ar = Arena()
        wv = ar.bf16(8 * 1024)
        wstage = ar.f32(1024)
        ob = ar.bf16(TC)
        lhs = [ar.bf16(128) for _ in range(8)]
        for k in range(8):
            ld = P.dma("sync", wstage[:, 0:1024], w_ap[k * 128:(k + 1) * 128, :])
            P.wait("scalar", "sync", ld)
            P.do("scalar", lambda e, k=k: e.copy(out=wv[:, k * 1024:(k + 1) * 1024], in_=wstage[:, 0:1024]))
            P.tick("scalar")
            P.wait("sync", "scalar", P.cnt["scalar"])
        P.barrier()
        for i in range(T // 128):
            for k in range(8):
                P.dma("sync", lhs[k][:], src_b[k * 128:(k + 1) * 128, i * 128:(i + 1) * 128])
            P.wait("tensor", "sync", P.cnt["sync"])
            for nh in range(2):
                bank = pB if nh == 0 else pC
                for k in range(8):
                    P.do("tensor", lambda e, k=k, nh=nh, bank=bank: e.matmul(
                        out=bank[:, 0:TC], lhsT=lhs[k][:], rhs=wv[:, k * 1024 + nh * 512:k * 1024 + (nh + 1) * 512],
                        start=(k == 0), stop=(k == 7)))
                tmm = P.tick("tensor")
                P.wait("scalar", "tensor", tmm)
                P.do("scalar", lambda e, bank=bank: e.copy(out=ob[:], in_=bank[:, 0:TC]))
                ts = P.tick("scalar")
                P.wait("gpsimd", "scalar", ts)
                P.dma("gpsimd", dst[i * 128:(i + 1) * 128, nh * 512:(nh + 1) * 512], ob[:])
                P.wait("tensor", "scalar", ts)
                P.wait("scalar", "gpsimd", P.cnt["gpsimd"])
            P.barrier()


    def stage_flip(srcb, dstb):
        ar = Arena()
        xv = ar.bf16(L); rv = ar.bf16(L)
        for s in range(NSEQ):
            for j in range(8):
                r = slice(j * 128, (j + 1) * 128)
                ld = P.dma("sync", xv[:], srcb[r, s * L:(s + 1) * L])
                P.wait("vector", "sync", ld)
                P.do("vector", lambda e: e.tensor_copy(out=rv[:], in_=xv[:, ::-1]))
                P.wait("gpsimd", "vector", P.tick("vector"))
                P.dma("gpsimd", dstb[r, s * L:(s + 1) * L], rv[:])
                P.wait("sync", "gpsimd", P.cnt["gpsimd"])
                P.wait("sync", "vector", P.cnt["vector"])
                P.wait("vector", "gpsimd", P.cnt["gpsimd"])
        P.barrier()

    def _ev_split_uz_l3(pfx):
        def evac(m, c, ps, bufs):
            of = bufs["f32"]
            P.do("scalar", lambda e: e.copy(out=of[:], in_=ps))
            ts = P.tick("scalar")
            P.wait("gpsimd", "scalar", ts)
            if m < 16:
                P.dma("gpsimd", S["B0"][m * 128:(m + 1) * 128, c * TC:(c + 1) * TC], of[:])
            else:
                P.dma("gpsimd", S["B1"][(m - 16) * 128:(m - 16 + 1) * 128, c * TC:(c + 1) * TC], of[:])
        return evac

    def mamba_l3(pfx, src_b, out_mode):
        stage_matmul(src_b, D, 2 * DI, I[f"l3_{pfx}_inw"], _ev_split_uz_l3(pfx))
        stage_conv_l3(pfx)
        stage_xproj(f"l3_{pfx}_xpw")
        stage_dt(f"l3_{pfx}_dtw", f"l3_{pfx}_dtb", double_sp=False)
        stage_scan(f"l3_{pfx}_alog", f"l3_{pfx}_d")

        if out_mode == "fw":
            stage_matmul(S["YB"], DI, D, I[f"l3_{pfx}_ow"], ev_store(dstf=S["F3"], scale=0.5))
        else:
            def evac(m, c, ps, bufs):
                of, o2 = bufs["f32"], bufs["f32b"]
                P.do("scalar", lambda e: e.mul(out=of[:], in_=ps, mul=0.5))
                P.wait("vector", "scalar", P.tick("scalar"))
                P.do("vector", lambda e: e.tensor_copy(out=o2[:], in_=of[:, ::-1]))
                tv = P.tick("vector")
                P.wait("gpsimd", "vector", tv)
                sidx = (c * TC) // L
                off = (c * TC) % L
                dst0 = sidx * L + (L - off - TC)
                P.dma("gpsimd", S["F3"][m * 128:(m + 1) * 128, dst0:dst0 + TC], o2[:], accum_op=ALU.add)
                P.wait("scalar", "gpsimd", P.cnt["gpsimd"])
                P.wait("vector", "gpsimd", P.cnt["gpsimd"])
            stage_matmul(S["YB"], DI, D, I[f"l3_{pfx}_ow"], evac)

    def stage_final_combine_l3():
        ar = Arena()
        f3 = ar.f32(TC); xn = ar.f32(TC); mc = ar.f32(TC); xt = ar.f32(TC)
        for c in range(NCH):
            for j in range(8):
                r = slice(j * 128, (j + 1) * 128)
                cs = slice(c * TC, (c + 1) * TC)
                P.dma("sync", f3[:], S["F3"][r, cs])
                P.dma("sync", xn[:], S["XN"][r, cs])
                P.dma("sync", mc[:], S["MACC"][r, cs])
                P.dma("sync", xt[:], S["XR"][r, cs])
                P.wait("vector", "sync", P.cnt["sync"])
                P.do("vector", lambda e: e.tensor_add(f3[:], f3[:], xn[:]))
                P.do("vector", lambda e: e.tensor_add(mc[:], mc[:], xn[:]))
                P.wait("scalar", "vector", P.tick("vector"))
                P.do("scalar", lambda e: e.activation(out=mc[:], in_=mc[:], func=AF.Silu))
                P.wait("vector", "scalar", P.tick("scalar"))
                P.do("vector", lambda e: e.tensor_mul(f3[:], f3[:], mc[:]))
                P.do("vector", lambda e: e.tensor_add(xt[:], xt[:], f3[:]))
                P.wait("gpsimd", "vector", P.tick("vector"))
                P.dma("gpsimd", S["XR"][r, cs], xt[:])
                P.wait("sync", "gpsimd", P.cnt["gpsimd"])
                P.wait("sync", "vector", P.cnt["vector"])
                P.wait("sync", "scalar", P.cnt["scalar"])
        P.barrier()

    def layer3_full():
        stage_ln(S["XR"], S["XN"], S["XNB"])
        stage_ln(S["XN"], S["WK"], S["WKB"], "l3_g", "l3_b")
        mamba_l3("fw", S["WKB"], "fw")
        stage_flip(S["WKB"], S["WKRB"])
        mamba_l3("bw", S["WKRB"], "bw")
        stage_ln(S["XN"], S["XG"], S["XGB"], "l3f_gam", "l3f_bet")
        stage_gate_cw(S["XG"], "l3f_gw", None, "top2")
        stage_moe(S["XGB"], I["l3f_w1"], I["l3f_b1"], I["l3f_w2"], I["l3f_b2"], 4096, AF.Gelu, S["MACC"])
        stage_final_combine_l3()

    def stage_output():
        ar = Arena()
        xt = ar.f32(1024)
        ot = ar.f32(1024)
        for i in range(T // 128):
            for j in range(8):
                P.dma("sync", xt[:, j * 128:(j + 1) * 128], S["XR"][j * 128:(j + 1) * 128, i * 128:(i + 1) * 128])
            P.wait("tensor", "sync", P.cnt["sync"])
            for j in range(8):
                P.do("tensor", lambda e, j=j: e.transpose(out=pB[:, 0:128], in_=xt[:, j * 128:(j + 1) * 128], identity=sb_idt[:]))
                P.wait("scalar", "tensor", P.tick("tensor"))
                P.do("scalar", lambda e, j=j: e.copy(out=ot[:, j * 128:(j + 1) * 128], in_=pB[:, 0:128]))
                P.wait("tensor", "scalar", P.tick("scalar"))
            P.wait("gpsimd", "scalar", P.cnt["scalar"])
            P.dma("gpsimd", OUT[i * 128:(i + 1) * 128, :], ot[:])
            P.wait("sync", "gpsimd", P.cnt["gpsimd"])
            P.wait("sync", "tensor", P.cnt["tensor"])
            P.wait("scalar", "gpsimd", P.cnt["gpsimd"])
        P.barrier()

    if "in" in stages:
        stage_input()
    if "l0" in stages:
        layer0_full()
    if "l1p" in stages:
        layer1_partial()
    if "l1" in stages:
        layer1_full()
    if "l2" in stages:
        layer2_full()
    if "l3" in stages:
        layer3_full()
    if "out" in stages:
        stage_output()
    P.barrier()

    block = es.enter_context(nc.Block())
    P.finalize(block)
    es.close()
    return nc


def make_host_constants():
    ident = np.eye(128, dtype=np.float32)
    tri = np.where(np.arange(128)[None, :] <= np.arange(128)[:, None], 0.0, NEG).astype(np.float32)
    pos = np.arange(L - 1, -1, -1, dtype=np.float32)
    inv_freq = 1.0 / 10000.0 ** (np.arange(0, D, 2, dtype=np.float32) / D)
    sinusoid = pos[:, None] * inv_freq[None, :]
    pe = np.concatenate([np.sin(sinusoid), np.cos(sinusoid)], axis=-1)
    pet = np.ascontiguousarray(pe.T)
    return ident, tri, pet


def make_in_map(inputs, shard):
    d = inputs
    ident, tri, pet = make_host_constants()
    cT = lambda a: np.ascontiguousarray(np.asarray(a, dtype=np.float32).T)
    cc = lambda a: np.ascontiguousarray(np.asarray(a, dtype=np.float32))
    col = lambda a: np.asarray(a, dtype=np.float32).reshape(-1, 1)
    m = {
        "x": cc(d["L0_attn_x"][shard * NSEQ:(shard + 1) * NSEQ].reshape(T, D)),
        "IDT": ident, "TRI": tri, "PET": pet,
        "l0_inw": cT(d["L0_attn_in_proj_w"]),
        "l0_c1w": cc(d["L0_attn_conv1_w"][:, 0, :]), "l0_c1b": col(d["L0_attn_conv1_b"]),
        "l0_c2w": cc(d["L0_attn_conv2_w"][:, 0, :]), "l0_c2b": col(d["L0_attn_conv2_b"]),
        "l0_tw": cc(d["L0_attn_tune_w"][:, 0, :]), "l0_tb": col(d["L0_attn_tune_b"]),
        "l0_xpw": cT(d["L0_attn_x_proj_w"]), "l0_dtw": cT(d["L0_attn_dt_proj_w"]),
        "l0_dtb": col(d["L0_attn_dt_proj_b"]), "l0_alog": cc(d["L0_attn_A_log"]),
        "l0_d": col(d["L0_attn_D"]), "l0_ow": cT(d["L0_attn_out_proj_w"]),
        "l0f_w": cc(d["L0_ffn_weight"]), "l0f_a1": cc(d["L0_ffn_aux_weight1"]),
        "l0f_a2": cc(d["L0_ffn_aux_weight2"]), "l0f_b": col(d["L0_ffn_bias"]),
        "l1_rw": cT(d["L1_attn_router_w"]), "l1_rb": col(d["L1_attn_router_b"]),
        "l1f_wqkv": cc(d["L1_ffn_Wqkv"]), "l1f_wo": cc(d["L1_ffn_Wo"]), "l1f_wr": cc(d["L1_ffn_Wr"]),
        "l1f_ub": col(d["L1_ffn_u_bias"]), "l1f_vb": col(d["L1_ffn_v_bias"]),
        "l1f_ln1g": col(d["L1_ffn_ln1_g"]), "l1f_ln1b": col(d["L1_ffn_ln1_b"]),
        "l1f_wff1": cc(d["L1_ffn_Wff1"]), "l1f_bff1": col(d["L1_ffn_bff1"]),
        "l1f_wff2": cc(d["L1_ffn_Wff2"]), "l1f_bff2": col(d["L1_ffn_bff2"]),
        "l1f_ln2g": col(d["L1_ffn_ln2_g"]), "l1f_ln2b": col(d["L1_ffn_ln2_b"]),
        "l1f_wg": cc(d["L1_ffn_Wg"]), "l1f_we1": cc(d["L1_ffn_We1"]), "l1f_be1": cc(d["L1_ffn_be1"]),
        "l1f_we2": cc(d["L1_ffn_We2"]), "l1f_be2": cc(d["L1_ffn_be2"]),
        "l1f_ln3g": col(d["L1_ffn_ln3_g"]), "l1f_ln3b": col(d["L1_ffn_ln3_b"]),
        "l1f_wout": cc(d["L1_ffn_Wout"]),
        "l2_wq": cT(d["L2_attn_Wq"]), "l2_wk": cT(d["L2_attn_Wk"]),
        "l2_wv": cT(d["L2_attn_Wv"]), "l2_wo": cT(d["L2_attn_Wo"]),
        "l2_bo": col(d["L2_attn_bo"]),
        "l2_lq1": cc(d["L2_attn_lambda_q1"]).reshape(1, 64), "l2_lk1": cc(d["L2_attn_lambda_k1"]).reshape(1, 64),
        "l2_lq2": cc(d["L2_attn_lambda_q2"]).reshape(1, 64), "l2_lk2": cc(d["L2_attn_lambda_k2"]).reshape(1, 64),
        "l2_dg": cc(d["L2_attn_delta_gain"]).reshape(1, 8), "l2_chd": cc(d["L2_attn_cos_head_delta"]).reshape(1, 8),
        "l2_clsr": cc(d["L2_attn_cos_logit_scale_raw"]).reshape(1, 1),
        "l2_subw": col(d["L2_attn_subln_weight"]),
        "l2f_wg": cc(d["L2_ffn_Wg"]), "l2f_bg": cc(d["L2_ffn_bg"]).reshape(1, 8),
        "l2f_w1": cc(d["L2_ffn_W1"]), "l2f_b1": cc(d["L2_ffn_b1"]),
        "l2f_w2": cc(d["L2_ffn_W2"]), "l2f_b2": cc(d["L2_ffn_b2"]),
        "l3_g": col(d["L3_attn_gamma"]), "l3_b": col(d["L3_attn_beta"]),
        "l3f_gw": cc(d["L3_ffn_gate_w"]), "l3f_w1": cc(d["L3_ffn_w1"]), "l3f_b1": cc(d["L3_ffn_b1"]),
        "l3f_w2": cc(d["L3_ffn_w2"]), "l3f_b2": cc(d["L3_ffn_b2"]),
        "l3f_gam": col(d["L3_ffn_gamma"]), "l3f_bet": col(d["L3_ffn_beta"]),
    }
    for p in ("fw", "bw"):
        m[f"l3_{p}_inw"] = cT(d[f"L3_attn_{p}_in_w"])
        m[f"l3_{p}_cw"] = cc(d[f"L3_attn_{p}_conv_w"])
        m[f"l3_{p}_cb"] = col(d[f"L3_attn_{p}_conv_b"])
        m[f"l3_{p}_xpw"] = cT(d[f"L3_attn_{p}_xproj_w"])
        m[f"l3_{p}_dtw"] = cT(d[f"L3_attn_{p}_dt_w"])
        m[f"l3_{p}_dtb"] = col(d[f"L3_attn_{p}_dt_b"])
        m[f"l3_{p}_alog"] = cc(d[f"L3_attn_{p}_A_log"])
        m[f"l3_{p}_d"] = col(d[f"L3_attn_{p}_D"])
        m[f"l3_{p}_ow"] = cT(d[f"L3_attn_{p}_out_w"])
    return m


def kernel(**inputs):
    nc = build_program()
    in_maps = [make_in_map(inputs, sh) for sh in range(8)]
    res = run_bass_kernel_spmd(nc, in_maps, core_ids=list(range(8)))
    out = np.concatenate([r["out"].reshape(NSEQ, L, D) for r in res.results], axis=0)
    return out.astype(np.float32)


# version 02 (final)
# speedup vs baseline: 1.0987x; 1.0987x over previous
import numpy as np
import concourse.bass as bass
import concourse.mybir as mybir
from concourse.ap import AP
from concourse.bass_utils import run_bass_kernel_spmd
from contextlib import ExitStack

F32 = mybir.dt.float32
BF16 = mybir.dt.bfloat16
AF = mybir.ActivationFunctionType
ALU = mybir.AluOpType
U32 = mybir.dt.uint32
AXX = mybir.AxisListType.X

L = 2048
NSEQ = 2
T = NSEQ * L
D = 1024
DI = 2048
NST = 16
DTR = 64
TC = 512
NCH = T // TC
NEG = -1e30
ARENA_COLS = 47104


class Prog:
    ENGINES = ("sync", "vector", "scalar", "tensor", "gpsimd")

    def __init__(self, nc, es):
        self.nc = nc
        self.q = {e: [] for e in self.ENGINES}
        self.sem = {e: es.enter_context(nc.semaphore(f"sem_{e}")) for e in self.ENGINES}
        self.cnt = {e: 0 for e in self.ENGINES}
        self.last = {e: None for e in self.ENGINES}
        self.lastwait = {e: {o: 0 for o in self.ENGINES} for e in self.ENGINES}

    AUTODEP = ("vector", "scalar", "gpsimd")

    def edge(self, eng):
        if self.last[eng] is not None:
            self.tick(eng)
        self.wait(eng, eng, self.cnt[eng])

    def do(self, eng, f, dep=True):
        if dep and eng in self.AUTODEP:
            self.edge(eng)
        cell = [0]
        self.q[eng].append((f, cell))
        self.last[eng] = cell

    def dma(self, eng, out, in_, **kw):
        sem = self.sem[eng]
        self.q[eng].append(
            (lambda e, o=out, i=in_, kw=kw: e.dma_start(out=o, in_=i, **kw).then_inc(sem, 16), [0]))
        self.cnt[eng] += 16
        self.last[eng] = None
        return self.cnt[eng]

    def tick(self, eng, n=1):
        assert self.last[eng] is not None, "no instruction to tick on " + eng
        self.last[eng][0] += n
        self.cnt[eng] += n
        self.last[eng] = None
        return self.cnt[eng]

    def wait(self, eng, other, val=None):
        if val is None:
            val = self.cnt[other]
        if val <= self.lastwait[eng][other]:
            return
        self.lastwait[eng][other] = val
        sem = self.sem[other]
        self.q[eng].append((lambda e, s=sem, v=val: e.wait_ge(s, v), [0]))

    def barrier(self):
        for e in self.ENGINES:
            if self.last[e] is not None:
                self.tick(e)
        snap = dict(self.cnt)
        for e in self.ENGINES:
            for o in self.ENGINES:
                self.wait(e, o, snap[o])

    def finalize(self, block):
        semmap = self.sem
        for ename in self.ENGINES:
            queue = self.q[ename]

            def body(engine, queue=queue, ename=ename):
                for f, cell in queue:
                    inst = f(engine)
                    if cell[0]:
                        inst.then_inc(semmap[ename], cell[0])
            getattr(block, ename)(body)


def build_program(stages=("in", "l0", "l1", "l2", "l3", "out"), debug_taps=()):
    nc = bass.Bass("TRN2", target_bir_lowering=False)
    es = ExitStack()
    I = {}

    def inp(name, shape):
        I[name] = nc.dram_tensor(name, list(shape), F32, kind="ExternalInput").ap()

    inp("x", (T, D)); inp("IDT", (128, 128)); inp("TRI", (128, 128)); inp("PET", (D, L))
    inp("l0_inw", (D, 2 * DI)); inp("l0_c1w", (DI, 2)); inp("l0_c1b", (DI, 1))
    inp("l0_c2w", (DI, 4)); inp("l0_c2b", (DI, 1)); inp("l0_tw", (DI, 3)); inp("l0_tb", (DI, 1))
    inp("l0_xpw", (DI, 96)); inp("l0_dtw", (DTR, DI)); inp("l0_dtb", (DI, 1))
    inp("l0_alog", (DI, NST)); inp("l0_d", (DI, 1)); inp("l0_ow", (DI, D))
    inp("l0f_w", (D, D)); inp("l0f_a1", (D, D)); inp("l0f_a2", (D, D)); inp("l0f_b", (D, 1))
    inp("l1_rw", (D, 1)); inp("l1_rb", (1, 1))
    inp("l1_wq_unused", (2048, D)); inp("l1_wk_unused", (2048, D)); inp("l1_wv_unused", (2048, D))
    inp("l1f_wqkv", (D, 3 * D)); inp("l1f_wo", (D, D)); inp("l1f_wr", (D, D))
    inp("l1f_ub", (D, 1)); inp("l1f_vb", (D, 1))
    inp("l1f_ln1g", (D, 1)); inp("l1f_ln1b", (D, 1))
    inp("l1f_wff1", (D, 4096)); inp("l1f_bff1", (4096, 1))
    inp("l1f_wff2", (4096, D)); inp("l1f_bff2", (D, 1))
    inp("l1f_ln2g", (D, 1)); inp("l1f_ln2b", (D, 1))
    inp("l1f_wg", (D, 8)); inp("l1f_we1", (8, D, 4096)); inp("l1f_be1", (8, 4096))
    inp("l1f_we2", (8, 4096, D)); inp("l1f_be2", (8, D))
    inp("l1f_ln3g", (D, 1)); inp("l1f_ln3b", (D, 1)); inp("l1f_wout", (D, D))
    inp("l2_wq", (D, D)); inp("l2_wk", (D, D)); inp("l2_wv", (D, D)); inp("l2_wo", (D, D))
    inp("l2_bo", (D, 1)); inp("l2_lq1", (1, 64)); inp("l2_lk1", (1, 64))
    inp("l2_lq2", (1, 64)); inp("l2_lk2", (1, 64)); inp("l2_dg", (1, 8))
    inp("l2_chd", (1, 8)); inp("l2_clsr", (1, 1)); inp("l2_subw", (128, 1))
    inp("l2f_wg", (D, 8)); inp("l2f_bg", (1, 8)); inp("l2f_w1", (8, D, 256))
    inp("l2f_b1", (8, 256)); inp("l2f_w2", (8, 256, D)); inp("l2f_b2", (8, D))
    inp("l3_g", (D, 1)); inp("l3_b", (D, 1))
    for p in ("fw", "bw"):
        inp(f"l3_{p}_inw", (D, 2 * DI)); inp(f"l3_{p}_cw", (DI, 4)); inp(f"l3_{p}_cb", (DI, 1))
        inp(f"l3_{p}_xpw", (DI, 96)); inp(f"l3_{p}_dtw", (DTR, DI)); inp(f"l3_{p}_dtb", (DI, 1))
        inp(f"l3_{p}_alog", (DI, NST)); inp(f"l3_{p}_d", (DI, 1)); inp(f"l3_{p}_ow", (DI, D))
    inp("l3f_gw", (D, 8)); inp("l3f_w1", (8, D, 4096)); inp("l3f_b1", (8, 4096))
    inp("l3f_w2", (8, 4096, D)); inp("l3f_b2", (8, D)); inp("l3f_gam", (D, 1)); inp("l3f_bet", (D, 1))

    OUT = nc.dram_tensor("out", [T, D], F32, kind="ExternalOutput").ap()

    S = {}

    def scratch(name, shape, dtype=F32):
        kind = "ExternalOutput" if name in debug_taps else "Internal"
        S[name] = nc.dram_tensor(name, list(shape), dtype, kind=kind).ap()

    scratch("XR", (D, T)); scratch("XN", (D, T)); scratch("XNB", (D, T), BF16)
    scratch("SS", (D, T)); scratch("B0", (DI, T)); scratch("B0B", (DI, T), BF16)
    scratch("B1", (DI, T)); scratch("B2", (DI, T))
    scratch("DBL", (96, T)); scratch("DBLB", (96, T), BF16)
    scratch("YB", (DI, T), BF16)
    scratch("LNROW", (2, TC)); scratch("RWD", (1, T)); scratch("MROW", (NSEQ, L))
    scratch("QU", (D, T), BF16); scratch("QV", (D, T), BF16); scratch("KBD", (D, T), BF16)
    scratch("VN", (T, D), BF16); scratch("RTB", (D, L), BF16); scratch("QRS", (L, L), BF16)
    scratch("CTX", (D, T), BF16)
    scratch("T1", (D, T)); scratch("H1", (D, T)); scratch("H1B", (D, T), BF16)
    scratch("FF1B", (4096, T), BF16)
    scratch("T2", (D, T)); scratch("H2", (D, T)); scratch("H2B", (D, T), BF16)
    scratch("CW", (8, T)); scratch("MACC", (D, T))
    scratch("T3", (D, T)); scratch("H3", (D, T)); scratch("H3B", (D, T), BF16)
    scratch("QT2", (D, T)); scratch("QT2B", (D, T), BF16)
    scratch("KT2", (D, T)); scratch("KT2B", (D, T), BF16)
    scratch("VN2", (T, D), BF16); scratch("NRM", (32, T))
    scratch("COSS", (1, 8)); scratch("NLAM", (1, 1)); scratch("CTX2", (D, T), BF16)
    scratch("WK", (D, T)); scratch("WKB", (D, T), BF16); scratch("WKRB", (D, T), BF16)
    scratch("F3", (D, T)); scratch("XG", (D, T)); scratch("XGB", (D, T), BF16)

    sb_idt = es.enter_context(nc.sbuf_tensor("sb_idt", [128, 128], F32))
    sb_idtb = es.enter_context(nc.sbuf_tensor("sb_idtb", [128, 128], BF16))
    sb_tri = es.enter_context(nc.sbuf_tensor("sb_tri", [128, 128], F32))
    sb_ones = es.enter_context(nc.sbuf_tensor("sb_ones", [128, 1], F32))
    sb_const = es.enter_context(nc.sbuf_tensor("sb_const", [128, 16], F32))
    AR = es.enter_context(nc.sbuf_tensor("AR", [128, ARENA_COLS], F32))
    ARB = AR[:].bitcast(BF16)

    pA = es.enter_context(nc.psum_tensor("pA", [128, 2048], F32))
    pB = es.enter_context(nc.psum_tensor("pB", [128, 512], F32))
    pC = es.enter_context(nc.psum_tensor("pC", [128, 512], F32))
    pD = es.enter_context(nc.psum_tensor("pD", [128, 512], F32))
    pT = es.enter_context(nc.psum_tensor("pT", [128, 1024], BF16))

    P = Prog(nc, es)

    class Arena:
        def __init__(self):
            self.off = 0

        def f32(self, cols):
            a = AR[:, self.off:self.off + cols]
            self.off += cols
            assert self.off <= ARENA_COLS, f"arena overflow {self.off}"
            return a

        def bf16(self, cols):
            a = ARB[:, 2 * self.off:2 * self.off + cols]
            self.off += (cols + 1) // 2
            assert self.off <= ARENA_COLS, f"arena overflow {self.off}"
            return a

    CONSTVALS = [1e-5, 1.0, -1.0, 0.5, 1.0 / 128, 15.0, 0.125, 1.0 / 1024, 1e-6, 0.0, 2.0, -0.5]
    for ci, cv in enumerate(CONSTVALS):
        P.do("vector", lambda e, ci=ci, cv=cv: e.memset(sb_const[:, ci:ci + 1], cv))
        nc.const_aps.register(F32, cv, sb_const[:, ci:ci + 1])
    P.tick("vector")
    P.dma("sync", sb_idt[:], I["IDT"])
    P.dma("sync", sb_tri[:], I["TRI"])
    P.wait("vector", "sync")
    P.do("vector", lambda e: e.memset(sb_ones[:], 1.0))
    P.do("vector", lambda e: e.tensor_copy(out=sb_idtb[:], in_=sb_idt[:]))
    P.barrier()


    def stage_input():
        ar = Arena()
        xt = ar.f32(1024); sq = ar.f32(1024); xn = ar.f32(1024)
        st = ar.f32(6); tr = ar.f32(320)
        for i in range(T // 128):
            ld = P.dma("sync", xt[:], I["x"][i * 128:(i + 1) * 128, :])
            P.wait("vector", "sync", ld)
            P.wait("scalar", "sync", ld)
            P.do("vector", lambda e: e.reduce_sum(out=st[:, 0:1], in_=xt[:], axis=AXX))
            P.do("vector", lambda e: e.tensor_scalar_mul(st[:, 1:2], st[:, 0:1], 1.0 / D))
            P.do("scalar", lambda e: e.activation(out=sq[:], in_=xt[:], func=AF.Square, accum_out=st[:, 2:3]))
            P.wait("vector", "scalar", P.tick("scalar"))
            P.do("vector", lambda e: e.tensor_mul(st[:, 5:6], st[:, 1:2], st[:, 1:2]))
            P.do("vector", lambda e: e.tensor_scalar(st[:, 3:4], st[:, 2:3], 1.0 / D, None, ALU.mult))
            P.do("vector", lambda e: e.tensor_sub(st[:, 3:4], st[:, 3:4], st[:, 5:6]))
            P.wait("scalar", "vector", P.tick("vector"))
            P.do("scalar", lambda e: e.activation(out=st[:, 4:5], in_=st[:, 3:4], func=AF.Sqrt, bias=1e-5))
            P.wait("vector", "scalar", P.tick("scalar"))
            P.do("vector", lambda e: e.reciprocal(out=st[:, 4:5], in_=st[:, 4:5]))
            P.do("vector", lambda e: e.tensor_scalar(xn[:], xt[:], st[:, 1:2], st[:, 4:5], ALU.subtract, ALU.mult))
            P.wait("tensor", "vector", P.tick("vector"))
            trb = tr[:].bitcast(BF16)
            for j in range(8):
                P.do("tensor", lambda e, j=j: e.transpose(out=pB[:, 0:128], in_=xt[:, j * 128:(j + 1) * 128], identity=sb_idt[:]))
                P.wait("scalar", "tensor", P.tick("tensor"))
                P.do("scalar", lambda e: e.copy(out=tr[:, 0:128], in_=pB[:, 0:128]))
                ts = P.tick("scalar")
                P.wait("gpsimd", "scalar", ts)
                P.dma("gpsimd", S["XR"][j * 128:(j + 1) * 128, i * 128:(i + 1) * 128], tr[:, 0:128])
                P.wait("scalar", "gpsimd", P.cnt["gpsimd"])
                P.wait("tensor", "scalar", ts)
                P.do("tensor", lambda e, j=j: e.transpose(out=pB[:, 0:128], in_=xn[:, j * 128:(j + 1) * 128], identity=sb_idt[:]))
                P.wait("scalar", "tensor", P.tick("tensor"))
                P.do("scalar", lambda e: e.copy(out=tr[:, 128:256], in_=pB[:, 0:128]))
                P.do("scalar", lambda e: e.copy(out=trb[:, 512:640], in_=pB[:, 0:128]))
                ts = P.tick("scalar")
                P.wait("gpsimd", "scalar", ts)
                P.dma("gpsimd", S["XN"][j * 128:(j + 1) * 128, i * 128:(i + 1) * 128], tr[:, 128:256])
                P.dma("gpsimd", S["XNB"][j * 128:(j + 1) * 128, i * 128:(i + 1) * 128], trb[:, 512:640])
                P.wait("scalar", "gpsimd", P.cnt["gpsimd"])
                P.wait("tensor", "scalar", ts)
            P.barrier()

    def stage_ln(src, dst, dstb, gname=None, bname=None, eps=1e-5):
        ar = Arena()
        xt = [ar.f32(TC) for _ in range(8)]
        sqt = ar.f32(TC); rowa = ar.f32(TC); rowb = ar.f32(TC)
        bca = ar.f32(TC); bcb = ar.f32(TC); outf = ar.f32(TC)
        gb = ar.f32(16)
        if gname is not None:
            for j in range(8):
                P.dma("sync", gb[:, 2 * j:2 * j + 1], I[gname][j * 128:(j + 1) * 128, 0:1])
                P.dma("sync", gb[:, 2 * j + 1:2 * j + 2], I[bname][j * 128:(j + 1) * 128, 0:1])
        for c in range(NCH):
            for j in range(8):
                P.dma("sync", xt[j][:], src[j * 128:(j + 1) * 128, c * TC:(c + 1) * TC])
            ld = P.cnt["sync"]
            P.wait("scalar", "sync", ld)
            P.wait("tensor", "sync", ld)
            for j in range(8):
                P.do("tensor", lambda e, j=j: e.matmul(out=pC[0:1, 0:TC], lhsT=sb_ones[:], rhs=xt[j][:],
                                                       start=(j == 0), stop=(j == 7)))
            tmm = P.tick("tensor")
            for j in range(8):
                P.do("scalar", lambda e, j=j: e.activation(out=sqt[:], in_=xt[j][:], func=AF.Square))
                P.wait("tensor", "scalar", P.tick("scalar"))
                P.do("tensor", lambda e, j=j: e.matmul(out=pD[0:1, 0:TC], lhsT=sb_ones[:], rhs=sqt[:],
                                                       start=(j == 0), stop=(j == 7)))
                P.wait("scalar", "tensor", P.tick("tensor"))
            tmm2 = P.cnt["tensor"]
            P.wait("scalar", "tensor", tmm2)
            P.do("scalar", lambda e: e.mul(out=rowa[0:1, :], in_=pC[0:1, 0:TC], mul=1.0 / D))
            P.do("scalar", lambda e: e.mul(out=rowb[0:1, :], in_=pD[0:1, 0:TC], mul=1.0 / D))
            P.wait("vector", "scalar", P.tick("scalar"))
            P.do("vector", lambda e: e.tensor_mul(sqt[0:1, 0:TC], rowa[0:1, :], rowa[0:1, :]))
            P.do("vector", lambda e: e.tensor_sub(rowb[0:1, :], rowb[0:1, :], sqt[0:1, 0:TC]))
            P.wait("scalar", "vector", P.tick("vector"))
            P.do("scalar", lambda e: e.activation(out=rowb[0:1, :], in_=rowb[0:1, :], func=AF.Sqrt, bias=eps))
            P.wait("vector", "scalar", P.tick("scalar"))
            P.do("vector", lambda e: e.reciprocal(out=rowb[0:1, :], in_=rowb[0:1, :]))
            P.do("vector", lambda e: e.tensor_mul(rowa[0:1, :], rowa[0:1, :], rowb[0:1, :]))
            P.do("vector", lambda e: e.tensor_scalar_mul(rowa[0:1, :], rowa[0:1, :], -1.0))
            P.wait("gpsimd", "vector", P.tick("vector"))
            P.dma("gpsimd", S["LNROW"][0:1, :], rowa[0:1, :])
            P.dma("gpsimd", S["LNROW"][1:2, :], rowb[0:1, :])
            P.wait("sync", "gpsimd", P.cnt["gpsimd"])
            P.dma("sync", bcb[:], S["LNROW"][0:1, :].partition_broadcast(128)[:, 0, :])
            P.dma("sync", bca[:], S["LNROW"][1:2, :].partition_broadcast(128)[:, 0, :])
            P.wait("vector", "sync", P.cnt["sync"])
            if gname is not None:
                P.wait("vector", "sync", P.cnt["sync"])
            for j in range(8):
                P.do("vector", lambda e, j=j: e.tensor_mul(outf[:], xt[j][:], bca[:]))
                P.do("vector", lambda e: e.tensor_add(outf[:], outf[:], bcb[:]))
                if gname is not None:
                    P.do("vector", lambda e, j=j: e.tensor_scalar(
                        outf[:], outf[:], gb[:, 2 * j:2 * j + 1], gb[:, 2 * j + 1:2 * j + 2],
                        ALU.mult, ALU.add))
                tv = P.tick("vector")
                P.wait("scalar", "vector", tv)
                ob = sqt[:].bitcast(BF16)
                P.do("scalar", lambda e: e.copy(out=ob[:, 0:TC], in_=outf[:]))
                ts = P.tick("scalar")
                P.wait("gpsimd", "vector", tv)
                P.wait("gpsimd", "scalar", ts)
                if dst is not None:
                    P.dma("gpsimd", dst[j * 128:(j + 1) * 128, c * TC:(c + 1) * TC], outf[:])
                if dstb is not None:
                    P.dma("gpsimd", dstb[j * 128:(j + 1) * 128, c * TC:(c + 1) * TC], ob[:, 0:TC])
                gp = P.cnt["gpsimd"]
                P.wait("vector", "scalar", ts)
                P.wait("vector", "gpsimd", gp)
                P.wait("scalar", "gpsimd", gp)
            P.barrier()

    def load_weights_bf16(ar, wt_ap, d_in, d_out):
        kt, mt = d_in // 128, d_out // 128
        wb = ar.bf16(kt * mt * 128)
        wstage = [ar.f32(min(d_out, 4096)) for _ in range(2)]
        mt_per_pass = min(d_out, 4096) // 128
        for k in range(kt):
            for mp0 in range(0, mt, mt_per_pass):
                buf = wstage[(k + mp0) % 2]
                npass = min(mt_per_pass, mt - mp0)
                ld = P.dma("sync", buf[:, 0:npass * 128],
                           wt_ap[k * 128:(k + 1) * 128, mp0 * 128:(mp0 + npass) * 128])
                P.wait("scalar", "sync", ld)
                for m in range(mp0, mp0 + npass):
                    P.do("scalar", lambda e, k=k, m=m, buf=buf, mp0=mp0: e.copy(
                        out=wb[:, (k * mt + m) * 128:(k * mt + m) * 128 + 128],
                        in_=buf[:, (m - mp0) * 128:(m - mp0 + 1) * 128]))
                P.tick("scalar")
                P.wait("sync", "scalar", P.cnt["scalar"])

        def tile(k, m):
            return wb[:, (k * mt + m) * 128:(k * mt + m) * 128 + 128]
        return wb, tile

    def stage_matmul(src_b, d_in, d_out, wt_ap, evac, cols=T, col0=0, ar=None):
        ar = ar or Arena()
        wb, wtile = load_weights_bf16(ar, wt_ap, d_in, d_out)
        kt, mt = d_in // 128, d_out // 128
        rhs = [ar.bf16(TC) for _ in range(kt)]
        bufs0 = {"f32": ar.f32(TC), "bf16": ar.bf16(TC), "f32b": ar.f32(TC), "ar": ar}
        bufs1 = {"f32": ar.f32(TC), "bf16": ar.bf16(TC), "f32b": ar.f32(TC), "ar": ar}
        P.barrier()
        nch = cols // TC
        for c in range(nch):
            cc = col0 + c * TC
            hist = {}
            for k in range(kt):
                P.dma("sync", rhs[k][:], src_b[k * 128:(k + 1) * 128, cc:cc + TC])
            P.wait("tensor", "sync", P.cnt["sync"])
            for m in range(mt):
                bi = m % 2
                bank = pB if bi == 0 else pC
                if bi in hist:
                    P.wait("tensor", "scalar", hist[bi][0])
                    P.wait("tensor", "vector", hist[bi][1])
                for k in range(kt):
                    P.do("tensor", lambda e, k=k, m=m, bank=bank: e.matmul(
                        out=bank[:, 0:TC], lhsT=wtile(k, m), rhs=rhs[k][:],
                        start=(k == 0), stop=(k == kt - 1)))
                tmm = P.tick("tensor")
                P.wait("scalar", "tensor", tmm)
                P.wait("vector", "tensor", tmm)
                if bi in hist:
                    P.wait("scalar", "gpsimd", hist[bi][2])
                    P.wait("vector", "gpsimd", hist[bi][2])
                evac(m, c, bank[:, 0:TC], bufs0 if bi == 0 else bufs1)
                for eng in ("scalar", "vector"):
                    if P.last[eng] is not None:
                        P.tick(eng)
                hist[bi] = (P.cnt["scalar"], P.cnt["vector"], P.cnt["gpsimd"])
            P.barrier()

    def ev_store(dstf=None, dstb=None, act=None, bias_sb=None, scale=1.0, row0=0):
        def evac(m, c, ps, bufs):
            of, ob = bufs["f32"], bufs["bf16"]
            bias = bias_sb[:, m:m + 1] if bias_sb is not None else 0.0
            fn = act if act is not None else (AF.Identity if bias_sb is not None else AF.Copy)
            P.do("scalar", lambda e: e.activation(out=of[:], in_=ps, func=fn, bias=bias, scale=scale))
            ts = P.tick("scalar")
            made_b = False
            if dstb is not None:
                P.do("scalar", lambda e: e.copy(out=ob[:], in_=of[:]))
                ts = P.tick("scalar")
                made_b = True
            P.wait("gpsimd", "scalar", ts)
            r0 = row0 + m * 128
            if dstf is not None:
                P.dma("gpsimd", dstf[r0:r0 + 128, c * TC:(c + 1) * TC], of[:])
            if made_b:
                P.dma("gpsimd", dstb[r0:r0 + 128, c * TC:(c + 1) * TC], ob[:])
        return evac

    def load_bias_cols(ar, ap, d_out):
        mt = d_out // 128
        b = ar.f32(mt)
        for m in range(mt):
            P.dma("sync", b[:, m:m + 1], ap[m * 128:(m + 1) * 128, 0:1])
        return b


    def stage_conv_l0():
        ar = Arena()
        up = ar.f32(L + 5)
        a1 = ar.f32(L); a2 = ar.f32(L); a3 = ar.f32(L); xf = ar.f32(L)
        wcols = ar.f32(16)
        for s in range(NSEQ):
            for j in range(DI // 128):
                r = slice(j * 128, (j + 1) * 128)
                P.dma("sync", wcols[:, 0:2], I["l0_c1w"][r, :])
                P.dma("sync", wcols[:, 2:6], I["l0_c2w"][r, :])
                P.dma("sync", wcols[:, 6:9], I["l0_tw"][r, :])
                P.dma("sync", wcols[:, 9:10], I["l0_c1b"][r, :])
                P.dma("sync", wcols[:, 10:11], I["l0_c2b"][r, :])
                P.dma("sync", wcols[:, 11:12], I["l0_tb"][r, :])
                P.wait("vector", "sync", P.cnt["sync"])
                P.do("vector", lambda e: e.memset(up[:, 0:3], 0.0))
                P.do("vector", lambda e: e.memset(up[:, L + 3:L + 4], 0.0))
                P.tick("vector")
                P.wait("sync", "vector", P.cnt["vector"])
                ld = P.dma("sync", up[:, 3:L + 3], S["B0"][r, s * L:(s + 1) * L])
                P.wait("vector", "sync", ld)
                P.do("vector", lambda e: e.tensor_scalar(a1[:], up[:, 2:L + 2], wcols[:, 0:1], None, ALU.mult))
                P.do("vector", lambda e: e.scalar_tensor_tensor(a1[:], up[:, 3:L + 3], wcols[:, 1:2], a1[:], ALU.mult, ALU.add))
                P.do("vector", lambda e: e.tensor_scalar(a2[:], up[:, 0:L], wcols[:, 2:3], None, ALU.mult))
                for kk in range(1, 4):
                    P.do("vector", lambda e, kk=kk: e.scalar_tensor_tensor(a2[:], up[:, kk:L + kk], wcols[:, 2 + kk:3 + kk], a2[:], ALU.mult, ALU.add))
                P.do("vector", lambda e: e.tensor_scalar(a3[:], up[:, 2:L + 2], wcols[:, 6:7], None, ALU.mult))
                for kk in range(1, 3):
                    P.do("vector", lambda e, kk=kk: e.scalar_tensor_tensor(a3[:], up[:, 2 + kk:L + 2 + kk], wcols[:, 6 + kk:7 + kk], a3[:], ALU.mult, ALU.add))
                P.do("vector", lambda e: e.tensor_scalar(a3[:], a3[:], wcols[:, 11:12], None, ALU.add))
                P.wait("scalar", "vector", P.tick("vector"))
                P.do("scalar", lambda e: e.activation(out=a1[:], in_=a1[:], func=AF.Silu, bias=wcols[:, 9:10]))
                P.do("scalar", lambda e: e.activation(out=a2[:], in_=a2[:], func=AF.Silu, bias=wcols[:, 10:11]))
                P.wait("vector", "scalar", P.tick("scalar"))
                P.do("vector", lambda e: e.tensor_mul(xf[:], a1[:], a2[:]))
                P.do("vector", lambda e: e.tensor_add(xf[:], xf[:], a3[:]))
                P.wait("scalar", "vector", P.tick("vector"))
                xfb = a1[:].bitcast(BF16)
                P.do("scalar", lambda e: e.copy(out=xfb[:, 0:L], in_=xf[:]))
                P.wait("gpsimd", "scalar", P.tick("scalar"))
                P.dma("gpsimd", S["B0"][r, s * L:(s + 1) * L], xf[:])
                P.dma("gpsimd", S["B0B"][r, s * L:(s + 1) * L], xfb[:, 0:L])
                P.barrier()

    def stage_conv_l3(pfx):
        ar = Arena()
        up = ar.f32(L + 4)
        a2 = ar.f32(L)
        wcols = ar.f32(8)
        for s in range(NSEQ):
            for j in range(DI // 128):
                r = slice(j * 128, (j + 1) * 128)
                P.dma("sync", wcols[:, 0:4], I[f"l3_{pfx}_cw"][r, :])
                P.dma("sync", wcols[:, 4:5], I[f"l3_{pfx}_cb"][r, :])
                P.wait("vector", "sync", P.cnt["sync"])
                P.do("vector", lambda e: e.memset(up[:, 0:3], 0.0))
                P.tick("vector")
                P.wait("sync", "vector", P.cnt["vector"])
                ld = P.dma("sync", up[:, 3:L + 3], S["B0"][r, s * L:(s + 1) * L])
                P.wait("vector", "sync", ld)
                P.do("vector", lambda e: e.tensor_scalar(a2[:], up[:, 0:L], wcols[:, 0:1], None, ALU.mult))
                for kk in range(1, 4):
                    P.do("vector", lambda e, kk=kk: e.scalar_tensor_tensor(a2[:], up[:, kk:L + kk], wcols[:, kk:kk + 1], a2[:], ALU.mult, ALU.add))
                P.wait("scalar", "vector", P.tick("vector"))
                P.do("scalar", lambda e: e.activation(out=a2[:], in_=a2[:], func=AF.Silu, bias=wcols[:, 4:5]))
                P.wait("vector", "scalar", P.tick("scalar"))
                ub = up[:].bitcast(BF16)
                P.do("vector", lambda e: e.tensor_copy(out=ub[:, 0:L], in_=a2[:]))
                P.wait("gpsimd", "vector", P.tick("vector"))
                P.dma("gpsimd", S["B0"][r, s * L:(s + 1) * L], a2[:])
                P.dma("gpsimd", S["B0B"][r, s * L:(s + 1) * L], ub[:, 0:L])
                P.barrier()

    def stage_dt(dtw_name, dtb_name, double_sp):
        ar = Arena()
        wf = ar.f32(DI)
        rhs = ar.f32(TC)
        xb = ar.f32(TC); ab = ar.f32(TC); sp = ar.f32(TC)
        bcols = load_bias_cols(ar, I[dtb_name], DI)
        ldw = P.dma("sync", wf[0:DTR, :], I[dtw_name])
        P.barrier()
        for c in range(NCH):
            P.dma("sync", rhs[0:DTR, :], S["DBL"][0:DTR, c * TC:(c + 1) * TC])
            P.wait("tensor", "sync", P.cnt["sync"])
            for m in range(DI // 128):
                bank = pB if (m % 2 == 0) else pC
                P.do("tensor", lambda e, m=m, bank=bank: e.matmul(
                    out=bank[:, 0:TC], lhsT=wf[0:DTR, m * 128:(m + 1) * 128], rhs=rhs[0:DTR, :],
                    start=True, stop=True))
                tmm = P.tick("tensor")
                P.wait("scalar", "tensor", tmm)
                P.do("scalar", lambda e, m=m, bank=bank: e.activation(out=xb[:], in_=bank[:, 0:TC], func=AF.Identity, bias=bcols[:, m:m + 1]))
                P.do("scalar", lambda e: e.activation(out=ab[:], in_=xb[:], func=AF.Abs))
                P.do("scalar", lambda e: e.activation(out=ab[:], in_=ab[:], func=AF.Exp, scale=-1.0))
                P.do("scalar", lambda e: e.activation(out=ab[:], in_=ab[:], func=AF.Ln, bias=1.0))
                P.wait("vector", "scalar", P.tick("scalar"))
                P.do("vector", lambda e: e.tensor_scalar_max(sp[:], xb[:], 0.0))
                P.do("vector", lambda e: e.tensor_add(sp[:], sp[:], ab[:]))
                tv = P.tick("vector")
                if double_sp:
                    P.wait("scalar", "vector", tv)
                    P.do("scalar", lambda e: e.activation(out=ab[:], in_=sp[:], func=AF.Exp, scale=-1.0))
                    P.do("scalar", lambda e: e.activation(out=ab[:], in_=ab[:], func=AF.Ln, bias=1.0))
                    P.wait("vector", "scalar", P.tick("scalar"))
                    P.do("vector", lambda e: e.tensor_add(sp[:], sp[:], ab[:]))
                    tv = P.tick("vector")
                P.wait("gpsimd", "vector", tv)
                P.dma("gpsimd", S["B2"][m * 128:(m + 1) * 128, c * TC:(c + 1) * TC], sp[:])
                gp = P.cnt["gpsimd"]
                P.wait("tensor", "scalar", P.cnt["scalar"])
                P.wait("scalar", "gpsimd", gp)
                P.wait("vector", "gpsimd", gp)
            P.barrier()

    def stage_scan(alog_name, d_name):
        SCH = 1024
        ar = Arena()
        dts = ar.f32(SCH); us = ar.f32(SCH); dtu = ar.f32(SCH)
        dA = ar.f32(SCH); in2 = ar.f32(SCH); hh = ar.f32(SCH)
        yacc = ar.f32(SCH); zs = ar.f32(SCH); tmp2 = ar.f32(SCH)
        Bb = [ar.bf16(SCH) for _ in range(NST)]
        Cb = [ar.bf16(SCH) for _ in range(NST)]
        An = ar.f32(NST)
        dcol = ar.f32(1)
        state = ar.f32(NST * (DI // 128))
        for s in range(NSEQ):
            P.do("vector", lambda e: e.memset(state[:], 0.0))
            P.tick("vector")
            P.barrier()
            for ch in range(L // SCH):
                c0 = s * L + ch * SCH
                for n in range(NST):
                    P.dma("sync", Bb[n][:], S["DBLB"][64 + n:65 + n, c0:c0 + SCH].partition_broadcast(128)[:, 0, :])
                    P.dma("sync", Cb[n][:], S["DBLB"][80 + n:81 + n, c0:c0 + SCH].partition_broadcast(128)[:, 0, :])
                ldbc = P.cnt["sync"]
                for j in range(DI // 128):
                    r = slice(j * 128, (j + 1) * 128)
                    P.dma("sync", dts[:], S["B2"][r, c0:c0 + SCH])
                    P.dma("sync", us[:], S["B0"][r, c0:c0 + SCH])
                    P.dma("sync", An[:], I[alog_name][r, :])
                    P.dma("sync", dcol[:], I[d_name][r, :])
                    P.dma("sync", zs[:], S["B1"][r, c0:c0 + SCH])
                    ld = P.cnt["sync"]
                    P.wait("vector", "sync", ld)
                    P.wait("scalar", "sync", ldbc)
                    P.wait("scalar", "sync", ld)
                    P.do("vector", lambda e: e.tensor_mul(dtu[:], dts[:], us[:]))
                    P.do("scalar", lambda e: e.activation(out=An[:], in_=An[:], func=AF.Exp))
                    P.wait("vector", "scalar", P.tick("scalar"))
                    P.do("vector", lambda e: e.tensor_scalar_mul(An[:], An[:], -1.0))
                    P.wait("scalar", "vector", P.tick("vector"))
                    for n in range(NST):
                        P.do("scalar", lambda e, n=n: e.activation(out=dA[:], in_=dts[:], func=AF.Exp, scale=An[:, n:n + 1]))
                        P.wait("vector", "scalar", P.tick("scalar"))
                        P.do("vector", lambda e, n=n: e.tensor_mul(in2[:], dtu[:], Bb[n][:]))
                        init = 0.0 if ch == 0 else state[:, j * NST + n:j * NST + n + 1]
                        P.do("vector", lambda e, init=init: e.tensor_tensor_scan(
                            out=hh[:], data0=dA[:], data1=in2[:], initial=init, op0=ALU.mult, op1=ALU.add))
                        P.do("vector", lambda e, n=n, j=j: e.tensor_copy(out=state[:, j * NST + n:j * NST + n + 1], in_=hh[:, SCH - 1:SCH]))
                        if n == 0:
                            P.do("vector", lambda e, n=n: e.tensor_mul(yacc[:], hh[:], Cb[n][:]))
                        else:
                            P.do("vector", lambda e, n=n: e.tensor_mul(tmp2[:], hh[:], Cb[n][:]))
                            P.do("vector", lambda e: e.tensor_add(yacc[:], yacc[:], tmp2[:]))
                        tv = P.tick("vector")
                        P.wait("scalar", "vector", tv)
                    P.do("vector", lambda e: e.scalar_tensor_tensor(yacc[:], us[:], dcol[:, 0:1], yacc[:], ALU.mult, ALU.add))
                    P.wait("scalar", "sync", ld)
                    P.do("scalar", lambda e: e.activation(out=zs[:], in_=zs[:], func=AF.Silu))
                    P.wait("vector", "scalar", P.tick("scalar"))
                    P.do("vector", lambda e: e.tensor_mul(yacc[:], yacc[:], zs[:]))
                    yb = tmp2[:].bitcast(BF16)
                    P.do("vector", lambda e: e.tensor_copy(out=yb[:, 0:SCH], in_=yacc[:]))
                    P.wait("gpsimd", "vector", P.tick("vector"))
                    P.dma("gpsimd", S["YB"][r, c0:c0 + SCH], yb[:, 0:SCH])
                    gp = P.cnt["gpsimd"]
                    P.wait("sync", "gpsimd", gp)
                    P.wait("sync", "vector", P.cnt["vector"])
                P.barrier()

    def stage_combine(g_src, s_src, extra=None):
        ar = Arena()
        gt = ar.f32(TC); st_ = ar.f32(TC); xt = ar.f32(TC)
        for c in range(NCH):
            for j in range(8):
                r = slice(j * 128, (j + 1) * 128)
                cs = slice(c * TC, (c + 1) * TC)
                P.dma("sync", gt[:], g_src[r, cs])
                P.dma("sync", st_[:], s_src[r, cs])
                P.dma("sync", xt[:], S["XR"][r, cs])
                ld = P.cnt["sync"]
                P.wait("scalar", "sync", ld)
                P.do("scalar", lambda e: e.activation(out=gt[:], in_=gt[:], func=AF.Silu))
                P.wait("vector", "scalar", P.tick("scalar"))
                P.do("vector", lambda e: e.tensor_mul(gt[:], gt[:], st_[:]))
                P.do("vector", lambda e: e.tensor_add(xt[:], xt[:], gt[:]))
                P.wait("gpsimd", "vector", P.tick("vector"))
                P.dma("gpsimd", S["XR"][r, cs], xt[:])
                P.wait("sync", "gpsimd", P.cnt["gpsimd"])
                P.wait("sync", "vector", P.cnt["vector"])
                P.wait("sync", "scalar", P.cnt["scalar"])
        P.barrier()


    def _ev_split_uz(m, c, ps, bufs):
        of = bufs["f32"]
        P.do("scalar", lambda e: e.copy(out=of[:], in_=ps))
        ts = P.tick("scalar")
        P.wait("gpsimd", "scalar", ts)
        if m < 16:
            P.dma("gpsimd", S["B0"][m * 128:(m + 1) * 128, c * TC:(c + 1) * TC], of[:])
        else:
            P.dma("gpsimd", S["B1"][(m - 16) * 128:(m - 16 + 1) * 128, c * TC:(c + 1) * TC], of[:])

    def stage_xproj(xpw_name):
        ar = Arena()
        wb = ar.bf16(16 * 96)
        wstage = ar.f32(96)
        rhs = [ar.bf16(TC) for _ in range(16)]
        of = ar.f32(TC); ob = ar.bf16(TC)
        for k in range(16):
            ld = P.dma("sync", wstage[:, 0:96], I[xpw_name][k * 128:(k + 1) * 128, :])
            P.wait("scalar", "sync", ld)
            P.do("scalar", lambda e, k=k: e.copy(out=wb[:, k * 96:(k + 1) * 96], in_=wstage[:, 0:96]))
            P.tick("scalar")
            P.wait("sync", "scalar", P.cnt["scalar"])
        P.barrier()
        for c in range(NCH):
            for k in range(16):
                P.dma("sync", rhs[k][:], S["B0B"][k * 128:(k + 1) * 128, c * TC:(c + 1) * TC])
            P.wait("tensor", "sync", P.cnt["sync"])
            for k in range(16):
                P.do("tensor", lambda e, k=k: e.matmul(
                    out=pB[0:96, 0:TC], lhsT=wb[:, k * 96:(k + 1) * 96], rhs=rhs[k][:],
                    start=(k == 0), stop=(k == 15)))
            tmm = P.tick("tensor")
            P.wait("scalar", "tensor", tmm)
            P.do("scalar", lambda e: e.copy(out=of[0:96, :], in_=pB[0:96, 0:TC]))
            P.do("scalar", lambda e: e.copy(out=ob[0:96, :], in_=pB[0:96, 0:TC]))
            ts = P.tick("scalar")
            P.wait("gpsimd", "scalar", ts)
            P.dma("gpsimd", S["DBL"][:, c * TC:(c + 1) * TC], of[0:96, :])
            P.dma("gpsimd", S["DBLB"][:, c * TC:(c + 1) * TC], ob[0:96, :])
            P.wait("tensor", "scalar", ts)
            P.wait("scalar", "gpsimd", P.cnt["gpsimd"])
        P.barrier()

    def layer0_full():
        stage_matmul(S["XNB"], D, 2 * DI, I["l0_inw"], _ev_split_uz)
        P.barrier()
        stage_conv_l0()
        stage_xproj("l0_xpw")
        stage_dt("l0_dtw", "l0_dtb", double_sp=True)
        stage_scan("l0_alog", "l0_d")
        stage_matmul(S["YB"], DI, D, I["l0_ow"], ev_store(dstf=S["SS"]))
        stage_matmul(S["XNB"], D, D, I["l0f_w"], ev_store(dstb=S["H1B"]))
        stage_matmul(S["H1B"], D, D, I["l0f_a1"], ev_store(dstb=S["H2B"]))
        ar = Arena()
        bco = load_bias_cols(ar, I["l0f_b"], D)
        stage_matmul(S["H2B"], D, D, I["l0f_a2"], ev_store(dstf=S["T1"], act=AF.Tanh, bias_sb=bco), ar=ar)
        stage_combine(S["T1"], S["SS"])


    def stage_router():
        ar = Arena()
        wf = ar.f32(8)
        rhs = ar.f32(TC)
        rb = ar.f32(1)
        orow = ar.f32(TC)
        for k in range(8):
            P.dma("sync", wf[:, k:k + 1], I["l1_rw"][k * 128:(k + 1) * 128, 0:1])
        P.dma("sync", rb[0:1, :], I["l1_rb"])
        P.barrier()
        for c in range(NCH):
            for k in range(8):
                ld = P.dma("sync", rhs[:], S["XN"][k * 128:(k + 1) * 128, c * TC:(c + 1) * TC])
                P.wait("tensor", "sync", ld)
                P.do("tensor", lambda e, k=k: e.matmul(out=pD[0:1, 0:TC], lhsT=wf[:, k:k + 1], rhs=rhs[:],
                                                       start=(k == 0), stop=(k == 7)))
                tmm = P.tick("tensor")
                P.wait("sync", "tensor", tmm)
            P.wait("scalar", "tensor", P.cnt["tensor"])
            P.do("scalar", lambda e: e.activation(out=orow[0:1, :], in_=pD[0:1, 0:TC], func=AF.Identity, bias=rb[0:1, 0:1]))
            P.wait("gpsimd", "scalar", P.tick("scalar"))
            P.dma("gpsimd", S["RWD"][0:1, c * TC:(c + 1) * TC], orow[0:1, :])
            P.wait("tensor", "gpsimd", P.cnt["gpsimd"])
            P.wait("scalar", "gpsimd", P.cnt["gpsimd"])
        P.barrier()

    def stage_topk():
        ar = Arena()
        rw = ar.f32(L)
        ge = ar.f32(L)
        lo = ar.f32(1); hi = ar.f32(1); mid = ar.f32(1); cnt_ = ar.f32(1); sel = ar.f32(1)
        P.dma("sync", rw[0:NSEQ, :], S["RWD"].rearrange("a (s l) -> (a s) l", s=NSEQ))
        P.wait("vector", "sync", P.cnt["sync"])
        P.do("vector", lambda e: e.tensor_reduce(out=lo[0:NSEQ, :], in_=rw[0:NSEQ, :], axis=AXX, op=ALU.min))
        P.do("vector", lambda e: e.tensor_scalar_add(lo[0:NSEQ, :], lo[0:NSEQ, :], -1.0))
        P.do("vector", lambda e: e.reduce_max(out=hi[0:NSEQ, :], in_=rw[0:NSEQ, :], axis=AXX))
        P.do("vector", lambda e: e.tensor_scalar_add(hi[0:NSEQ, :], hi[0:NSEQ, :], 1.0))
        for _ in range(48):
            P.do("vector", lambda e: e.tensor_add(mid[0:NSEQ, :], lo[0:NSEQ, :], hi[0:NSEQ, :]))
            P.do("vector", lambda e: e.tensor_scalar_mul(mid[0:NSEQ, :], mid[0:NSEQ, :], 0.5))
            P.do("vector", lambda e: e.tensor_scalar(ge[0:NSEQ, :], rw[0:NSEQ, :], mid[0:NSEQ, 0:1], None, ALU.is_ge))
            P.do("vector", lambda e: e.reduce_sum(out=cnt_[0:NSEQ, :], in_=ge[0:NSEQ, :], axis=AXX))
            selu = sel[:].bitcast(U32)
            P.do("vector", lambda e, selu=selu: e.tensor_scalar(selu[0:NSEQ, :], cnt_[0:NSEQ, :], float(L // 2), None, ALU.is_ge))
            P.do("vector", lambda e, selu=selu: e.select(out=lo[0:NSEQ, :], mask=selu[0:NSEQ, :], on_true=mid[0:NSEQ, :], on_false=lo[0:NSEQ, :]))
            P.do("vector", lambda e, selu=selu: e.select(out=hi[0:NSEQ, :], mask=selu[0:NSEQ, :], on_true=hi[0:NSEQ, :], on_false=mid[0:NSEQ, :]))
        P.do("vector", lambda e: e.tensor_scalar(ge[0:NSEQ, :], rw[0:NSEQ, :], lo[0:NSEQ, 0:1], None, ALU.is_ge))
        P.wait("gpsimd", "vector", P.tick("vector"))
        P.dma("gpsimd", S["MROW"], ge[0:NSEQ, :])
        P.barrier()

    def stage_s1():
        ar = Arena()
        mb = ar.f32(L); xv = ar.f32(L); rv = ar.f32(L)
        for s in range(NSEQ):
            P.dma("sync", mb[:], S["MROW"][s:s + 1, :].partition_broadcast(128)[:, 0, :])
            for j in range(8):
                r = slice(j * 128, (j + 1) * 128)
                ld = P.dma("sync", xv[:], S["XN"][r, s * L:(s + 1) * L])
                P.wait("vector", "sync", ld)
                P.do("vector", lambda e: e.tensor_copy(out=rv[:], in_=xv[:, ::-1]))
                P.do("vector", lambda e: e.tensor_mul(rv[:], rv[:], mb[:]))
                P.wait("gpsimd", "vector", P.tick("vector"))
                P.dma("gpsimd", S["SS"][r, s * L:(s + 1) * L], rv[:])
                P.wait("sync", "gpsimd", P.cnt["gpsimd"])
                P.wait("sync", "vector", P.cnt["vector"])
                P.wait("vector", "gpsimd", P.cnt["gpsimd"])
        P.barrier()

    def stage_qkv():
        ar = Arena()
        wb, wtile = load_weights_bf16(ar, I["l1f_wqkv"], D, 2 * D)
        rhs = [ar.bf16(TC) for _ in range(8)]
        of = ar.f32(TC); ob = ar.bf16(TC)
        ub = load_bias_cols(ar, I["l1f_ub"], D)
        vb = load_bias_cols(ar, I["l1f_vb"], D)
        ob2 = ar.bf16(TC)
        P.barrier()
        for c in range(NCH):
            for k in range(8):
                P.dma("sync", rhs[k][:], S["XNB"][k * 128:(k + 1) * 128, c * TC:(c + 1) * TC])
            P.wait("tensor", "sync", P.cnt["sync"])
            for m in range(16):
                bank = pB if (m % 2 == 0) else pC
                for k in range(8):
                    P.do("tensor", lambda e, k=k, m=m, bank=bank: e.matmul(
                        out=bank[:, 0:TC], lhsT=wtile(k, m), rhs=rhs[k][:], start=(k == 0), stop=(k == 7)))
                tmm = P.tick("tensor")
                P.wait("vector", "tensor", tmm)
                cs = slice(c * TC, (c + 1) * TC)
                if m < 8:
                    r = slice(m * 128, (m + 1) * 128)
                    P.do("vector", lambda e, m=m, bank=bank: e.tensor_scalar(ob[:], bank[:, 0:TC], ub[:, m:m + 1], 0.125, ALU.add, ALU.mult))
                    P.do("vector", lambda e, m=m, bank=bank: e.tensor_scalar(ob2[:], bank[:, 0:TC], vb[:, m:m + 1], 0.125, ALU.add, ALU.mult))
                    tv = P.tick("vector")
                    P.wait("gpsimd", "vector", tv)
                    P.dma("gpsimd", S["QU"][r, cs], ob[:])
                    P.dma("gpsimd", S["QV"][r, cs], ob2[:])
                else:
                    r = slice((m - 8) * 128, (m - 8 + 1) * 128)
                    P.do("vector", lambda e, bank=bank: e.tensor_copy(out=ob[:], in_=bank[:, 0:TC]))
                    tv = P.tick("vector")
                    P.wait("gpsimd", "vector", tv)
                    P.dma("gpsimd", S["KBD"][r, cs], ob[:])
                gp = P.cnt["gpsimd"]
                P.wait("tensor", "vector", P.cnt["vector"])
                P.wait("vector", "gpsimd", gp)
            P.barrier()
        wv = ar.bf16(8 * 1024)
        wstage = ar.f32(1024)
        for k in range(8):
            ld = P.dma("sync", wstage[:, 0:1024], I["l1f_wqkv"][k * 128:(k + 1) * 128, 2048:3072])
            P.wait("scalar", "sync", ld)
            P.do("scalar", lambda e, k=k: e.copy(out=wv[:, k * 1024:(k + 1) * 1024], in_=wstage[:, 0:1024]))
            P.tick("scalar")
            P.wait("sync", "scalar", P.cnt["scalar"])
        lhs = [ar.bf16(128) for _ in range(8)]
        P.barrier()
        for i in range(T // 128):
            for k in range(8):
                P.dma("sync", lhs[k][:], S["XNB"][k * 128:(k + 1) * 128, i * 128:(i + 1) * 128])
            P.wait("tensor", "sync", P.cnt["sync"])
            for nh in range(2):
                bank = pB if nh == 0 else pC
                for k in range(8):
                    P.do("tensor", lambda e, k=k, nh=nh, bank=bank: e.matmul(
                        out=bank[:, 0:TC], lhsT=lhs[k][:], rhs=wv[:, k * 1024 + nh * 512:k * 1024 + (nh + 1) * 512],
                        start=(k == 0), stop=(k == 7)))
                tmm = P.tick("tensor")
                P.wait("scalar", "tensor", tmm)
                P.do("scalar", lambda e, bank=bank: e.copy(out=ob[:], in_=bank[:, 0:TC]))
                ts = P.tick("scalar")
                P.wait("gpsimd", "scalar", ts)
                P.dma("gpsimd", S["VN"][i * 128:(i + 1) * 128, nh * 512:(nh + 1) * 512], ob[:])
                P.wait("tensor", "scalar", ts)
                P.wait("scalar", "gpsimd", P.cnt["gpsimd"])
            P.barrier()

    def stage_rt():
        ar = Arena()
        wb, wtile = load_weights_bf16(ar, I["l1f_wr"], D, D)
        rhs = [ar.bf16(TC) for _ in range(8)]
        pestage = ar.f32(TC)
        ob = ar.bf16(TC)
        P.barrier()
        for c in range(L // TC):
            for k in range(8):
                ld = P.dma("sync", pestage[:], I["PET"][k * 128:(k + 1) * 128, c * TC:(c + 1) * TC])
                P.wait("vector", "sync", ld)
                P.do("vector", lambda e, k=k: e.tensor_copy(out=rhs[k][:], in_=pestage[:]))
                P.wait("sync", "vector", P.tick("vector"))
            P.wait("tensor", "vector", P.cnt["vector"])
            for m in range(8):
                bank = pB if (m % 2 == 0) else pC
                for k in range(8):
                    P.do("tensor", lambda e, k=k, m=m, bank=bank: e.matmul(
                        out=bank[:, 0:TC], lhsT=wtile(k, m), rhs=rhs[k][:], start=(k == 0), stop=(k == 7)))
                tmm = P.tick("tensor")
                P.wait("scalar", "tensor", tmm)
                P.do("scalar", lambda e, bank=bank: e.copy(out=ob[:], in_=bank[:, 0:TC]))
                ts = P.tick("scalar")
                P.wait("gpsimd", "scalar", ts)
                P.dma("gpsimd", S["RTB"][m * 128:(m + 1) * 128, c * TC:(c + 1) * TC], ob[:])
                P.wait("tensor", "scalar", ts)
                P.wait("scalar", "gpsimd", P.cnt["gpsimd"])
            P.barrier()

    def stage_attn_l1():
        ar = Arena()
        qv_h = ar.bf16(L)
        qu_h = ar.bf16(L)
        kt_h = ar.bf16(L)
        rt_h = ar.bf16(L)
        vnh = ar.bf16(1024)
        qr_sh = ar.bf16(L)
        sc = ar.f32(L)
        ex = ar.f32(L)
        pbf = ar.bf16(L)
        ptr_sb = ar.bf16(L)
        stt = ar.f32(4)
        ob = ar.bf16(TC)
        ctxs = ar.bf16(128)
        for s in range(NSEQ):
            for h in range(16):
                hr = slice(h * 64, h * 64 + 64)
                cs = slice(s * L, (s + 1) * L)
                P.dma("sync", qv_h[0:64, :], S["QV"][hr, cs])
                P.dma("sync", qu_h[0:64, :], S["QU"][hr, cs])
                P.dma("sync", kt_h[0:64, :], S["KBD"][hr, cs])
                P.dma("sync", rt_h[0:64, :], S["RTB"][hr, :])
                P.barrier()
                for qt in range(16):
                    bank = pB if (qt % 2 == 0) else pC
                    for pc in range(4):
                        P.do("tensor", lambda e, qt=qt, pc=pc, bank=bank: e.matmul(
                            out=bank[:, 0:TC], lhsT=qv_h[0:64, qt * 128:(qt + 1) * 128],
                            rhs=rt_h[0:64, pc * TC:(pc + 1) * TC], start=True, stop=True))
                        tmm = P.tick("tensor")
                        P.wait("scalar", "tensor", tmm)
                        P.do("scalar", lambda e, bank=bank: e.copy(out=ob[:], in_=bank[:, 0:TC]))
                        ts = P.tick("scalar")
                        P.wait("gpsimd", "scalar", ts)
                        P.dma("gpsimd", S["QRS"][qt * 128:(qt + 1) * 128, pc * TC:(pc + 1) * TC], ob[:])
                        P.wait("tensor", "scalar", ts)
                        P.wait("scalar", "gpsimd", P.cnt["gpsimd"])
                for kc in range(16):
                    P.dma("sync", vnh[:, kc * 64:(kc + 1) * 64],
                          S["VN"][s * L + kc * 128:s * L + (kc + 1) * 128, h * 64:h * 64 + 64])
                P.barrier()
                for qt in range(16):
                    q0 = qt * 128
                    for pc in range(4):
                        P.do("tensor", lambda e, qt=qt, pc=pc: e.matmul(
                            out=pA[:, pc * TC:(pc + 1) * TC], lhsT=qu_h[0:64, qt * 128:(qt + 1) * 128],
                            rhs=kt_h[0:64, pc * TC:(pc + 1) * TC], start=True, stop=True))
                    tmm = P.tick("tensor")
                    P.dma("sync", qr_sh[:],
                          AP(S["QRS"].tensor, (q0 * (L - 1) + L - 1), [[L - 1, 128], [1, L]]))
                    ldq = P.cnt["sync"]
                    P.wait("vector", "tensor", tmm)
                    P.wait("vector", "sync", ldq)
                    P.do("vector", lambda e: e.tensor_add(sc[:], pA[:, 0:L], qr_sh[:]))
                    if q0 + 128 < L:
                        P.do("vector", lambda e, q0=q0: e.memset(sc[:, q0 + 128:L], NEG))
                    P.do("vector", lambda e, q0=q0: e.tensor_add(sc[:, q0:q0 + 128], sc[:, q0:q0 + 128], sb_tri[:]))
                    P.do("vector", lambda e: e.reduce_max(out=stt[:, 0:1], in_=sc[:], axis=AXX))
                    P.do("vector", lambda e: e.tensor_scalar_mul(stt[:, 1:2], stt[:, 0:1], -1.0))
                    P.wait("scalar", "vector", P.tick("vector"))
                    P.do("scalar", lambda e: e.activation(out=ex[:], in_=sc[:], func=AF.Exp, bias=stt[:, 1:2], accum_out=stt[:, 2:3]))
                    P.wait("vector", "scalar", P.tick("scalar"))
                    P.do("vector", lambda e: e.reciprocal(out=stt[:, 3:4], in_=stt[:, 2:3]))
                    P.do("vector", lambda e: e.tensor_scalar(pbf[:], ex[:], stt[:, 3:4], None, ALU.mult))
                    P.wait("tensor", "vector", P.tick("vector"))
                    for kc in range(16):
                        P.do("tensor", lambda e, kc=kc: e.transpose(
                            out=pT[:, 0:128], in_=pbf[:, kc * 128:(kc + 1) * 128], identity=sb_idtb[:]))
                        P.wait("scalar", "tensor", P.tick("tensor"))
                        P.do("scalar", lambda e, kc=kc: e.copy(out=ptr_sb[:, kc * 128:(kc + 1) * 128], in_=pT[:, 0:128]))
                        P.wait("tensor", "scalar", P.tick("scalar"))
                    P.wait("tensor", "scalar", P.cnt["scalar"])
                    for kc in range(16):
                        P.do("tensor", lambda e, kc=kc: e.matmul(
                            out=pD[0:64, 0:128], lhsT=vnh[:, kc * 64:(kc + 1) * 64],
                            rhs=ptr_sb[:, kc * 128:(kc + 1) * 128], start=(kc == 0), stop=(kc == 15)))
                    tmm = P.tick("tensor")
                    P.wait("scalar", "tensor", tmm)
                    P.do("scalar", lambda e: e.copy(out=ctxs[0:64, :], in_=pD[0:64, 0:128]))
                    ts = P.tick("scalar")
                    P.wait("gpsimd", "scalar", ts)
                    P.dma("gpsimd", S["CTX"][h * 64:h * 64 + 64, s * L + q0:s * L + q0 + 128], ctxs[0:64, :])
                    P.wait("tensor", "scalar", ts)
                    P.wait("scalar", "gpsimd", P.cnt["gpsimd"])
                    P.wait("vector", "scalar", P.cnt["scalar"])
                    P.wait("sync", "vector", P.cnt["vector"])
                P.barrier()

    def stage_addln(srcb, add_src, wt_ap, d_in, tdst, hdst, hbdst, gname, bname, bias_name=None):
        ar = Arena()
        bco = load_bias_cols(ar, I[bias_name], D) if bias_name else None

        def evac(m, c, ps, bufs):
            of = bufs["f32"]
            x2 = bufs["f32b"]
            cs = slice(c * TC, (c + 1) * TC)
            r = slice(m * 128, (m + 1) * 128)
            P.wait("sync", "vector", P.cnt["vector"])
            P.dma("sync", x2[:], add_src[r, cs])
            P.wait("scalar", "sync", P.cnt["sync"])
            P.wait("vector", "sync", P.cnt["sync"])
            if bco is not None:
                P.do("scalar", lambda e: e.activation(out=of[:], in_=ps, func=AF.Identity, bias=bco[:, m:m + 1]))
                P.wait("vector", "scalar", P.tick("scalar"))
                P.do("vector", lambda e: e.tensor_add(of[:], of[:], x2[:]))
            else:
                P.do("vector", lambda e: e.tensor_add(of[:], ps, x2[:]))
            tv = P.tick("vector")
            P.wait("gpsimd", "vector", tv)
            P.dma("gpsimd", tdst[r, cs], of[:])
            P.wait("sync", "gpsimd", P.cnt["gpsimd"])
        stage_matmul(srcb, d_in, D, wt_ap, evac, ar=ar)
        stage_ln(tdst, hdst, hbdst, gname, bname)

    def stage_gate_cw(src_f32, wg_name, bg_name, mode):
        ar = Arena()
        lhs = [ar.f32(128) for _ in range(8)]
        wg_all = ar.f32(64)
        pr = ar.f32(8); srt = ar.f32(8); cw = ar.f32(8)
        stt = ar.f32(4)
        bgb = ar.f32(8)
        cwt = ar.f32(128)
        for k in range(8):
            P.dma("sync", wg_all[:, k * 8:(k + 1) * 8], I[wg_name][k * 128:(k + 1) * 128, :])
        if bg_name:
            P.dma("sync", bgb[:], I[bg_name].partition_broadcast(128)[:, 0, :])
        P.barrier()
        for i in range(T // 128):
            for k in range(8):
                P.dma("sync", lhs[k][:], src_f32[k * 128:(k + 1) * 128, i * 128:(i + 1) * 128])
            P.wait("tensor", "sync", P.cnt["sync"])
            for k in range(8):
                P.do("tensor", lambda e, k=k: e.matmul(out=pD[:, 0:8], lhsT=lhs[k][:], rhs=wg_all[:, k * 8:(k + 1) * 8],
                                                       start=(k == 0), stop=(k == 7)))
            tmm = P.tick("tensor")
            P.wait("vector", "tensor", tmm)
            if bg_name:
                P.do("vector", lambda e: e.tensor_add(pr[:], pD[:, 0:8], bgb[:]))
            else:
                P.do("vector", lambda e: e.tensor_copy(out=pr[:], in_=pD[:, 0:8]))
            P.do("vector", lambda e: e.reduce_max(out=stt[:, 0:1], in_=pr[:], axis=AXX))
            P.do("vector", lambda e: e.tensor_scalar_mul(stt[:, 1:2], stt[:, 0:1], -1.0))
            P.wait("scalar", "vector", P.tick("vector"))
            P.do("scalar", lambda e: e.activation(out=pr[:], in_=pr[:], func=AF.Exp, bias=stt[:, 1:2], accum_out=stt[:, 2:3]))
            P.wait("vector", "scalar", P.tick("scalar"))
            P.do("vector", lambda e: e.reciprocal(out=stt[:, 3:4], in_=stt[:, 2:3]))
            P.do("vector", lambda e: e.tensor_scalar(pr[:], pr[:], stt[:, 3:4], None, ALU.mult))
            if mode == "argmax":
                P.do("vector", lambda e: e.reduce_max(out=stt[:, 0:1], in_=pr[:], axis=AXX))
                P.do("vector", lambda e: e.tensor_scalar(cw[:], pr[:], stt[:, 0:1], None, ALU.is_ge))
                P.do("vector", lambda e: e.tensor_scalar(cw[:], cw[:], stt[:, 0:1], None, ALU.mult))
            elif mode == "sort":
                P.do("vector", lambda e: e.max(out=cw[:], in_=pr[:]))
            else:
                P.do("vector", lambda e: e.max(out=srt[:], in_=pr[:]))
                P.do("vector", lambda e: e.tensor_scalar(cw[:], pr[:], srt[:, 1:2], None, ALU.is_ge))
                P.do("vector", lambda e: e.tensor_mul(cw[:], cw[:], pr[:]))
                P.do("vector", lambda e: e.tensor_add(stt[:, 0:1], srt[:, 0:1], srt[:, 1:2]))
                P.do("vector", lambda e: e.reciprocal(out=stt[:, 1:2], in_=stt[:, 0:1]))
                P.do("vector", lambda e: e.tensor_scalar(cw[:], cw[:], stt[:, 1:2], None, ALU.mult))
            P.wait("tensor", "vector", P.tick("vector"))
            P.do("tensor", lambda e: e.transpose(out=pB[0:8, 0:128], in_=cw[:], identity=sb_idt[:]))
            P.wait("scalar", "tensor", P.tick("tensor"))
            P.do("scalar", lambda e: e.copy(out=cwt[0:8, :], in_=pB[0:8, 0:128]))
            P.wait("gpsimd", "scalar", P.tick("scalar"))
            P.dma("gpsimd", S["CW"][0:8, i * 128:(i + 1) * 128], cwt[0:8, :])
            gp = P.cnt["gpsimd"]
            P.wait("tensor", "scalar", P.cnt["scalar"])
            P.wait("scalar", "gpsimd", gp)
            P.wait("vector", "scalar", P.cnt["scalar"])
            P.wait("sync", "vector", P.cnt["vector"])
        P.barrier()

    def stage_moe(src_b, we1_ap, be1_ap, we2_ap, be2_ap, d_hid, act, dst):
        kt1, mt1 = D // 128, d_hid // 128
        kt2, mt2 = d_hid // 128, D // 128
        ar = Arena()
        w1b = ar.bf16(kt1 * mt1 * 128)
        w2b = ar.bf16(kt2 * mt2 * 128)
        wstage = ar.f32(1024)
        rhs = [ar.bf16(TC) for _ in range(kt1)]
        hid = [ar.bf16(TC) for _ in range(mt1)]
        cwb = ar.f32(TC)
        be1c = ar.f32(mt1)
        be2row = ar.bf16(D)
        cwrow = ar.bf16(TC)
        cwstage = ar.f32(TC)
        of = ar.f32(TC)
        of2 = ar.f32(TC)
        ncol1 = min(d_hid, 1024)
        for ei in range(8):
            for k in range(kt1):
                for c0 in range(0, d_hid, ncol1):
                    ld = P.dma("sync", wstage[:, 0:ncol1], we1_ap[ei, k * 128:(k + 1) * 128, c0:c0 + ncol1])
                    P.wait("scalar", "sync", ld)
                    for mm_ in range(ncol1 // 128):
                        m = c0 // 128 + mm_
                        P.do("scalar", lambda e, k=k, m=m, mm_=mm_: e.copy(
                            out=w1b[:, (k * mt1 + m) * 128:(k * mt1 + m) * 128 + 128],
                            in_=wstage[:, mm_ * 128:(mm_ + 1) * 128]))
                    P.tick("scalar")
                    P.wait("sync", "scalar", P.cnt["scalar"])
            for k in range(kt2):
                ld = P.dma("sync", wstage[:, 0:D], we2_ap[ei, k * 128:(k + 1) * 128, :])
                P.wait("scalar", "sync", ld)
                for m in range(mt2):
                    P.do("scalar", lambda e, k=k, m=m: e.copy(
                        out=w2b[:, (k * mt2 + m) * 128:(k * mt2 + m) * 128 + 128], in_=wstage[:, m * 128:(m + 1) * 128]))
                P.tick("scalar")
                P.wait("sync", "scalar", P.cnt["scalar"])
            for m in range(mt1):
                P.dma("sync", be1c[:, m:m + 1], be1_ap[ei, m * 128:(m + 1) * 128])
            ld = P.dma("sync", wstage[0:1, 0:D], be2_ap[ei:ei + 1, :])
            P.wait("vector", "sync", ld)
            P.do("vector", lambda e: e.tensor_copy(out=be2row[0:1, :], in_=wstage[0:1, 0:D]))
            P.tick("vector")
            P.barrier()
            for c in range(NCH):
                cs = slice(c * TC, (c + 1) * TC)
                for k in range(kt1):
                    P.dma("sync", rhs[k][:], src_b[k * 128:(k + 1) * 128, cs])
                P.dma("sync", cwstage[0:1, :], S["CW"][ei:ei + 1, cs])
                ldc = P.cnt["sync"]
                P.wait("vector", "sync", ldc)
                P.do("vector", lambda e: e.tensor_copy(out=cwrow[0:1, :], in_=cwstage[0:1, :]))
                P.tick("vector")
                P.wait("sync", "vector", P.cnt["vector"])
                P.dma("sync", cwb[:], S["CW"][ei:ei + 1, cs].partition_broadcast(128)[:, 0, :])
                ldb = P.cnt["sync"]
                P.wait("tensor", "sync", ldb)
                histm = {}
                for m in range(mt1):
                    bi = m % 2
                    bank = pB if bi == 0 else pC
                    ob_ = of if bi == 0 else of2
                    if bi in histm:
                        P.wait("tensor", "vector", histm[bi])
                    for k in range(kt1):
                        P.do("tensor", lambda e, k=k, m=m, bank=bank: e.matmul(
                            out=bank[:, 0:TC], lhsT=w1b[:, (k * mt1 + m) * 128:(k * mt1 + m) * 128 + 128],
                            rhs=rhs[k][:], start=(k == 0), stop=(k == kt1 - 1)))
                    tmm = P.tick("tensor")
                    P.wait("scalar", "tensor", tmm)
                    if bi in histm:
                        P.wait("scalar", "vector", histm[bi])
                    P.do("scalar", lambda e, m=m, bank=bank, ob_=ob_: e.activation(out=ob_[:], in_=bank[:, 0:TC], func=act, bias=be1c[:, m:m + 1]))
                    P.wait("vector", "scalar", P.tick("scalar"))
                    P.do("vector", lambda e, m=m, ob_=ob_: e.tensor_mul(hid[m][:], ob_[:], cwb[:]))
                    histm[bi] = P.tick("vector")
                P.wait("tensor", "vector", P.cnt["vector"])
                histe = {}
                for m in range(mt2):
                    bi = m % 2
                    bank = pB if bi == 0 else pC
                    ob_ = of if bi == 0 else of2
                    if bi in histe:
                        P.wait("tensor", "scalar", histe[bi][0])
                    P.do("tensor", lambda e, m=m, bank=bank: e.matmul(
                        out=bank[:, 0:TC], lhsT=be2row[0:1, m * 128:(m + 1) * 128], rhs=cwrow[0:1, :],
                        start=True, stop=False))
                    for k in range(kt2):
                        P.do("tensor", lambda e, k=k, m=m, bank=bank: e.matmul(
                            out=bank[:, 0:TC], lhsT=w2b[:, (k * mt2 + m) * 128:(k * mt2 + m) * 128 + 128],
                            rhs=hid[k][:], start=False, stop=(k == kt2 - 1)))
                    tmm = P.tick("tensor")
                    P.wait("scalar", "tensor", tmm)
                    if bi in histe:
                        P.wait("scalar", "gpsimd", histe[bi][1])
                    P.do("scalar", lambda e, bank=bank, ob_=ob_: e.copy(out=ob_[:], in_=bank[:, 0:TC]))
                    ts = P.tick("scalar")
                    P.wait("gpsimd", "scalar", ts)
                    if ei == 0:
                        P.dma("gpsimd", dst[m * 128:(m + 1) * 128, cs], ob_[:])
                    else:
                        P.dma("gpsimd", dst[m * 128:(m + 1) * 128, cs], ob_[:], accum_op=ALU.add)
                    histe[bi] = (ts, P.cnt["gpsimd"])
                P.barrier()

    def layer1_full():
        stage_ln(S["XR"], S["XN"], S["XNB"])
        stage_router()
        stage_topk()
        stage_s1()
        stage_qkv()
        stage_rt()
        stage_attn_l1()
        stage_addln(S["CTX"], S["XN"], I["l1f_wo"], D, S["T1"], S["H1"], S["H1B"], "l1f_ln1g", "l1f_ln1b")
        ar = Arena()
        b1 = load_bias_cols(ar, I["l1f_bff1"], 4096)
        stage_matmul(S["H1B"], D, 4096, I["l1f_wff1"], ev_store(dstb=S["FF1B"], act=AF.Relu, bias_sb=b1), ar=ar)
        stage_addln(S["FF1B"], S["H1"], I["l1f_wff2"], 4096, S["T2"], S["H2"], S["H2B"], "l1f_ln2g", "l1f_ln2b", bias_name="l1f_bff2")
        stage_gate_cw(S["H2"], "l1f_wg", None, "argmax")
        stage_moe(S["H2B"], I["l1f_we1"], I["l1f_be1"], I["l1f_we2"], I["l1f_be2"], 4096, AF.Relu, S["MACC"])
        stage_add(S["H2"], S["MACC"], S["T3"])
        stage_ln(S["T3"], S["H3"], S["H3B"], "l1f_ln3g", "l1f_ln3b")
        stage_matmul(S["H3B"], D, D, I["l1f_wout"], ev_store(dstf=S["T1"]))
        stage_combine(S["T1"], S["SS"])

    def layer1_partial():
        stage_ln(S["XR"], S["XN"], S["XNB"])
        stage_router()
        stage_topk()
        stage_s1()
        stage_qkv()
        stage_rt()
        stage_attn_l1()
        stage_addln(S["CTX"], S["XN"], I["l1f_wo"], D, S["T1"], S["H1"], S["H1B"], "l1f_ln1g", "l1f_ln1b")

    def stage_add(a_src, b_src, dst):
        ar = Arena()
        at = ar.f32(TC); bt = ar.f32(TC)
        for c in range(NCH):
            for j in range(8):
                r = slice(j * 128, (j + 1) * 128)
                cs = slice(c * TC, (c + 1) * TC)
                P.dma("sync", at[:], a_src[r, cs])
                P.dma("sync", bt[:], b_src[r, cs])
                P.wait("vector", "sync", P.cnt["sync"])
                P.do("vector", lambda e: e.tensor_add(at[:], at[:], bt[:]))
                P.wait("gpsimd", "vector", P.tick("vector"))
                P.dma("gpsimd", dst[r, cs], at[:])
                P.wait("sync", "gpsimd", P.cnt["gpsimd"])
                P.wait("sync", "vector", P.cnt["vector"])
        P.barrier()


    def stage_l2_pre():
        ar = Arena()
        a = ar.f32(64); b = ar.f32(64); t = ar.f32(8); u = ar.f32(8)
        P.dma("sync", a[0:1, :], I["l2_lq1"]); P.dma("sync", b[0:1, :], I["l2_lk1"])
        P.wait("vector", "sync", P.cnt["sync"])
        P.do("vector", lambda e: e.tensor_mul(a[0:1, :], a[0:1, :], b[0:1, :]))
        P.do("vector", lambda e: e.reduce_sum(out=t[0:1, 0:1], in_=a[0:1, :], axis=AXX))
        P.wait("sync", "vector", P.tick("vector"))
        P.dma("sync", a[0:1, :], I["l2_lq2"]); P.dma("sync", b[0:1, :], I["l2_lk2"])
        P.wait("vector", "sync", P.cnt["sync"])
        P.do("vector", lambda e: e.tensor_mul(a[0:1, :], a[0:1, :], b[0:1, :]))
        P.do("vector", lambda e: e.reduce_sum(out=t[0:1, 1:2], in_=a[0:1, :], axis=AXX))
        P.wait("scalar", "vector", P.tick("vector"))
        P.do("scalar", lambda e: e.activation(out=t[0:1, 2:3], in_=t[0:1, 0:1], func=AF.Exp))
        P.do("scalar", lambda e: e.activation(out=t[0:1, 3:4], in_=t[0:1, 1:2], func=AF.Exp))
        P.wait("vector", "scalar", P.tick("scalar"))
        P.do("vector", lambda e: e.tensor_sub(t[0:1, 4:5], t[0:1, 3:4], t[0:1, 2:3]))
        P.do("vector", lambda e: e.tensor_scalar_add(t[0:1, 4:5], t[0:1, 4:5], -0.5))
        P.wait("gpsimd", "vector", P.tick("vector"))
        P.dma("gpsimd", S["NLAM"], t[0:1, 4:5])
        P.wait("sync", "vector", P.cnt["vector"])
        P.dma("sync", u[0:1, 0:1], I["l2_clsr"])
        P.dma("sync", a[0:1, 0:8], I["l2_chd"])
        P.wait("scalar", "sync", P.cnt["sync"])
        P.do("scalar", lambda e: e.activation(out=u[0:1, 1:2], in_=u[0:1, 0:1], func=AF.Sigmoid))
        P.do("scalar", lambda e: e.mul(out=u[0:1, 1:2], in_=u[0:1, 1:2], mul=15.0))
        P.wait("vector", "scalar", P.tick("scalar"))
        P.do("vector", lambda e: e.reduce_sum(out=u[0:1, 2:3], in_=a[0:1, 0:8], axis=AXX))
        P.do("vector", lambda e: e.tensor_scalar_mul(u[0:1, 2:3], u[0:1, 2:3], 1.0 / 8))
        P.do("vector", lambda e: e.tensor_scalar(a[0:1, 0:8], a[0:1, 0:8], u[0:1, 2:3], None, ALU.subtract))
        P.wait("scalar", "vector", P.tick("vector"))
        P.do("scalar", lambda e: e.activation(out=a[0:1, 0:8], in_=a[0:1, 0:8], func=AF.Tanh))
        P.do("scalar", lambda e: e.activation(out=a[0:1, 0:8], in_=a[0:1, 0:8], func=AF.Identity, bias=1.0, scale=0.5))
        P.wait("vector", "scalar", P.tick("scalar"))
        P.do("vector", lambda e: e.tensor_scalar(a[0:1, 0:8], a[0:1, 0:8], u[0:1, 1:2], None, ALU.mult))
        P.wait("gpsimd", "vector", P.tick("vector"))
        P.dma("gpsimd", S["COSS"], a[0:1, 0:8])
        P.barrier()

    def stage_l2_norms():
        ar = Arena()
        xt = ar.f32(TC); sq = ar.f32(TC); row = ar.f32(TC)
        P.barrier()
        for (src, base) in ((S["QT2"], 0), (S["KT2"], 16)):
            for h in range(8):
                for c in range(NCH):
                    ld = P.dma("sync", xt[:], src[h * 128:(h + 1) * 128, c * TC:(c + 1) * TC])
                    P.wait("scalar", "sync", ld)
                    P.do("scalar", lambda e: e.activation(out=sq[:], in_=xt[:], func=AF.Square))
                    P.wait("tensor", "scalar", P.tick("scalar"))
                    for half in range(2):
                        bank = pC if half == 0 else pD
                        P.do("tensor", lambda e, half=half, bank=bank: e.matmul(
                            out=bank[0:1, 0:TC], lhsT=sb_ones[half * 64:half * 64 + 64, :],
                            rhs=sq[half * 64:half * 64 + 64, :], start=True, stop=True))
                        tmm = P.tick("tensor")
                        P.wait("scalar", "tensor", tmm)
                        P.do("scalar", lambda e, bank=bank: e.activation(out=row[0:1, :], in_=bank[0:1, 0:TC], func=AF.Sqrt))
                        P.wait("vector", "scalar", P.tick("scalar"))
                        P.do("vector", lambda e: e.tensor_scalar_max(row[0:1, :], row[0:1, :], 1e-6))
                        P.do("vector", lambda e: e.reciprocal(out=row[0:1, :], in_=row[0:1, :]))
                        P.wait("gpsimd", "vector", P.tick("vector"))
                        P.dma("gpsimd", S["NRM"][base + 2 * h + half:base + 2 * h + half + 1, c * TC:(c + 1) * TC], row[0:1, :])
                        P.wait("scalar", "gpsimd", P.cnt["gpsimd"])
                        P.wait("vector", "gpsimd", P.cnt["gpsimd"])
                        P.wait("tensor", "scalar", P.cnt["scalar"])
                    P.wait("scalar", "tensor", P.cnt["tensor"])
                    P.wait("sync", "scalar", P.cnt["scalar"])
        P.barrier()

    def stage_attn_l2():
        ar = Arena()
        q_h = ar.bf16(L)
        k_h = ar.bf16(L)
        vnh = ar.bf16(2048)
        rk1 = ar.f32(L); rk2 = ar.f32(L)
        rq = ar.f32(32)
        l1s = ar.f32(L); l2s = ar.f32(L); gg = ar.f32(L); ex = ar.f32(L)
        pbf = ar.bf16(L); ptr_sb = ar.bf16(L)
        stt = ar.f32(8)
        csc = ar.f32(4)
        sq = ar.f32(128); row = ar.f32(128); rb = ar.f32(128)
        ctxs = ar.bf16(128)
        subw = ar.f32(1)
        ones_loc = sb_ones
        P.dma("sync", subw[:, 0:1], I["l2_subw"])
        P.barrier()
        for s in range(NSEQ):
            cs = slice(s * L, (s + 1) * L)
            for h in range(8):
                hr = slice(h * 128, (h + 1) * 128)
                P.dma("sync", q_h[:], S["QT2B"][hr, cs])
                P.dma("sync", k_h[:], S["KT2B"][hr, cs])
                P.dma("sync", rk1[:], S["NRM"][16 + 2 * h:17 + 2 * h, cs].partition_broadcast(128)[:, 0, :])
                P.dma("sync", rk2[:], S["NRM"][17 + 2 * h:18 + 2 * h, cs].partition_broadcast(128)[:, 0, :])
                for half in range(2):
                    P.dma("sync", sq[0:16, 0:128], AP(S["NRM"].tensor, (2 * h + half) * T + s * L, [[128, 16], [1, 128]]))
                    P.wait("tensor", "sync", P.cnt["sync"])
                    P.do("tensor", lambda e: e.transpose(out=pB[:, 0:16], in_=sq[0:16, 0:128], identity=sb_idt[0:16, 0:16]))
                    P.wait("scalar", "tensor", P.tick("tensor"))
                    P.do("scalar", lambda e, half=half: e.copy(out=rq[:, half * 16:half * 16 + 16], in_=pB[:, 0:16]))
                    P.wait("sync", "scalar", P.tick("scalar"))
                    P.wait("tensor", "scalar", P.cnt["scalar"])
                P.dma("sync", csc[:, 0:1], AP(S["COSS"].tensor, h, [[0, 128], [1, 1]]))
                P.dma("sync", csc[:, 1:2], AP(I["l2_dg"].tensor, h, [[0, 128], [1, 1]]))
                P.dma("sync", csc[:, 2:3], AP(S["NLAM"].tensor, 0, [[0, 128], [1, 1]]))
                for kc in range(16):
                    P.dma("sync", vnh[:, kc * 128:(kc + 1) * 128],
                          S["VN2"][s * L + kc * 128:s * L + (kc + 1) * 128, h * 128:(h + 1) * 128])
                P.barrier()
                for qt in range(16):
                    q0 = qt * 128
                    for pc in range(4):
                        P.do("tensor", lambda e, qt=qt, pc=pc: e.matmul(
                            out=pA[:, pc * TC:(pc + 1) * TC], lhsT=q_h[0:64, qt * 128:(qt + 1) * 128],
                            rhs=k_h[0:64, pc * TC:(pc + 1) * TC], start=True, stop=True))
                    tmm = P.tick("tensor")
                    P.wait("vector", "tensor", tmm)
                    P.do("vector", lambda e, qt=qt: e.tensor_scalar(l1s[:], pA[:, 0:L], rq[:, qt:qt + 1], csc[:, 0:1], ALU.mult, ALU.mult))
                    P.do("vector", lambda e: e.tensor_mul(l1s[:], l1s[:], rk1[:]))
                    tv = P.tick("vector")
                    P.wait("tensor", "vector", tv)
                    for pc in range(4):
                        P.do("tensor", lambda e, qt=qt, pc=pc: e.matmul(
                            out=pA[:, pc * TC:(pc + 1) * TC], lhsT=q_h[64:128, qt * 128:(qt + 1) * 128],
                            rhs=k_h[64:128, pc * TC:(pc + 1) * TC], start=True, stop=True))
                    tmm = P.tick("tensor")
                    P.wait("vector", "tensor", tmm)
                    P.do("vector", lambda e, qt=qt: e.tensor_scalar(l2s[:], pA[:, 0:L], rq[:, 16 + qt:17 + qt], csc[:, 0:1], ALU.mult, ALU.mult))
                    P.do("vector", lambda e: e.tensor_mul(l2s[:], l2s[:], rk2[:]))
                    P.do("vector", lambda e: e.tensor_sub(gg[:], l1s[:], l2s[:]))
                    P.wait("scalar", "vector", P.tick("vector"))
                    P.do("scalar", lambda e: e.activation(out=gg[:], in_=gg[:], func=AF.Tanh, scale=csc[:, 1:2]))
                    P.wait("vector", "scalar", P.tick("scalar"))
                    P.do("vector", lambda e: e.tensor_add(l1s[:], l1s[:], gg[:]))
                    P.do("vector", lambda e: e.tensor_sub(l2s[:], l2s[:], gg[:]))
                    P.do("vector", lambda e: e.reduce_max(out=stt[:, 0:1], in_=l1s[:], axis=AXX))
                    P.do("vector", lambda e: e.tensor_scalar_mul(stt[:, 1:2], stt[:, 0:1], -1.0))
                    P.wait("scalar", "vector", P.tick("vector"))
                    P.do("scalar", lambda e: e.activation(out=ex[:], in_=l1s[:], func=AF.Exp, bias=stt[:, 1:2], accum_out=stt[:, 2:3]))
                    P.wait("vector", "scalar", P.tick("scalar"))
                    P.do("vector", lambda e: e.reciprocal(out=stt[:, 3:4], in_=stt[:, 2:3]))
                    P.do("vector", lambda e: e.tensor_scalar(l1s[:], ex[:], stt[:, 3:4], None, ALU.mult))
                    P.do("vector", lambda e: e.reduce_max(out=stt[:, 4:5], in_=l2s[:], axis=AXX))
                    P.do("vector", lambda e: e.tensor_scalar_mul(stt[:, 5:6], stt[:, 4:5], -1.0))
                    P.wait("scalar", "vector", P.tick("vector"))
                    P.do("scalar", lambda e: e.activation(out=ex[:], in_=l2s[:], func=AF.Exp, bias=stt[:, 5:6], accum_out=stt[:, 6:7]))
                    P.wait("vector", "scalar", P.tick("scalar"))
                    P.do("vector", lambda e: e.reciprocal(out=stt[:, 7:8], in_=stt[:, 6:7]))
                    P.do("vector", lambda e: e.tensor_scalar(ex[:], ex[:], stt[:, 7:8], None, ALU.mult))
                    P.do("vector", lambda e: e.scalar_tensor_tensor(pbf[:], ex[:], csc[:, 2:3], l1s[:], ALU.mult, ALU.add))
                    P.wait("tensor", "vector", P.tick("vector"))
                    for kc in range(16):
                        P.do("tensor", lambda e, kc=kc: e.transpose(
                            out=pT[:, 0:128], in_=pbf[:, kc * 128:(kc + 1) * 128], identity=sb_idtb[:]))
                        P.wait("scalar", "tensor", P.tick("tensor"))
                        P.do("scalar", lambda e, kc=kc: e.copy(out=ptr_sb[:, kc * 128:(kc + 1) * 128], in_=pT[:, 0:128]))
                        P.wait("tensor", "scalar", P.tick("scalar"))
                    for kc in range(16):
                        P.do("tensor", lambda e, kc=kc: e.matmul(
                            out=pB[:, 0:128], lhsT=vnh[:, kc * 128:(kc + 1) * 128],
                            rhs=ptr_sb[:, kc * 128:(kc + 1) * 128], start=(kc == 0), stop=(kc == 15)))
                    tmm = P.tick("tensor")
                    P.wait("scalar", "tensor", tmm)
                    P.do("scalar", lambda e: e.activation(out=sq[:], in_=pB[:, 0:128], func=AF.Square))
                    P.wait("tensor", "scalar", P.tick("scalar"))
                    P.do("tensor", lambda e: e.matmul(out=pD[0:1, 0:128], lhsT=ones_loc[:], rhs=sq[:], start=True, stop=True))
                    tmm2 = P.tick("tensor")
                    P.wait("scalar", "tensor", tmm2)
                    P.do("scalar", lambda e: e.activation(out=row[0:1, 0:128], in_=pD[0:1, 0:128], func=AF.Sqrt, bias=1e-5, scale=1.0 / 128))
                    P.wait("vector", "scalar", P.tick("scalar"))
                    P.do("vector", lambda e: e.reciprocal(out=row[0:1, 0:128], in_=row[0:1, 0:128]))
                    P.wait("gpsimd", "vector", P.tick("vector"))
                    P.wait("gpsimd", "sync", P.cnt["sync"])
                    P.dma("gpsimd", S["LNROW"][0:1, 0:128], row[0:1, 0:128])
                    P.wait("sync", "gpsimd", P.cnt["gpsimd"])
                    P.dma("sync", rb[:, 0:128], S["LNROW"][0:1, 0:128].partition_broadcast(128)[:, 0, :])
                    P.wait("vector", "sync", P.cnt["sync"])
                    P.do("vector", lambda e: e.tensor_mul(sq[:], pB[:, 0:128], rb[:, 0:128]))
                    P.do("vector", lambda e: e.tensor_scalar(ctxs[:], sq[:], subw[:, 0:1], 0.5, ALU.mult, ALU.mult))
                    P.wait("gpsimd", "vector", P.tick("vector"))
                    P.dma("gpsimd", S["CTX2"][hr, s * L + q0:s * L + q0 + 128], ctxs[:])
                    gp = P.cnt["gpsimd"]
                    P.wait("tensor", "vector", P.cnt["vector"])
                    P.wait("vector", "gpsimd", gp)
                    P.wait("scalar", "gpsimd", gp)
                P.barrier()

    def layer2_full():
        stage_ln(S["XR"], S["XN"], S["XNB"])
        stage_l2_pre()
        stage_matmul(S["XNB"], D, D, I["l2_wq"], ev_store(dstf=S["QT2"], dstb=S["QT2B"]))
        stage_matmul(S["XNB"], D, D, I["l2_wk"], ev_store(dstf=S["KT2"], dstb=S["KT2B"]))
        stage_vnat(S["XNB"], I["l2_wv"], S["VN2"])
        stage_l2_norms()
        stage_attn_l2()
        ar = Arena()
        bo = load_bias_cols(ar, I["l2_bo"], D)
        stage_matmul(S["CTX2"], D, D, I["l2_wo"], ev_store(dstf=S["SS"], bias_sb=bo), ar=ar)
        stage_gate_cw(S["XN"], "l2f_wg", "l2f_bg", "sort")
        stage_moe(S["XNB"], I["l2f_w1"], I["l2f_b1"], I["l2f_w2"], I["l2f_b2"], 256, AF.Gelu, S["MACC"])
        stage_combine(S["MACC"], S["SS"])

    def stage_vnat(src_b, w_ap, dst):
        ar = Arena()
        wv = ar.bf16(8 * 1024)
        wstage = ar.f32(1024)
        ob = ar.bf16(TC)
        lhs = [ar.bf16(128) for _ in range(8)]
        for k in range(8):
            ld = P.dma("sync", wstage[:, 0:1024], w_ap[k * 128:(k + 1) * 128, :])
            P.wait("scalar", "sync", ld)
            P.do("scalar", lambda e, k=k: e.copy(out=wv[:, k * 1024:(k + 1) * 1024], in_=wstage[:, 0:1024]))
            P.tick("scalar")
            P.wait("sync", "scalar", P.cnt["scalar"])
        P.barrier()
        for i in range(T // 128):
            for k in range(8):
                P.dma("sync", lhs[k][:], src_b[k * 128:(k + 1) * 128, i * 128:(i + 1) * 128])
            P.wait("tensor", "sync", P.cnt["sync"])
            for nh in range(2):
                bank = pB if nh == 0 else pC
                for k in range(8):
                    P.do("tensor", lambda e, k=k, nh=nh, bank=bank: e.matmul(
                        out=bank[:, 0:TC], lhsT=lhs[k][:], rhs=wv[:, k * 1024 + nh * 512:k * 1024 + (nh + 1) * 512],
                        start=(k == 0), stop=(k == 7)))
                tmm = P.tick("tensor")
                P.wait("scalar", "tensor", tmm)
                P.do("scalar", lambda e, bank=bank: e.copy(out=ob[:], in_=bank[:, 0:TC]))
                ts = P.tick("scalar")
                P.wait("gpsimd", "scalar", ts)
                P.dma("gpsimd", dst[i * 128:(i + 1) * 128, nh * 512:(nh + 1) * 512], ob[:])
                P.wait("tensor", "scalar", ts)
                P.wait("scalar", "gpsimd", P.cnt["gpsimd"])
            P.barrier()


    def stage_flip(srcb, dstb):
        ar = Arena()
        xv = ar.bf16(L); rv = ar.bf16(L)
        for s in range(NSEQ):
            for j in range(8):
                r = slice(j * 128, (j + 1) * 128)
                ld = P.dma("sync", xv[:], srcb[r, s * L:(s + 1) * L])
                P.wait("vector", "sync", ld)
                P.do("vector", lambda e: e.tensor_copy(out=rv[:], in_=xv[:, ::-1]))
                P.wait("gpsimd", "vector", P.tick("vector"))
                P.dma("gpsimd", dstb[r, s * L:(s + 1) * L], rv[:])
                P.wait("sync", "gpsimd", P.cnt["gpsimd"])
                P.wait("sync", "vector", P.cnt["vector"])
                P.wait("vector", "gpsimd", P.cnt["gpsimd"])
        P.barrier()

    def _ev_split_uz_l3(pfx):
        def evac(m, c, ps, bufs):
            of = bufs["f32"]
            P.do("scalar", lambda e: e.copy(out=of[:], in_=ps))
            ts = P.tick("scalar")
            P.wait("gpsimd", "scalar", ts)
            if m < 16:
                P.dma("gpsimd", S["B0"][m * 128:(m + 1) * 128, c * TC:(c + 1) * TC], of[:])
            else:
                P.dma("gpsimd", S["B1"][(m - 16) * 128:(m - 16 + 1) * 128, c * TC:(c + 1) * TC], of[:])
        return evac

    def mamba_l3(pfx, src_b, out_mode):
        stage_matmul(src_b, D, 2 * DI, I[f"l3_{pfx}_inw"], _ev_split_uz_l3(pfx))
        stage_conv_l3(pfx)
        stage_xproj(f"l3_{pfx}_xpw")
        stage_dt(f"l3_{pfx}_dtw", f"l3_{pfx}_dtb", double_sp=False)
        stage_scan(f"l3_{pfx}_alog", f"l3_{pfx}_d")

        if out_mode == "fw":
            stage_matmul(S["YB"], DI, D, I[f"l3_{pfx}_ow"], ev_store(dstf=S["F3"], scale=0.5))
        else:
            def evac(m, c, ps, bufs):
                of, o2 = bufs["f32"], bufs["f32b"]
                P.do("scalar", lambda e: e.mul(out=of[:], in_=ps, mul=0.5))
                P.wait("vector", "scalar", P.tick("scalar"))
                P.do("vector", lambda e: e.tensor_copy(out=o2[:], in_=of[:, ::-1]))
                tv = P.tick("vector")
                P.wait("gpsimd", "vector", tv)
                sidx = (c * TC) // L
                off = (c * TC) % L
                dst0 = sidx * L + (L - off - TC)
                P.dma("gpsimd", S["F3"][m * 128:(m + 1) * 128, dst0:dst0 + TC], o2[:], accum_op=ALU.add)
                P.wait("scalar", "gpsimd", P.cnt["gpsimd"])
                P.wait("vector", "gpsimd", P.cnt["gpsimd"])
            stage_matmul(S["YB"], DI, D, I[f"l3_{pfx}_ow"], evac)

    def stage_final_combine_l3():
        ar = Arena()
        f3 = ar.f32(TC); xn = ar.f32(TC); mc = ar.f32(TC); xt = ar.f32(TC)
        for c in range(NCH):
            for j in range(8):
                r = slice(j * 128, (j + 1) * 128)
                cs = slice(c * TC, (c + 1) * TC)
                P.dma("sync", f3[:], S["F3"][r, cs])
                P.dma("sync", xn[:], S["XN"][r, cs])
                P.dma("sync", mc[:], S["MACC"][r, cs])
                P.dma("sync", xt[:], S["XR"][r, cs])
                P.wait("vector", "sync", P.cnt["sync"])
                P.do("vector", lambda e: e.tensor_add(f3[:], f3[:], xn[:]))
                P.do("vector", lambda e: e.tensor_add(mc[:], mc[:], xn[:]))
                P.wait("scalar", "vector", P.tick("vector"))
                P.do("scalar", lambda e: e.activation(out=mc[:], in_=mc[:], func=AF.Silu))
                P.wait("vector", "scalar", P.tick("scalar"))
                P.do("vector", lambda e: e.tensor_mul(f3[:], f3[:], mc[:]))
                P.do("vector", lambda e: e.tensor_add(xt[:], xt[:], f3[:]))
                P.wait("gpsimd", "vector", P.tick("vector"))
                P.dma("gpsimd", S["XR"][r, cs], xt[:])
                P.wait("sync", "gpsimd", P.cnt["gpsimd"])
                P.wait("sync", "vector", P.cnt["vector"])
                P.wait("sync", "scalar", P.cnt["scalar"])
        P.barrier()

    def layer3_full():
        stage_ln(S["XR"], S["XN"], S["XNB"])
        stage_ln(S["XN"], S["WK"], S["WKB"], "l3_g", "l3_b")
        mamba_l3("fw", S["WKB"], "fw")
        stage_flip(S["WKB"], S["WKRB"])
        mamba_l3("bw", S["WKRB"], "bw")
        stage_ln(S["XN"], S["XG"], S["XGB"], "l3f_gam", "l3f_bet")
        stage_gate_cw(S["XG"], "l3f_gw", None, "top2")
        stage_moe(S["XGB"], I["l3f_w1"], I["l3f_b1"], I["l3f_w2"], I["l3f_b2"], 4096, AF.Gelu, S["MACC"])
        stage_final_combine_l3()

    def stage_output():
        ar = Arena()
        xt = ar.f32(1024)
        ot = ar.f32(1024)
        for i in range(T // 128):
            for j in range(8):
                P.dma("sync", xt[:, j * 128:(j + 1) * 128], S["XR"][j * 128:(j + 1) * 128, i * 128:(i + 1) * 128])
            P.wait("tensor", "sync", P.cnt["sync"])
            for j in range(8):
                P.do("tensor", lambda e, j=j: e.transpose(out=pB[:, 0:128], in_=xt[:, j * 128:(j + 1) * 128], identity=sb_idt[:]))
                P.wait("scalar", "tensor", P.tick("tensor"))
                P.do("scalar", lambda e, j=j: e.copy(out=ot[:, j * 128:(j + 1) * 128], in_=pB[:, 0:128]))
                P.wait("tensor", "scalar", P.tick("scalar"))
            P.wait("gpsimd", "scalar", P.cnt["scalar"])
            P.dma("gpsimd", OUT[i * 128:(i + 1) * 128, :], ot[:])
            P.wait("sync", "gpsimd", P.cnt["gpsimd"])
            P.wait("sync", "tensor", P.cnt["tensor"])
            P.wait("scalar", "gpsimd", P.cnt["gpsimd"])
        P.barrier()

    if "in" in stages:
        stage_input()
    if "l0" in stages:
        layer0_full()
    if "l1p" in stages:
        layer1_partial()
    if "l1" in stages:
        layer1_full()
    if "l2" in stages:
        layer2_full()
    if "l3" in stages:
        layer3_full()
    if "out" in stages:
        stage_output()
    P.barrier()

    block = es.enter_context(nc.Block())
    P.finalize(block)
    es.close()
    return nc


def make_host_constants():
    ident = np.eye(128, dtype=np.float32)
    tri = np.where(np.arange(128)[None, :] <= np.arange(128)[:, None], 0.0, NEG).astype(np.float32)
    pos = np.arange(L - 1, -1, -1, dtype=np.float32)
    inv_freq = 1.0 / 10000.0 ** (np.arange(0, D, 2, dtype=np.float32) / D)
    sinusoid = pos[:, None] * inv_freq[None, :]
    pe = np.concatenate([np.sin(sinusoid), np.cos(sinusoid)], axis=-1)
    pet = np.ascontiguousarray(pe.T)
    return ident, tri, pet


def make_in_map(inputs, shard):
    d = inputs
    ident, tri, pet = make_host_constants()
    cT = lambda a: np.ascontiguousarray(np.asarray(a, dtype=np.float32).T)
    cc = lambda a: np.ascontiguousarray(np.asarray(a, dtype=np.float32))
    col = lambda a: np.asarray(a, dtype=np.float32).reshape(-1, 1)
    m = {
        "x": cc(d["L0_attn_x"][shard * NSEQ:(shard + 1) * NSEQ].reshape(T, D)),
        "IDT": ident, "TRI": tri, "PET": pet,
        "l0_inw": cT(d["L0_attn_in_proj_w"]),
        "l0_c1w": cc(d["L0_attn_conv1_w"][:, 0, :]), "l0_c1b": col(d["L0_attn_conv1_b"]),
        "l0_c2w": cc(d["L0_attn_conv2_w"][:, 0, :]), "l0_c2b": col(d["L0_attn_conv2_b"]),
        "l0_tw": cc(d["L0_attn_tune_w"][:, 0, :]), "l0_tb": col(d["L0_attn_tune_b"]),
        "l0_xpw": cT(d["L0_attn_x_proj_w"]), "l0_dtw": cT(d["L0_attn_dt_proj_w"]),
        "l0_dtb": col(d["L0_attn_dt_proj_b"]), "l0_alog": cc(d["L0_attn_A_log"]),
        "l0_d": col(d["L0_attn_D"]), "l0_ow": cT(d["L0_attn_out_proj_w"]),
        "l0f_w": cc(d["L0_ffn_weight"]), "l0f_a1": cc(d["L0_ffn_aux_weight1"]),
        "l0f_a2": cc(d["L0_ffn_aux_weight2"]), "l0f_b": col(d["L0_ffn_bias"]),
        "l1_rw": cT(d["L1_attn_router_w"]), "l1_rb": col(d["L1_attn_router_b"]),
        "l1_wq_unused": cc(d["L1_attn_wq"]), "l1_wk_unused": cc(d["L1_attn_wk"]),
        "l1_wv_unused": cc(d["L1_attn_wv"]),
        "l1f_wqkv": cc(d["L1_ffn_Wqkv"]), "l1f_wo": cc(d["L1_ffn_Wo"]), "l1f_wr": cc(d["L1_ffn_Wr"]),
        "l1f_ub": col(d["L1_ffn_u_bias"]), "l1f_vb": col(d["L1_ffn_v_bias"]),
        "l1f_ln1g": col(d["L1_ffn_ln1_g"]), "l1f_ln1b": col(d["L1_ffn_ln1_b"]),
        "l1f_wff1": cc(d["L1_ffn_Wff1"]), "l1f_bff1": col(d["L1_ffn_bff1"]),
        "l1f_wff2": cc(d["L1_ffn_Wff2"]), "l1f_bff2": col(d["L1_ffn_bff2"]),
        "l1f_ln2g": col(d["L1_ffn_ln2_g"]), "l1f_ln2b": col(d["L1_ffn_ln2_b"]),
        "l1f_wg": cc(d["L1_ffn_Wg"]), "l1f_we1": cc(d["L1_ffn_We1"]), "l1f_be1": cc(d["L1_ffn_be1"]),
        "l1f_we2": cc(d["L1_ffn_We2"]), "l1f_be2": cc(d["L1_ffn_be2"]),
        "l1f_ln3g": col(d["L1_ffn_ln3_g"]), "l1f_ln3b": col(d["L1_ffn_ln3_b"]),
        "l1f_wout": cc(d["L1_ffn_Wout"]),
        "l2_wq": cT(d["L2_attn_Wq"]), "l2_wk": cT(d["L2_attn_Wk"]),
        "l2_wv": cT(d["L2_attn_Wv"]), "l2_wo": cT(d["L2_attn_Wo"]),
        "l2_bo": col(d["L2_attn_bo"]),
        "l2_lq1": cc(d["L2_attn_lambda_q1"]).reshape(1, 64), "l2_lk1": cc(d["L2_attn_lambda_k1"]).reshape(1, 64),
        "l2_lq2": cc(d["L2_attn_lambda_q2"]).reshape(1, 64), "l2_lk2": cc(d["L2_attn_lambda_k2"]).reshape(1, 64),
        "l2_dg": cc(d["L2_attn_delta_gain"]).reshape(1, 8), "l2_chd": cc(d["L2_attn_cos_head_delta"]).reshape(1, 8),
        "l2_clsr": cc(d["L2_attn_cos_logit_scale_raw"]).reshape(1, 1),
        "l2_subw": col(d["L2_attn_subln_weight"]),
        "l2f_wg": cc(d["L2_ffn_Wg"]), "l2f_bg": cc(d["L2_ffn_bg"]).reshape(1, 8),
        "l2f_w1": cc(d["L2_ffn_W1"]), "l2f_b1": cc(d["L2_ffn_b1"]),
        "l2f_w2": cc(d["L2_ffn_W2"]), "l2f_b2": cc(d["L2_ffn_b2"]),
        "l3_g": col(d["L3_attn_gamma"]), "l3_b": col(d["L3_attn_beta"]),
        "l3f_gw": cc(d["L3_ffn_gate_w"]), "l3f_w1": cc(d["L3_ffn_w1"]), "l3f_b1": cc(d["L3_ffn_b1"]),
        "l3f_w2": cc(d["L3_ffn_w2"]), "l3f_b2": cc(d["L3_ffn_b2"]),
        "l3f_gam": col(d["L3_ffn_gamma"]), "l3f_bet": col(d["L3_ffn_beta"]),
    }
    for p in ("fw", "bw"):
        m[f"l3_{p}_inw"] = cT(d[f"L3_attn_{p}_in_w"])
        m[f"l3_{p}_cw"] = cc(d[f"L3_attn_{p}_conv_w"])
        m[f"l3_{p}_cb"] = col(d[f"L3_attn_{p}_conv_b"])
        m[f"l3_{p}_xpw"] = cT(d[f"L3_attn_{p}_xproj_w"])
        m[f"l3_{p}_dtw"] = cT(d[f"L3_attn_{p}_dt_w"])
        m[f"l3_{p}_dtb"] = col(d[f"L3_attn_{p}_dt_b"])
        m[f"l3_{p}_alog"] = cc(d[f"L3_attn_{p}_A_log"])
        m[f"l3_{p}_d"] = col(d[f"L3_attn_{p}_D"])
        m[f"l3_{p}_ow"] = cT(d[f"L3_attn_{p}_out_w"])
    return m


def kernel(**inputs):
    nc = build_program()
    in_maps = [make_in_map(inputs, sh) for sh in range(8)]
    res = run_bass_kernel_spmd(nc, in_maps, core_ids=list(range(8)))
    out = np.concatenate([r["out"].reshape(NSEQ, L, D) for r in res.results], axis=0)
    return out.astype(np.float32)
